# version 02
# baseline (speedup 1.0000x reference)
import numpy as np
import contextlib
import concourse.bass as bass
import concourse.mybir as mybir
from concourse.bass_utils import run_bass_kernel_spmd

F32 = mybir.dt.float32
BF16 = mybir.dt.bfloat16
AF = mybir.ActivationFunctionType
ALU = mybir.AluOpType
AX = mybir.AxisListType

NCORE = 8
T = 2048
D = 2048
J = 8192
TC = T // 128
DC = D // 128
JC = J // 128
PREV = (NCORE - 1) * TC
EPS = 1e-5
SCALE = 64.0


class Q:

    def __init__(self, name, dma=False):
        self.name = name
        self.dma = dma
        self.sem = None
        self.ops = []
        self.n = 0

    def emit(self, f, inc=None):
        if inc is None:
            inc = 16 if self.dma else 1
        sem = lambda: self.sem

        def op(eng, f=f, inc=inc):
            ins = f(eng)
            ins.then_inc(self.sem, inc)
        self.ops.append(op)
        self.n += inc
        return self.n

    def raw(self, f):
        self.ops.append(lambda eng, f=f: f(eng))

    def wait(self, other, val):
        if val > 0:
            self.ops.append(lambda eng, o=other, v=val: eng.wait_ge(o.sem, v))


class _CutDone(Exception):
    def __init__(self, env):
        self.env = env


def _flush_all(blk, PE, DV, AC, SY, GP):
    def flush(q, eng):
        q.eng = eng
        for op in q.ops:
            op(eng)
    blk.tensor(lambda eng: flush(PE, eng))
    blk.vector(lambda eng: flush(DV, eng))
    blk.scalar(lambda eng: flush(AC, eng))
    blk.sync(lambda eng: flush(SY, eng))
    blk.gpsimd(lambda eng: flush(GP, eng))


class SubQ:

    def __init__(self, parent):
        self.parent = parent
        self.sem = None
        self.n = 0

    def emit(self, f):
        def op(eng, f=f):
            f(eng).then_inc(self.sem, 16)
        self.parent.ops.append(op)
        self.n += 16
        return self.n


def build_nc():
    import os
    kphase = int(os.environ.get("KPHASE", "99"))
    ka0 = int(os.environ.get("KA0", "99"))
    nc = bass.Bass("TRN2", target_bir_lowering=False)

    xfull = nc.dram_tensor("xfull", [PREV * 128, D], F32, kind="ExternalInput")
    xown = nc.dram_tensor("xown", [T, D], F32, kind="ExternalInput")
    wzT = nc.dram_tensor("wzT", [D, D], F32, kind="ExternalInput")
    wupT = nc.dram_tensor("wupT", [D, J], F32, kind="ExternalInput")
    wgateT = nc.dram_tensor("wgateT", [D, J], F32, kind="ExternalInput")
    wdnT = nc.dram_tensor("wdnT", [J, D], F32, kind="ExternalInput")
    lamlog = nc.dram_tensor("lamlog", [1, 4], F32, kind="ExternalInput")
    etain = nc.dram_tensor("etain", [1, 4], F32, kind="ExternalInput")
    gammain = nc.dram_tensor("gammain", [1, D], F32, kind="ExternalInput")
    betain = nc.dram_tensor("betain", [1, D], F32, kind="ExternalInput")
    suugin = nc.dram_tensor("suugin", [1, D], F32, kind="ExternalInput")
    sudpin = nc.dram_tensor("sudpin", [64, 128], F32, kind="ExternalInput")
    wsupin = nc.dram_tensor("wsupin", [64, 128], F32, kind="ExternalInput")
    wsgin = nc.dram_tensor("wsgin", [64, 128], F32, kind="ExternalInput")
    wsdpin = nc.dram_tensor("wsdpin", [1, D], F32, kind="ExternalInput")
    h128in = nc.dram_tensor("h128in", [128, 128], BF16, kind="ExternalInput")
    i128fin = nc.dram_tensor("i128fin", [128, 128], F32, kind="ExternalInput")
    i128bin = nc.dram_tensor("i128bin", [128, 128], BF16, kind="ExternalInput")
    tmsin = nc.dram_tensor("tmsin", [128, 128], F32, kind="ExternalInput")
    mgein = nc.dram_tensor("mgein", [128, 128], F32, kind="ExternalInput")
    riotain = nc.dram_tensor("riotain", [128, 4], F32, kind="ExternalInput")
    grid4in = nc.dram_tensor("grid4in", [4, 128], F32, kind="ExternalInput")
    expoin = nc.dram_tensor("expoin", [128, PREV], F32, kind="ExternalInput")
    pmaskin = nc.dram_tensor("pmaskin", [128, PREV], F32, kind="ExternalInput")

    yout = nc.dram_tensor("yout", [T, D], F32, kind="ExternalOutput")
    x2_dram = nc.dram_tensor("x2_dram", [T, D], F32, kind="Internal")
    wup_bf = nc.dram_tensor("wup_bf", [D, J], BF16, kind="Internal")
    wgate_bf = nc.dram_tensor("wgate_bf", [D, J], BF16, kind="Internal")
    v_dram = nc.dram_tensor("v_dram", [J, T], BF16, kind="Internal")
    v2_dram = nc.dram_tensor("v2_dram", [J, T], BF16, kind="Internal")

    PE, DV, AC = Q("pe"), Q("dv"), Q("ac")
    SY, GP = Q("sy", dma=True), Q("gp", dma=True)
    QS = (PE, DV, AC, SY, GP)

    SYG = [SubQ(SY), SubQ(SY)]
    SYX2 = SubQ(SY)
    SYW = [SubQ(SY), SubQ(SY)]
    SYV = [SubQ(SY), SubQ(SY)]
    SYVQ = SubQ(SY)
    SYV2 = SubQ(SY)
    SYSL = SubQ(SY)
    SYE = [SubQ(SY), SubQ(SY)]
    SYY = [SubQ(SY), SubQ(SY)]
    SYR = SubQ(SY)
    GPW = SubQ(GP)
    GPZ = SubQ(GP)
    GPB = SubQ(GP)
    GPD = SubQ(GP)
    SUBS = [SYG[0], SYG[1], SYX2, SYW[0], SYW[1], SYV[0], SYV[1], SYVQ, SYV2,
            SYSL, SYE[0], SYE[1], SYY[0], SYY[1], SYR, GPW, GPZ, GPB, GPD]

    with contextlib.ExitStack() as top:
        for q, nm in zip(QS, ("s_pe", "s_dv", "s_ac", "s_sy", "s_gp")):
            q.sem = top.enter_context(nc.semaphore(nm))
        for i, sq in enumerate(SUBS):
            sq.sem = top.enter_context(nc.semaphore(f"s_sub{i}"))

        def sb(name, shape, dt=F32):
            return top.enter_context(nc.sbuf_tensor(name, shape, dt))

        h128_s = sb("h128_s", [128, 128], BF16)
        i128f_s = sb("i128f_s", [128, 128], F32)
        i128b_s = sb("i128b_s", [128, 128], BF16)
        tms_s = sb("tms_s", [128, 128], F32)
        mge_s = sb("mge_s", [128, 128], F32)
        riota_s = sb("riota_s", [128, 4], F32)
        grid4_s = sb("grid4_s", [4, 128], F32)
        consts = sb("consts", [128, 8], F32)
        onesr = sb("onesr", [1, 128], F32)
        srow = sb("srow", [1, 16], F32)
        scol = sb("scol", [4, 4], F32)
        lnlam_b = sb("lnlam_b", [128, 4], F32)
        eta_b = sb("eta_b", [128, 4], F32)
        bmat = sb("bmat", [128, 128], BF16)
        bmf = sb("bmf", [128, 128], F32)
        cscr = sb("cscr", [128, 128], F32)
        clcoef = sb("clcoef", [128, 4], BF16)
        aloc = sb("aloc", [4, 128], BF16)
        am = sb("am", [4, TC * 128], BF16)
        a4t = sb("a4t", [4, 128], F32)
        rowload = sb("rowload", [64, 128], F32)
        sudpT = sb("sudpT", [128, 64], F32)
        wsupT = sb("wsupT", [128, 64], F32)
        wsgT = sb("wsgT", [128, 64], F32)
        marow = sb("marow", [4, D], BF16)
        carry = sb("carry", [4, D], F32)
        carry_bf = sb("carry_bf", [4, D], BF16)
        pprev_bf = sb("pprev_bf", [128, 64], BF16)
        expo_s = sb("expo_s", [128, PREV], F32)
        pmask_s = sb("pmask_s", [128, PREV], F32)
        statc = sb("statc", [128, 8], F32)
        w4f = sb("w4f", [128, 4], F32)
        w4b = [sb(f"w4b{i}", [128, 4], BF16) for i in range(2)]
        G = [sb(f"G{i}", [128, D], F32) for i in range(2)]
        G2h = sb("G2h", [128, 1024], F32)
        Hh = [sb(f"H{i}", [128, D], BF16) for i in range(2)]
        zbf = sb("zbf", [128, D], BF16)
        hb = sb("hb", [128, D], F32)
        onesb = sb("onesb", [1, 128], BF16)

        blk = top.enter_context(nc.Block())
        _open_stacks = []

        def _close_stack(s):
            s.close()
            _open_stacks.remove(s)

        def _cut():
            for s in reversed(_open_stacks):
                s.close()
            _open_stacks.clear()
            _flush_all(blk, PE, DV, AC, SY, GP)

        for c in range(DC):
            GPW.emit(lambda gp, c=c: gp.dma_start(
                out=wup_bf[c * 128:(c + 1) * 128, :], in_=wupT[c * 128:(c + 1) * 128, :]))
        for c in range(DC):
            GPW.emit(lambda gp, c=c: gp.dma_start(
                out=wgate_bf[c * 128:(c + 1) * 128, :], in_=wgateT[c * 128:(c + 1) * 128, :]))
        m_gp_upgate = GPW.n

        m_sy0 = [None]
        for nm, (dst, src) in {
            "h128": (h128_s, h128in), "i128f": (i128f_s, i128fin), "i128b": (i128b_s, i128bin),
            "tms": (tms_s, tmsin), "mge": (mge_s, mgein), "riota": (riota_s, riotain),
            "grid4": (grid4_s, grid4in), "expo": (expo_s, expoin), "pmask": (pmask_s, pmaskin),
        }.items():
            SY.emit(lambda sy, d=dst, s=src: sy.dma_start(out=d[:, :], in_=s[:, :]))
        SY.emit(lambda sy: sy.dma_start(out=srow[0:1, 0:4], in_=lamlog[:, :]))
        SY.emit(lambda sy: sy.dma_start(out=srow[0:1, 8:12], in_=etain[:, :]))
        m_syc = SY.n

        stP = contextlib.ExitStack()
        _open_stacks.append(stP)
        pstat = stP.enter_context(nc.psum_tensor("pstat", [128, 512], F32))
        pp2 = stP.enter_context(nc.psum_tensor("pp2", [128, 512], F32))
        ptA = [stP.enter_context(nc.psum_tensor(f"ptA{i}", [128, 1024], BF16))
               for i in range(2)]

        for col, val in ((0, EPS), (1, 1.0), (2, -0.5), (3, -1.0)):
            DV.emit(lambda dv, c=col, v=val: dv.memset(consts[:, c:c + 1], v))
        DV.emit(lambda dv: dv.memset(onesr[:, :], 1.0))
        DV.emit(lambda dv: dv.memset(onesb[:, :], 1.0))
        DV.emit(lambda dv: dv.memset(carry[:, :], 0.0))
        DV.emit(lambda dv: dv.memset(carry_bf[:, :], 0.0))
        m_dvc = DV.n

        AC.wait(SY, m_syc)
        AC.wait(DV, m_dvc)
        m1 = AC.emit(lambda ac: ac.activation(srow[0:1, 0:4], srow[0:1, 0:4], AF.Exp,
                                              scale=consts[0:1, 3:4]))
        DV.wait(AC, m1)
        DV.emit(lambda dv: dv.tensor_scalar_add(srow[0:1, 0:4], srow[0:1, 0:4], 1.0))
        DV.wait(DV, DV.n)
        m2 = DV.emit(lambda dv: dv.reciprocal(srow[0:1, 0:4], srow[0:1, 0:4]))
        AC.wait(DV, m2)
        m3 = AC.emit(lambda ac: ac.activation(srow[0:1, 4:8], srow[0:1, 0:4], AF.Ln))
        DV.wait(AC, m3)
        m4 = DV.emit(lambda dv: dv.tensor_scalar_mul(srow[0:1, 12:16], srow[0:1, 4:8], 128.0))
        AC.wait(DV, m4)
        m5 = AC.emit(lambda ac: ac.activation(srow[0:1, 12:16], srow[0:1, 12:16], AF.Exp))

        PE.wait(AC, m5)
        PE.emit(lambda pe: pe.matmul(pstat[:, 0:4], onesr[0:1, :], srow[0:1, 4:8],
                                     start=True, stop=True))
        PE.emit(lambda pe: pe.matmul(pstat[:, 4:8], onesr[0:1, :], srow[0:1, 8:12],
                                     start=True, stop=True))
        for i, c0 in enumerate((0, 4, 8, 12)):
            PE.emit(lambda pe, i=i, c0=c0: pe.transpose(
                pstat[0:4, 8 + i:9 + i], srow[0:1, c0:c0 + 4], i128f_s[0:1, 0:1]))
        m6 = PE.n
        DV.wait(PE, m6)
        DV.emit(lambda dv: dv.tensor_copy(lnlam_b[:, :], pstat[:, 0:4]))
        DV.emit(lambda dv: dv.tensor_copy(eta_b[:, :], pstat[:, 4:8]))
        m7 = DV.emit(lambda dv: dv.tensor_copy(scol[0:4, 0:4], pstat[0:4, 8:12]))
        m_scalars = m7

        SYR.emit(lambda sy: sy.dma_start(out=rowload[:, :], in_=sudpin[:, :]))
        m8 = SYR.n
        PE.wait(SYR, m8)
        PE.wait(DV, m7)
        m9 = PE.emit(lambda pe: pe.transpose(pstat[:, 0:64], rowload[0:64, :], i128f_s[0:64, 0:64]))
        DV.wait(PE, m9)
        m10 = DV.emit(lambda dv: dv.tensor_scalar_mul(
            sudpT[:, :], pstat[:, 0:64], 1.0 / (SCALE * float(np.sqrt(J)))))
        SY.wait(DV, m10)
        SYR.emit(lambda sy: sy.dma_start(out=rowload[:, :], in_=wsupin[:, :]))
        m11 = SYR.n
        PE.wait(SYR, m11)
        PE.wait(DV, m10)
        m12 = PE.emit(lambda pe: pe.transpose(pstat[:, 64:128], rowload[0:64, :], i128f_s[0:64, 0:64]))
        DV.wait(PE, m12)
        m13 = DV.emit(lambda dv: dv.tensor_scalar_mul(wsupT[:, :], pstat[:, 64:128], SCALE))
        SY.wait(PE, m12)
        SYR.emit(lambda sy: sy.dma_start(out=rowload[:, :], in_=wsgin[:, :]))
        m14 = SYR.n
        PE.wait(SYR, m14)
        PE.wait(DV, m13)
        m15 = PE.emit(lambda pe: pe.transpose(pstat[:, 128:192], rowload[0:64, :], i128f_s[0:64, 0:64]))
        DV.wait(PE, m15)
        m16 = DV.emit(lambda dv: dv.tensor_scalar_mul(wsgT[:, :], pstat[:, 128:192], SCALE))

        DV.wait(DV, m_scalars)
        DV.emit(lambda dv: dv.memset(bmf[:, :], 0.0))
        for k in range(4):
            DV.wait(DV, DV.n)
            mk = DV.emit(lambda dv, k=k: dv.tensor_scalar(
                cscr[:, :], tms_s[:, :], lnlam_b[:, k:k + 1], None, ALU.mult))
            AC.wait(DV, mk)
            ma = AC.emit(lambda ac: ac.activation(cscr[:, :], cscr[:, :], AF.Exp))
            DV.wait(AC, ma)
            DV.emit(lambda dv, k=k: dv.tensor_scalar(
                cscr[:, :], cscr[:, :], eta_b[:, k:k + 1], None, ALU.mult))
            DV.wait(DV, DV.n)
            DV.emit(lambda dv: dv.tensor_add(bmf[:, :], bmf[:, :], cscr[:, :]))
        DV.wait(DV, DV.n)
        DV.emit(lambda dv: dv.tensor_mul(bmf[:, :], bmf[:, :], mge_s[:, :]))
        DV.wait(DV, DV.n)
        DV.emit(lambda dv: dv.tensor_add(bmf[:, :], bmf[:, :], i128f_s[:, :]))
        DV.wait(DV, DV.n)
        m_bmat = DV.emit(lambda dv: dv.tensor_copy(bmat[:, :], bmf[:, :]))

        DV.emit(lambda dv: dv.tensor_mul(w4f[:, :], riota_s[:, :], lnlam_b[:, :]))
        DV.wait(DV, DV.n)
        mcl = DV.n
        AC.wait(DV, mcl)
        ma = AC.emit(lambda ac: ac.activation(w4f[:, :], w4f[:, :], AF.Exp))
        DV.wait(AC, ma)
        m_cl = DV.emit(lambda dv: dv.tensor_copy(clcoef[:, :], w4f[:, :]))

        def coef4(dst_ap_fn, off):
            DV.wait(DV, DV.n)
            DV.emit(lambda dv: dv.tensor_scalar_add(a4t[:, :], grid4_s[:, :], float(off)))
            DV.wait(DV, DV.n)
            mx = DV.emit(lambda dv: dv.tensor_scalar(
                a4t[:, :], a4t[:, :], scol[0:4, 1:2], None, ALU.mult))
            AC.wait(DV, mx)
            my = AC.emit(lambda ac: ac.activation(a4t[:, :], a4t[:, :], AF.Exp))
            DV.wait(AC, my)
            return DV.emit(lambda dv: dv.tensor_scalar(
                dst_ap_fn(), a4t[:, :], scol[0:4, 2:3], None, ALU.mult))

        coef4(lambda: aloc[:, :], 1.0)
        for jj in range(TC):
            coef4(lambda jj=jj: am[:, jj * 128:(jj + 1) * 128], 128.0 * jj + 1.0)
        m_coefs = DV.n

        if kphase < 2:
            _cut()
            return nc
        m_prep_ld = []
        m_prep_dv = []
        m_prep_ac = []
        m_prep_pe = []
        for jp in range(PREV):
            b = jp % 2
            if jp >= 2:
                SY.wait(AC, m_prep_ac[jp - 2])
            SY.wait(DV, m_dvc)
            m_ld = SYG[b].emit(lambda sy, jp=jp, b=b: sy.dma_start(
                out=G[b][:, :], in_=xfull[jp * 128:(jp + 1) * 128, :]))
            m_prep_ld.append(m_ld)
            DV.wait(SYG[b], m_ld)
            DV.wait(DV, m_coefs)
            if jp >= 1:
                DV.wait(AC, m_prep_ac[jp - 1])
            DV.emit(lambda dv, b=b: dv.reduce_sum(statc[:, 0:1], G[b][:, :], axis=AX.X))
            DV.wait(DV, DV.n)
            mneg = DV.emit(lambda dv: dv.tensor_scalar_mul(statc[:, 1:2], statc[:, 0:1], -1.0 / D))
            AC.wait(DV, DV.n)
            msq = AC.emit(lambda ac, b=b: ac.activation(G2h[:, :], G[b][:, 0:1024], AF.Square))
            DV.wait(AC, msq)
            mr0 = DV.emit(lambda dv: dv.reduce_sum(statc[:, 2:3], G2h[:, :], axis=AX.X))
            AC.wait(DV, mr0)
            msq2 = AC.emit(lambda ac, b=b: ac.activation(G2h[:, :], G[b][:, 1024:2048], AF.Square))
            DV.wait(AC, msq2)
            DV.emit(lambda dv: dv.reduce_sum(statc[:, 6:7], G2h[:, :], axis=AX.X))
            DV.wait(DV, DV.n)
            DV.emit(lambda dv: dv.tensor_add(statc[:, 2:3], statc[:, 2:3], statc[:, 6:7]))
            DV.wait(DV, DV.n)
            DV.emit(lambda dv: dv.tensor_scalar_mul(statc[:, 3:4], statc[:, 2:3], 1.0 / D))
            DV.emit(lambda dv: dv.tensor_mul(statc[:, 5:6], statc[:, 1:2], statc[:, 1:2]))
            DV.wait(DV, DV.n)
            mvar = DV.emit(lambda dv: dv.tensor_sub(statc[:, 3:4], statc[:, 3:4], statc[:, 5:6]))
            AC.wait(DV, mvar)
            AC.emit(lambda ac: ac.activation(statc[:, 4:5], statc[:, 3:4], AF.Ln,
                                             bias=consts[:, 0:1]))
            AC.wait(AC, AC.n)
            mrs = AC.emit(lambda ac: ac.activation(statc[:, 4:5], statc[:, 4:5], AF.Exp,
                                                   scale=consts[:, 2:3]))
            DV.wait(AC, mrs)
            DV.emit(lambda dv, jp=jp: dv.tensor_scalar(
                w4f[:, :], lnlam_b[:, :], expo_s[:, jp:jp + 1], None, ALU.mult))
            DV.wait(DV, DV.n)
            mw = DV.n
            AC.wait(DV, mw)
            mwe = AC.emit(lambda ac: ac.activation(w4f[:, :], w4f[:, :], AF.Exp))
            DV.wait(AC, mwe)
            if jp >= 2:
                DV.wait(PE, m_prep_pe[jp - 2])
            DV.emit(lambda dv: dv.tensor_scalar(
                w4f[:, :], w4f[:, :], statc[:, 4:5], None, ALU.mult))
            DV.wait(DV, DV.n)
            mw4 = DV.emit(lambda dv, jp=jp, b=b: dv.tensor_scalar(
                w4b[b][:, :], w4f[:, :], pmask_s[:, jp:jp + 1], None, ALU.mult))
            m_prep_dv.append(mw4)
            AC.wait(DV, mw4)
            if jp >= 2:
                AC.wait(PE, m_prep_pe[jp - 2])
            mcast = AC.emit(lambda ac, b=b: ac.activation(
                Hh[b][:, :], G[b][:, :], AF.Identity, bias=statc[:, 1:2]))
            m_prep_ac.append(mcast)
            PE.wait(AC, mcast)
            PE.wait(DV, mw4)

            def pmm(pe, jp=jp, b=b):
                last = None
                for c in range(DC):
                    last = pe.matmul(pp2[:, 4 * c:4 * c + 4],
                                     Hh[b][:, c * 128:(c + 1) * 128],
                                     w4b[b][:, :],
                                     start=(jp == 0 and c == 0),
                                     stop=(jp == PREV - 1 and c == DC - 1))
                return last
            m_pe = PE.emit(pmm)
            m_prep_pe.append(m_pe)
        DV.wait(PE, m_prep_pe[-1])
        m_pev = DV.emit(lambda dv: dv.tensor_copy(pprev_bf[:, :], pp2[:, 0:64]))

        if kphase < 3:
            _cut()
            return nc
        stA = contextlib.ExitStack()
        _open_stacks.append(stA)
        wzT_s = stA.enter_context(nc.sbuf_tensor("wzT_s", [128, DC * D], BF16))
        xlnT_s = stA.enter_context(nc.sbuf_tensor("xlnT_s", [128, DC * T], BF16))
        gam_b = stA.enter_context(nc.sbuf_tensor("gam_b", [128, D], BF16))
        bet_b = stA.enter_context(nc.sbuf_tensor("bet_b", [128, D], BF16))

        for c in range(DC):
            GPZ.emit(lambda gp, c=c: gp.dma_start(
                out=wzT_s[:, c * D:(c + 1) * D], in_=wzT[c * 128:(c + 1) * 128, :]))
        GPB.emit(lambda gp: gp.dma_start(out=gam_b[0:1, :], in_=gammain[:, :]))
        GPB.emit(lambda gp: gp.dma_start(out=bet_b[0:1, :], in_=betain[:, :]))
        m_gp_wz = GPZ.n
        m_gb = GPB.n

        PE.wait(GPB, m_gb)
        PE.wait(DV, m16)
        for row_t in (gam_b, bet_b):
            for s in range(4):
                mmm = PE.emit(lambda pe, r=row_t, s=s: pe.matmul(
                    pstat[:, 0:512], onesb[0:1, :], r[0:1, s * 512:(s + 1) * 512],
                    start=True, stop=True))
                DV.wait(PE, mmm)
                mvv = DV.emit(lambda dv, r=row_t, s=s: dv.tensor_copy(
                    r[:, s * 512:(s + 1) * 512], pstat[:, 0:512]))
                PE.wait(DV, mvv)
        m_gbb = DV.n

        PE.wait(DV, m_pev)
        PE.wait(GPZ, m_gp_wz)
        for s in range(4):
            def mmm(pe, s=s):
                last = None
                for c in range(DC):
                    last = pe.matmul(pstat[0:4, 0:512],
                                     pprev_bf[:, 4 * c:4 * c + 4],
                                     wzT_s[:, c * D + s * 512: c * D + (s + 1) * 512],
                                     start=(c == 0), stop=(c == DC - 1))
                return last
            mp = PE.emit(mmm)
            DV.wait(PE, mp)
            md = DV.emit(lambda dv, s=s: dv.tensor_copy(
                marow[0:4, s * 512:(s + 1) * 512], pstat[0:4, 0:512]))
            PE.wait(DV, md)
        m_M = DV.n

        if kphase < 4:
            _cut()
            return nc
        m_a0_ac = []
        m_prep_end_dv = DV.n
        m_prep_end_ac = AC.n
        m_prep_end_pe = PE.n
        slot_user = [0] * 2
        for j in range(TC):
            b = j % 2
            SY.wait(AC, m_prep_end_ac)
            if j >= 2:
                SY.wait(AC, m_a0_ac[j - 2])
            m_ld = SYG[b].emit(lambda sy, j=j, b=b: sy.dma_start(
                out=G[b][:, :], in_=xown[j * 128:(j + 1) * 128, :]))
            DV.wait(SYG[b], m_ld)
            DV.emit(lambda dv, b=b: dv.reduce_sum(statc[:, 0:1], G[b][:, :], axis=AX.X))
            DV.wait(DV, DV.n)
            DV.emit(lambda dv: dv.tensor_scalar_mul(statc[:, 1:2], statc[:, 0:1], -1.0 / D))
            mdn = DV.n
            AC.wait(DV, mdn)
            msq = AC.emit(lambda ac, b=b: ac.activation(G2h[:, :], G[b][:, 0:1024], AF.Square))
            DV.wait(AC, msq)
            mr0 = DV.emit(lambda dv: dv.reduce_sum(statc[:, 2:3], G2h[:, :], axis=AX.X))
            AC.wait(DV, mr0)
            msq2 = AC.emit(lambda ac, b=b: ac.activation(G2h[:, :], G[b][:, 1024:2048], AF.Square))
            DV.wait(AC, msq2)
            DV.emit(lambda dv: dv.reduce_sum(statc[:, 6:7], G2h[:, :], axis=AX.X))
            DV.wait(DV, DV.n)
            DV.emit(lambda dv: dv.tensor_add(statc[:, 2:3], statc[:, 2:3], statc[:, 6:7]))
            DV.wait(DV, DV.n)
            DV.emit(lambda dv: dv.tensor_scalar_mul(statc[:, 3:4], statc[:, 2:3], 1.0 / D))
            DV.emit(lambda dv: dv.tensor_mul(statc[:, 5:6], statc[:, 1:2], statc[:, 1:2]))
            DV.wait(DV, DV.n)
            mvar = DV.emit(lambda dv: dv.tensor_sub(statc[:, 3:4], statc[:, 3:4], statc[:, 5:6]))
            AC.wait(DV, mvar)
            AC.emit(lambda ac: ac.activation(statc[:, 4:5], statc[:, 3:4], AF.Ln,
                                             bias=consts[:, 0:1]))
            AC.wait(AC, AC.n)
            mrs = AC.emit(lambda ac: ac.activation(statc[:, 4:5], statc[:, 4:5], AF.Exp,
                                                   scale=consts[:, 2:3]))
            if ka0 < 2:
                m_a0_ac.append(AC.n)
                try:
                    m_a0_pe.append(PE.n)
                except NameError:
                    m_a0_pe = [PE.n]
                continue
            DV.wait(AC, mrs)
            if j >= 2:
                DV.wait(PE, m_a0_pe[j - 2])
            else:
                DV.wait(PE, m_prep_end_pe)
            mnorm = DV.emit(lambda dv, b=b: dv.tensor_scalar(
                Hh[b][:, :], G[b][:, :], statc[:, 1:2], statc[:, 4:5], ALU.add, ALU.mult))
            if ka0 < 3:
                m_a0_ac.append(AC.n)
                try:
                    m_a0_pe.append(PE.n)
                except NameError:
                    m_a0_pe = [PE.n]
                continue
            PE.wait(DV, mnorm)
            for c in range(DC):
                slot = c % 2
                PE.wait(AC, slot_user[slot])
                mt = PE.emit(lambda pe, b=b, c=c, slot=slot: pe.transpose(
                    ptA[slot][:, 0:128],
                    Hh[b][:, c * 128:(c + 1) * 128], i128b_s[:, :]))
                AC.wait(PE, mt)
                mcp = AC.emit(lambda ac, c=c, j=j, slot=slot: ac.copy(
                    xlnT_s[:, c * T + j * 128: c * T + (j + 1) * 128],
                    ptA[slot][:, 0:128]))
                slot_user[slot] = mcp
            m_a0_ac.append(AC.n)
            try:
                m_a0_pe.append(PE.n)
            except NameError:
                m_a0_pe = [PE.n]

        m_a0_done_pe = PE.n
        m_a0_end_dv = DV.n

        if kphase < 5:
            _cut()
            return nc
        stAz = contextlib.ExitStack()
        _open_stacks.append(stAz)
        pz = stAz.enter_context(nc.psum_tensor("pz", [128, 1024], F32))
        ph = stAz.enter_context(nc.psum_tensor("ph", [128, 1024], F32))

        m_zev = 0
        m_hev = 0
        m_ccast = 0
        m_clmm = [0, 0]
        m_x2st = []
        for j in range(TC):
            b = j % 2
            for hh in range(2):
                PE.wait(AC, m_a0_ac[j])
                PE.wait(GPZ, m_gp_wz)
                PE.wait(AC, m_zev)
                PE.wait(DV, m_ccast)

                def zmm(pe, j=j, hh=hh):
                    last = None
                    for c in range(DC):
                        for s in range(2):
                            last = pe.matmul(
                                pz[:, s * 512:(s + 1) * 512],
                                xlnT_s[:, c * T + j * 128: c * T + (j + 1) * 128],
                                wzT_s[:, c * D + hh * 1024 + s * 512: c * D + hh * 1024 + (s + 1) * 512],
                                start=(c == 0), stop=(c == DC - 1))
                    return last
                mz = PE.emit(zmm)
                AC.wait(PE, mz)
                AC.wait(PE, m_clmm[hh])
                m_zev = AC.emit(lambda ac, hh=hh: ac.copy(
                    zbf[:, hh * 1024:(hh + 1) * 1024], pz[:, :]))
                PE.wait(AC, m_zev)
                PE.wait(DV, max(m_hev, m_M, m_ccast))

                def hmm(pe, j=j, hh=hh):
                    last = None
                    for s in range(2):
                        sl = slice(hh * 1024 + s * 512, hh * 1024 + (s + 1) * 512)
                        psl = slice(s * 512, (s + 1) * 512)
                        pe.matmul(ph[:, psl], bmat[:, :], zbf[:, sl], start=True, stop=False)
                        pe.matmul(ph[:, psl], aloc[:, :], carry_bf[0:4, sl], start=False, stop=False)
                        last = pe.matmul(ph[:, psl], am[:, j * 128:(j + 1) * 128],
                                         marow[0:4, sl], start=False, stop=True)
                    return last
                mh = PE.emit(hmm)
                DV.wait(PE, mh)
                if j >= 1:
                    DV.wait(SYX2, m_x2st[j - 1])
                m_hev = DV.emit(lambda dv, hh=hh: dv.tensor_copy(
                    hb[:, hh * 1024:(hh + 1) * 1024], ph[:, :]))
                PE.wait(AC, m_zev)
                PE.wait(DV, m_ccast)

                def clmm(pe, hh=hh):
                    last = None
                    for s in range(2):
                        sl = slice(hh * 1024 + s * 512, hh * 1024 + (s + 1) * 512)
                        last = pe.matmul(pz[0:4, s * 512:(s + 1) * 512], clcoef[:, :],
                                         zbf[:, sl], start=True, stop=True)
                    return last
                mcl2 = PE.emit(clmm)
                m_clmm[hh] = mcl2
                DV.wait(PE, mcl2)
                DV.emit(lambda dv, hh=hh: dv.scalar_tensor_tensor(
                    carry[0:4, hh * 1024:(hh + 1) * 1024],
                    carry[0:4, hh * 1024:(hh + 1) * 1024],
                    scol[0:4, 3:4], pz[0:4, 0:1024], ALU.mult, ALU.add))
                DV.wait(DV, DV.n)
                m_ccast = DV.emit(lambda dv, hh=hh: dv.tensor_copy(
                    carry_bf[0:4, hh * 1024:(hh + 1) * 1024],
                    carry[0:4, hh * 1024:(hh + 1) * 1024]))
            DV.wait(DV, m_hev)
            DV.emit(lambda dv: dv.reduce_sum(statc[:, 0:1], hb[:, :], axis=AX.X))
            DV.wait(DV, DV.n)
            mdn = DV.emit(lambda dv: dv.tensor_scalar_mul(statc[:, 1:2], statc[:, 0:1], -1.0 / D))
            AC.wait(DV, mdn)
            msq = AC.emit(lambda ac: ac.activation(G2h[:, :], hb[:, 0:1024], AF.Square))
            DV.wait(AC, msq)
            mr0 = DV.emit(lambda dv: dv.reduce_sum(statc[:, 2:3], G2h[:, :], axis=AX.X))
            AC.wait(DV, mr0)
            msq2 = AC.emit(lambda ac: ac.activation(G2h[:, :], hb[:, 1024:2048], AF.Square))
            DV.wait(AC, msq2)
            DV.emit(lambda dv: dv.reduce_sum(statc[:, 6:7], G2h[:, :], axis=AX.X))
            DV.wait(DV, DV.n)
            DV.emit(lambda dv: dv.tensor_add(statc[:, 2:3], statc[:, 2:3], statc[:, 6:7]))
            DV.wait(DV, DV.n)
            DV.emit(lambda dv: dv.tensor_scalar_mul(statc[:, 3:4], statc[:, 2:3], 1.0 / D))
            DV.emit(lambda dv: dv.tensor_mul(statc[:, 5:6], statc[:, 1:2], statc[:, 1:2]))
            DV.wait(DV, DV.n)
            mvar = DV.emit(lambda dv: dv.tensor_sub(statc[:, 3:4], statc[:, 3:4], statc[:, 5:6]))
            AC.wait(DV, mvar)
            AC.emit(lambda ac: ac.activation(statc[:, 4:5], statc[:, 3:4], AF.Ln,
                                             bias=consts[:, 0:1]))
            AC.wait(AC, AC.n)
            mrs = AC.emit(lambda ac: ac.activation(statc[:, 4:5], statc[:, 4:5], AF.Exp,
                                                   scale=consts[:, 2:3]))
            if j >= 2:
                SY.wait(DV, m_x2dv[j - 2])
            else:
                SY.wait(DV, m_a0_end_dv)
            SY.wait(DV, m_a0_end_dv)
            mxl = SYG[b].emit(lambda sy, j=j, b=b: sy.dma_start(
                out=G[b][:, :], in_=xown[j * 128:(j + 1) * 128, :]))
            DV.wait(AC, mrs)
            DV.emit(lambda dv: dv.tensor_scalar(
                hb[:, :], hb[:, :], statc[:, 1:2], statc[:, 4:5], ALU.add, ALU.mult))
            DV.wait(DV, DV.n)
            DV.emit(lambda dv: dv.tensor_mul(hb[:, :], hb[:, :], gam_b[:, :]))
            DV.wait(DV, DV.n)
            DV.emit(lambda dv: dv.tensor_add(hb[:, :], hb[:, :], bet_b[:, :]))
            DV.wait(DV, DV.n)
            DV.wait(SYG[b], mxl)
            mx2 = DV.emit(lambda dv, b=b: dv.tensor_add(hb[:, :], hb[:, :], G[b][:, :]))
            try:
                m_x2dv.append(mx2)
            except NameError:
                m_x2dv = [mx2]
            SY.wait(DV, mx2)
            mst = SYX2.emit(lambda sy, j=j: sy.dma_start(
                out=x2_dram[j * 128:(j + 1) * 128, :], in_=hb[:, :]))
            m_x2st.append(mst)

        if kphase < 6:
            _cut()
            return nc
        m_a1z_end = dict(pe=PE.n, dv=DV.n, ac=AC.n, sy=SY.n)
        _close_stack(stAz)
        _close_stack(stP)
        _close_stack(stA)

        stB1 = contextlib.ExitStack()
        _open_stacks.append(stB1)
        uT_s = stB1.enter_context(nc.sbuf_tensor("uT_s", [128, DC * T], BF16))
        su_b = stB1.enter_context(nc.sbuf_tensor("su_b", [128, D], F32))
        wslab = [stB1.enter_context(nc.sbuf_tensor(f"wslab{i}", [128, 16 * 256], BF16)) for i in range(2)]
        gslab = [stB1.enter_context(nc.sbuf_tensor(f"gslab{i}", [128, 16 * 256], BF16)) for i in range(2)]
        btmp = stB1.enter_context(nc.sbuf_tensor("btmp", [128, 1024], BF16))
        silb2 = stB1.enter_context(nc.sbuf_tensor("silb2", [128, 1024], F32))
        vtmp2 = stB1.enter_context(nc.sbuf_tensor("vtmp2", [128, 1024], F32))
        vchB = [stB1.enter_context(nc.sbuf_tensor(f"vchB{i}", [128, T], BF16)) for i in range(2)]

        stB0 = contextlib.ExitStack()
        _open_stacks.append(stB0)
        ptB = [stB0.enter_context(nc.psum_tensor(f"ptB{i}", [128, 1024], BF16))
               for i in range(2)]
        pH0 = stB0.enter_context(nc.psum_tensor("pH0", [128, 512], F32))
        pH1 = stB0.enter_context(nc.psum_tensor("pH1", [128, 512], F32))

        SYR.emit(lambda sy: sy.dma_start(out=su_b[0:1, :], in_=suugin[:, :]))
        m_sul = SYR.n
        PE.wait(SYR, m_sul)
        su_scale = 1.0 / (SCALE * float(np.sqrt(D)))
        for s in range(4):
            mmm = PE.emit(lambda pe, s=s: pe.matmul(
                pH0[:, 0:512], onesr[0:1, :], su_b[0:1, s * 512:(s + 1) * 512],
                start=True, stop=True))
            DV.wait(PE, mmm)
            mvv = DV.emit(lambda dv, s=s: dv.tensor_scalar_mul(
                su_b[:, s * 512:(s + 1) * 512], pH0[:, 0:512], su_scale))
            PE.wait(DV, mvv)
        m_sub = DV.n

        m_b0_ac = []
        slotB = [0] * 2
        m_b0_pe_chunks = []
        for j in range(TC):
            b = j % 2
            SY.wait(SYX2, m_x2st[j])
            if j >= 2:
                SY.wait(DV, m_b0_norm[j - 2])
            else:
                SY.wait(DV, m_a1z_end["dv"])
            mld = SYG[b].emit(lambda sy, j=j, b=b: sy.dma_start(
                out=G[b][:, :], in_=x2_dram[j * 128:(j + 1) * 128, :]))
            DV.wait(SYG[b], mld)
            if j < 2:
                DV.wait(DV, m_a1z_end["dv"])
            DV.emit(lambda dv, b=b: dv.reduce_sum(statc[:, 0:1], G[b][:, :], axis=AX.X))
            DV.wait(DV, DV.n)
            mdn = DV.emit(lambda dv: dv.tensor_scalar_mul(statc[:, 1:2], statc[:, 0:1], -1.0 / D))
            AC.wait(DV, mdn)
            msq = AC.emit(lambda ac, b=b: ac.activation(G2h[:, :], G[b][:, 0:1024], AF.Square))
            DV.wait(AC, msq)
            mr0 = DV.emit(lambda dv: dv.reduce_sum(statc[:, 2:3], G2h[:, :], axis=AX.X))
            AC.wait(DV, mr0)
            msq2 = AC.emit(lambda ac, b=b: ac.activation(G2h[:, :], G[b][:, 1024:2048], AF.Square))
            DV.wait(AC, msq2)
            DV.emit(lambda dv: dv.reduce_sum(statc[:, 6:7], G2h[:, :], axis=AX.X))
            DV.wait(DV, DV.n)
            DV.emit(lambda dv: dv.tensor_add(statc[:, 2:3], statc[:, 2:3], statc[:, 6:7]))
            DV.wait(DV, DV.n)
            DV.emit(lambda dv: dv.tensor_scalar_mul(statc[:, 3:4], statc[:, 2:3], 1.0 / D))
            DV.emit(lambda dv: dv.tensor_mul(statc[:, 5:6], statc[:, 1:2], statc[:, 1:2]))
            DV.wait(DV, DV.n)
            mvar = DV.emit(lambda dv: dv.tensor_sub(statc[:, 3:4], statc[:, 3:4], statc[:, 5:6]))
            AC.wait(DV, mvar)
            AC.emit(lambda ac: ac.activation(statc[:, 4:5], statc[:, 3:4], AF.Ln,
                                             bias=consts[:, 0:1]))
            AC.wait(AC, AC.n)
            mrs = AC.emit(lambda ac: ac.activation(statc[:, 4:5], statc[:, 4:5], AF.Exp,
                                                   scale=consts[:, 2:3]))
            DV.wait(AC, mrs)
            DV.wait(DV, m_sub)
            if j >= 2:
                DV.wait(PE, m_b0_pe_chunks[j - 2])
            else:
                DV.wait(PE, m_a1z_end["pe"])
            for hx in range(2):
                DV.emit(lambda dv, b=b, hx=hx: dv.tensor_scalar(
                    G2h[:, :], G[b][:, hx * 1024:(hx + 1) * 1024],
                    statc[:, 1:2], statc[:, 4:5], ALU.add, ALU.mult))
                DV.wait(DV, DV.n)
                DV.emit(lambda dv, b=b, hx=hx: dv.tensor_mul(
                    Hh[b][:, hx * 1024:(hx + 1) * 1024], G2h[:, :],
                    su_b[:, hx * 1024:(hx + 1) * 1024]))
                DV.wait(DV, DV.n)
            mu_ = DV.n
            prev = mu_
            for st_i, dist in enumerate((1, 2, 4, 8)):
                blkw = dist * 128
                ngrp = DC // (2 * dist)

                def bfly(dv, b=b, dist=dist, blkw=blkw, ngrp=ngrp, which=0):
                    vv = Hh[b][:, :].rearrange("p (g two q) -> p g two q", two=2, q=blkw)
                    a = vv[:, :, 0, :]
                    bb = vv[:, :, 1, :]
                    t = btmp[:, 0:ngrp * blkw].rearrange("p (g q) -> p g q", q=blkw)
                    if which == 0:
                        return dv.tensor_add(t, a, bb)
                    elif which == 1:
                        return dv.tensor_sub(bb, a, bb)
                    else:
                        return dv.tensor_copy(a, t)
                DV.wait(DV, prev)
                DV.emit(lambda dv, f=bfly: f(dv, which=0))
                DV.wait(DV, DV.n)
                DV.emit(lambda dv, f=bfly: f(dv, which=1))
                DV.wait(DV, DV.n)
                prev = DV.emit(lambda dv, f=bfly: f(dv, which=2))
            m_b0n = prev
            try:
                m_b0_norm.append(m_b0n)
            except NameError:
                m_b0_norm = [m_b0n]
            PE.wait(DV, m_b0n)
            for c in range(DC):
                slot = c % 2
                PE.wait(AC, slotB[slot])
                mt = PE.emit(lambda pe, b=b, c=c, slot=slot: pe.transpose(
                    ptB[slot][:, 0:128],
                    Hh[b][:, c * 128:(c + 1) * 128], i128b_s[:, :]))
                AC.wait(PE, mt)
                mcp = AC.emit(lambda ac, c=c, j=j, slot=slot: ac.copy(
                    uT_s[:, c * T + j * 128: c * T + (j + 1) * 128],
                    ptB[slot][:, 0:128]))
                slotB[slot] = mcp
            m_b0_ac.append(AC.n)
            m_b0_pe_chunks.append(PE.n)
        m_h128 = []
        PE.wait(AC, m_b0_ac[-1])
        prev_ev = [0, 0]
        for c in range(DC):
            for s in range(4):
                par = (c * 4 + s) % 2
                ptgt = pH0 if par == 0 else pH1
                PE.wait(AC, prev_ev[par])
                mm_ = PE.emit(lambda pe, c=c, s=s, ptgt=ptgt: pe.matmul(
                    ptgt[:, 0:512], h128_s[:, :],
                    uT_s[:, c * T + s * 512: c * T + (s + 1) * 512],
                    start=True, stop=True))
                AC.wait(PE, mm_)
                mev = AC.emit(lambda ac, c=c, s=s, ptgt=ptgt: ac.copy(
                    uT_s[:, c * T + s * 512: c * T + (s + 1) * 512], ptgt[:, 0:512]))
                prev_ev[par] = mev
        m_uT = AC.n

        if kphase < 7:
            _cut()
            return nc
        _close_stack(stB0)
        stB1p = contextlib.ExitStack()
        _open_stacks.append(stB1p)
        pu = [stB1p.enter_context(nc.psum_tensor(f"pu{i}", [128, 1024], F32)) for i in range(2)]
        pg = [stB1p.enter_context(nc.psum_tensor(f"pg{i}", [128, 1024], F32)) for i in range(2)]

        def emit_slab(w, wbuf, gbuf):
            sq = SYW[w % 2]
            for c in range(DC):
                sq.emit(lambda sy, w=w, c=c, wbuf=wbuf: sy.dma_start(
                    out=wbuf[:, c * 256:(c + 1) * 256],
                    in_=wup_bf[c * 128:(c + 1) * 128, w * 256:(w + 1) * 256]))
            for c in range(DC):
                sq.emit(lambda sy, w=w, c=c, gbuf=gbuf: sy.dma_start(
                    out=gbuf[:, c * 256:(c + 1) * 256],
                    in_=wgate_bf[c * 128:(c + 1) * 128, w * 256:(w + 1) * 256]))
            return sq.n

        m_slab = {}
        m_win_pe = {}
        SY.wait(GPW, m_gp_upgate)
        m_slab[0] = emit_slab(0, wslab[0], gslab[0])
        m_vst = []
        m_vdv = {}
        m_stt_prev = 0
        for jc in range(JC):
            w = jc // 2
            if jc % 2 == 0 and w + 1 < JC // 2:
                if w >= 1:
                    SY.wait(PE, m_win_pe[w - 1])
                m_slab[w + 1] = emit_slab(w + 1, wslab[(w + 1) % 2], gslab[(w + 1) % 2])
            for hh in range(2):
                par = (jc * 2 + hh) % 2
                PE.wait(SYW[w % 2], m_slab[w])
                PE.wait(AC, m_uT)
                if (jc * 2 + hh) >= 2:
                    PE.wait(DV, m_vdv[(jc * 2 + hh) - 2])

                def upmm(pe, jc=jc, hh=hh, w=w, par=par):
                    jj = jc % 2
                    last = None
                    for c in range(DC):
                        for s in range(2):
                            last = pe.matmul(
                                pu[par][:, s * 512:(s + 1) * 512],
                                wslab[w % 2][:, c * 256 + jj * 128: c * 256 + (jj + 1) * 128],
                                uT_s[:, c * T + hh * 1024 + s * 512: c * T + hh * 1024 + (s + 1) * 512],
                                start=(c == 0), stop=(c == DC - 1))
                    for c in range(DC):
                        for s in range(2):
                            last = pe.matmul(
                                pg[par][:, s * 512:(s + 1) * 512],
                                gslab[w % 2][:, c * 256 + jj * 128: c * 256 + (jj + 1) * 128],
                                uT_s[:, c * T + hh * 1024 + s * 512: c * T + hh * 1024 + (s + 1) * 512],
                                start=(c == 0), stop=(c == DC - 1))
                    return last
                mup = PE.emit(upmm)
                AC.wait(PE, mup)
                AC.wait(DV, m_stt_prev)
                msil = AC.emit(lambda ac, jc=jc, par=par: ac.activation(
                    silb2[:, :], pg[par][:, :], AF.Silu, scale=wsgT[:, jc:jc + 1]))
                DV.wait(AC, msil)
                if jc >= 2:
                    DV.wait(SYV[jc % 2], m_vst[jc - 2])
                DV.wait(DV, DV.n)
                m_stt_prev = DV.emit(lambda dv, jc=jc, par=par: dv.scalar_tensor_tensor(
                    vtmp2[:, :], pu[par][:, :], wsupT[:, jc:jc + 1], silb2[:, :],
                    ALU.mult, ALU.mult))
                DV.wait(DV, DV.n)
                mvd = DV.emit(lambda dv, jc=jc, hh=hh: dv.tensor_scalar(
                    vchB[jc % 2][:, hh * 1024:(hh + 1) * 1024], vtmp2[:, :],
                    sudpT[:, jc:jc + 1], None, ALU.mult))
                m_vdv[jc * 2 + hh] = mvd
            SY.wait(DV, m_vdv[jc * 2 + 1])
            mst = SYV[jc % 2].emit(lambda sy, jc=jc: sy.dma_start(
                out=v_dram[jc * 128:(jc + 1) * 128, :], in_=vchB[jc % 2][:, :]))
            m_vst.append(mst)
            if jc % 2 == 1:
                m_win_pe[jc // 2] = PE.n
        m_b1_end = dict(pe=PE.n, dv=DV.n, ac=AC.n, sy=SY.n)

        if kphase < 8:
            _cut()
            return nc
        _close_stack(stB1p)
        _close_stack(stB1)
        stB2 = contextlib.ExitStack()
        _open_stacks.append(stB2)
        vq = stB2.enter_context(nc.sbuf_tensor("vq", [128, JC * 512], BF16))
        btmp2 = stB2.enter_context(nc.sbuf_tensor("btmp2", [128, 32 * 512], BF16))
        pq = [stB2.enter_context(nc.psum_tensor(f"pq{i}", [128, 512], F32)) for i in range(2)]

        m_v2st = {}
        for q in range(4):
            if q == 0:
                SY.wait(SYV[0], SYV[0].n)
                SY.wait(SYV[1], SYV[1].n)
            if q >= 1:
                SY.wait(SYV2, m_v2st[q - 1])
            for c in range(JC):
                SYVQ.emit(lambda sy, q=q, c=c: sy.dma_start(
                    out=vq[:, c * 512:(c + 1) * 512],
                    in_=v_dram[c * 128:(c + 1) * 128, q * 512:(q + 1) * 512]))
            mldq = SYVQ.n
            prev_evq = [0, 0]
            mmq_last = None
            for c in range(JC):
                par = c % 2
                PE.wait(SYVQ, mldq)
                PE.wait(AC, prev_evq[par])
                mmq = PE.emit(lambda pe, c=c, par=par: pe.matmul(
                    pq[par][:, :], h128_s[:, :], vq[:, c * 512:(c + 1) * 512],
                    start=True, stop=True))
                AC.wait(PE, mmq)
                mevq = AC.emit(lambda ac, c=c, par=par: ac.copy(
                    vq[:, c * 512:(c + 1) * 512], pq[par][:, :]))
                prev_evq[par] = mevq
            m_h128q = AC.n
            DV.wait(AC, m_h128q)
            prev = DV.n
            for dist in (1, 2, 4, 8, 16, 32):
                blkw = dist * 512
                ngrp = JC // (2 * dist)

                def bfq(dv, dist=dist, blkw=blkw, ngrp=ngrp, which=0):
                    vv = vq[:, :].rearrange("p (g two q2) -> p g two q2", two=2, q2=blkw)
                    a = vv[:, :, 0, :]
                    bb = vv[:, :, 1, :]
                    t = btmp2[:, 0:ngrp * blkw].rearrange("p (g q2) -> p g q2", q2=blkw)
                    if which == 0:
                        return dv.tensor_add(t, a, bb)
                    elif which == 1:
                        return dv.tensor_sub(bb, a, bb)
                    else:
                        return dv.tensor_copy(a, t)
                DV.wait(DV, prev)
                DV.emit(lambda dv, f=bfq: f(dv, which=0))
                DV.wait(DV, DV.n)
                DV.emit(lambda dv, f=bfq: f(dv, which=1))
                DV.wait(DV, DV.n)
                prev = DV.emit(lambda dv, f=bfq: f(dv, which=2))
            SY.wait(DV, prev)
            for c in range(JC):
                SYV2.emit(lambda sy, q=q, c=c: sy.dma_start(
                    out=v2_dram[c * 128:(c + 1) * 128, q * 512:(q + 1) * 512],
                    in_=vq[:, c * 512:(c + 1) * 512]))
            m_v2st[q] = SYV2.n
        m_b2_end_sy = SYV2.n
        m_b2_end_pe = PE.n
        m_b2_end_dv = DV.n

        if kphase < 9:
            _cut()
            return nc
        _close_stack(stB2)
        stB3 = contextlib.ExitStack()
        _open_stacks.append(stB3)
        acc = stB3.enter_context(nc.sbuf_tensor("acc", [128, TC * 512], F32))
        v2sl = stB3.enter_context(nc.sbuf_tensor("v2sl", [128, 16 * T], BF16))
        wdsl = stB3.enter_context(nc.sbuf_tensor("wdsl", [128, 16 * 512], BF16))
        ep = [stB3.enter_context(nc.sbuf_tensor(f"ep{i}", [128, 512], F32)) for i in range(2)]
        epw = stB3.enter_context(nc.sbuf_tensor("epw", [128, 512], F32))
        wsdp_b = stB3.enter_context(nc.sbuf_tensor("wsdp_b", [128, D], F32))
        pd = [stB3.enter_context(nc.psum_tensor(f"pd{i}", [128, 512], F32)) for i in range(4)]

        SYR.emit(lambda sy: sy.dma_start(out=wsdp_b[0:1, :], in_=wsdpin[:, :]))
        m_wsl = SYR.n
        PE.wait(SYR, m_wsl)
        for s in range(4):
            mmm = PE.emit(lambda pe, s=s: pe.matmul(
                pd[0][:, :], onesr[0:1, :], wsdp_b[0:1, s * 512:(s + 1) * 512],
                start=True, stop=True))
            DV.wait(PE, mmm)
            mvv = DV.emit(lambda dv, s=s: dv.tensor_scalar_mul(
                wsdp_b[:, s * 512:(s + 1) * 512], pd[0][:, :], SCALE))
            PE.wait(DV, mvv)
        m_wsb = DV.n

        m_yst = []
        m_padd = {}
        DV.wait(SYV2, m_b2_end_sy)
        for ihq in range(4):
            m_accz = DV.emit(lambda dv: dv.memset(acc[:, :], 0.0))
            m_add = {}
            for g in range(4):
                SY.wait(SYV2, m_b2_end_sy)
                if not (ihq == 0 and g == 0):
                    SY.wait(PE, m_g_pe)
                for jj in range(16):
                    SYSL.emit(lambda sy, g=g, jj=jj: sy.dma_start(
                        out=v2sl[:, jj * T:(jj + 1) * T],
                        in_=v2_dram[(g * 16 + jj) * 128:(g * 16 + jj + 1) * 128, :]))
                mv2 = SYSL.n
                if not (ihq == 0 and g == 0):
                    GP.wait(PE, m_g_pe)
                else:
                    GP.wait(PE, m_b2_end_pe)
                    GP.wait(DV, m_b2_end_dv)
                    GP.wait(SYV2, m_b2_end_sy)
                for jj in range(16):
                    GPD.emit(lambda gp, g=g, ihq=ihq, jj=jj: gp.dma_start(
                        out=wdsl[:, jj * 512:(jj + 1) * 512],
                        in_=wdnT[(g * 16 + jj) * 128:(g * 16 + jj + 1) * 128,
                                 ihq * 512:(ihq + 1) * 512]))
                mwd = GPD.n
                for tch in range(TC):
                    par = tch % 4
                    PE.wait(SYSL, mv2)
                    PE.wait(GPD, mwd)
                    PE.wait(DV, m_padd.get(par, 0))
                    PE.wait(DV, m_accz)

                    def dmm(pe, g=g, tch=tch, par=par):
                        last = None
                        for jj in range(16):
                            last = pe.matmul(
                                pd[par][:, :],
                                v2sl[:, jj * T + tch * 128: jj * T + (tch + 1) * 128],
                                wdsl[:, jj * 512:(jj + 1) * 512],
                                start=(jj == 0), stop=(jj == 15))
                        return last
                    mdm = PE.emit(dmm)
                    DV.wait(PE, mdm)
                    madd = DV.emit(lambda dv, tch=tch, par=par: dv.tensor_add(
                        acc[:, tch * 512:(tch + 1) * 512],
                        acc[:, tch * 512:(tch + 1) * 512], pd[par][:, :]))
                    m_add[(g, par)] = madd
                    m_padd[par] = madd
                m_g_pe = PE.n
            for tch in range(TC):
                b = tch % 2
                if SYY[b].n > 0:
                    SY.wait(SYY[b], SYY[b].n)
                if len(m_yst) >= 2:
                    SY.wait(DV, m_ep_dv[-2])
                mxl = SYE[b].emit(lambda sy, tch=tch, ihq=ihq, b=b: sy.dma_start(
                    out=ep[b][:, :],
                    in_=x2_dram[tch * 128:(tch + 1) * 128, ihq * 512:(ihq + 1) * 512]))
                DV.wait(SYE[b], mxl)
                DV.wait(DV, m_add[(3, tch % 4)])
                DV.emit(lambda dv, tch=tch, ihq=ihq: dv.tensor_mul(
                    epw[:, :], acc[:, tch * 512:(tch + 1) * 512],
                    wsdp_b[:, ihq * 512:(ihq + 1) * 512]))
                DV.wait(DV, DV.n)
                mepd = DV.emit(lambda dv, b=b: dv.tensor_add(
                    ep[b][:, :], ep[b][:, :], epw[:, :]))
                try:
                    m_ep_dv.append(mepd)
                except NameError:
                    m_ep_dv = [mepd]
                SY.wait(DV, mepd)
                myst = SYY[b].emit(lambda sy, tch=tch, ihq=ihq, b=b: sy.dma_start(
                    out=yout[tch * 128:(tch + 1) * 128, ihq * 512:(ihq + 1) * 512],
                    in_=ep[b][:, :]))
                m_yst.append(myst)
        _close_stack(stB3)

        SY.wait(SYY[0], SYY[0].n)
        SY.wait(SYY[1], SYY[1].n)
        SY.wait(SY, SY.n)
        GP.wait(GP, GP.n)

        _flush_all(blk, PE, DV, AC, SY, GP)

    return nc


def _hadamard(n):
    h = np.array([[1.0]], dtype=np.float64)
    while h.shape[0] < n:
        h = np.block([[h, h], [h, -h]])
    return h


_NC_CACHE = [None]


def kernel(**inputs):
    import ml_dtypes
    x = np.asarray(inputs["L0_attn_x"])
    W = np.asarray(inputs["L0_attn_W"])
    lam_logit = np.asarray(inputs["L0_attn_lam_logit"]).reshape(1, 4)
    eta = np.asarray(inputs["L0_attn_eta"]).reshape(1, 4)
    gamma = np.asarray(inputs["L0_attn_gamma"]).reshape(1, D)
    beta = np.asarray(inputs["L0_attn_beta"]).reshape(1, D)
    su_ug = np.asarray(inputs["L0_ffn_SU_ug"]).reshape(1, D)
    su_dp = np.asarray(inputs["L0_ffn_SU_dp"]).reshape(64, 128)
    ws_ug = np.asarray(inputs["L0_ffn_Wscale_ug"])
    ws_dp = np.asarray(inputs["L0_ffn_Wscale_dp"]).reshape(1, D)
    W_up = np.asarray(inputs["L0_ffn_W_up"])
    W_gate = np.asarray(inputs["L0_ffn_W_gate"])
    W_down = np.asarray(inputs["L0_ffn_W_down"])

    xf = np.ascontiguousarray(x.reshape(NCORE * T, D))
    xprev = np.ascontiguousarray(xf[:PREV * 128, :])
    wzT = np.ascontiguousarray(W.T)
    wupT = np.ascontiguousarray(W_up.T)
    wgateT = np.ascontiguousarray(W_gate.T)
    wdnT = np.ascontiguousarray(W_down.T)

    h128 = (_hadamard(128)).astype(ml_dtypes.bfloat16)
    i128b = np.eye(128, dtype=np.float32).astype(ml_dtypes.bfloat16)
    i128f = np.eye(128, dtype=np.float32)
    ti = np.arange(128, dtype=np.float32)
    tms = (ti[None, :] - ti[:, None]).astype(np.float32)
    mge = (ti[None, :] >= ti[:, None]).astype(np.float32)
    riota = np.repeat((127.0 - ti)[:, None], 4, axis=1).astype(np.float32)
    grid4 = np.repeat(ti[None, :], 4, axis=0).astype(np.float32)

    common = {
        "wzT": wzT, "wupT": wupT, "wgateT": wgateT, "wdnT": wdnT,
        "lamlog": lam_logit.astype(np.float32), "etain": eta.astype(np.float32),
        "gammain": gamma.astype(np.float32), "betain": beta.astype(np.float32),
        "suugin": su_ug.astype(np.float32), "sudpin": su_dp.astype(np.float32),
        "wsupin": np.ascontiguousarray(ws_ug[:J].reshape(64, 128)).astype(np.float32),
        "wsgin": np.ascontiguousarray(ws_ug[J:].reshape(64, 128)).astype(np.float32),
        "wsdpin": ws_dp.astype(np.float32),
        "h128in": h128, "i128fin": i128f, "i128bin": i128b,
        "tmsin": tms, "mgein": mge, "riotain": riota, "grid4in": grid4,
        "xfull": xprev,
    }
    in_maps = []
    for c in range(NCORE):
        tau = np.arange(128, dtype=np.float64)[:, None]
        jj = np.arange(PREV, dtype=np.float64)[None, :]
        expo = (2048.0 * c - 1.0) - 128.0 * jj - tau
        live = (jj < 16 * c) & np.ones((128, 1), bool)
        expo = np.where(live, expo, 1.0e6)
        pmask = live.astype(np.float32)
        m = dict(common)
        m["xown"] = np.ascontiguousarray(xf[c * T:(c + 1) * T, :])
        m["expoin"] = expo.astype(np.float32)
        m["pmaskin"] = np.ascontiguousarray(pmask)
        in_maps.append(m)

    if _NC_CACHE[0] is None:
        _NC_CACHE[0] = build_nc()
    nc = _NC_CACHE[0]
    res = run_bass_kernel_spmd(nc, in_maps, core_ids=list(range(NCORE)))
    out = np.concatenate([np.asarray(r["yout"]) for r in res.results], axis=0)
    return out.reshape(1, NCORE * T, D).astype(np.float32)


# version 03
# speedup vs baseline: 1.0337x; 1.0337x over previous
import numpy as np
import contextlib
import concourse.bass as bass
import concourse.mybir as mybir
from concourse.bass_utils import run_bass_kernel_spmd

F32 = mybir.dt.float32
BF16 = mybir.dt.bfloat16
AF = mybir.ActivationFunctionType
ALU = mybir.AluOpType
AX = mybir.AxisListType

NCORE = 8
T = 2048
D = 2048
J = 8192
TC = T // 128
DC = D // 128
JC = J // 128
PREV = (NCORE - 1) * TC
EPS = 1e-5
SCALE = 64.0


class Q:

    def __init__(self, name, dma=False):
        self.name = name
        self.dma = dma
        self.sem = None
        self.ops = []
        self.n = 0

    def emit(self, f, inc=None):
        if inc is None:
            inc = 16 if self.dma else 1
        sem = lambda: self.sem

        def op(eng, f=f, inc=inc):
            ins = f(eng)
            ins.then_inc(self.sem, inc)
        self.ops.append(op)
        self.n += inc
        return self.n

    def raw(self, f):
        self.ops.append(lambda eng, f=f: f(eng))

    def wait(self, other, val):
        if val > 0:
            self.ops.append(lambda eng, o=other, v=val: eng.wait_ge(o.sem, v))


class _CutDone(Exception):
    def __init__(self, env):
        self.env = env


def _flush_all(blk, PE, DV, AC, SY, GP):
    def flush(q, eng):
        q.eng = eng
        for op in q.ops:
            op(eng)
    blk.tensor(lambda eng: flush(PE, eng))
    blk.vector(lambda eng: flush(DV, eng))
    blk.scalar(lambda eng: flush(AC, eng))
    blk.sync(lambda eng: flush(SY, eng))
    blk.gpsimd(lambda eng: flush(GP, eng))


class SubQ:

    def __init__(self, parent):
        self.parent = parent
        self.sem = None
        self.n = 0

    def emit(self, f):
        def op(eng, f=f):
            f(eng).then_inc(self.sem, 16)
        self.parent.ops.append(op)
        self.n += 16
        return self.n


def build_nc():
    import os
    kphase = int(os.environ.get("KPHASE", "99"))
    ka0 = int(os.environ.get("KA0", "99"))
    nc = bass.Bass("TRN2", target_bir_lowering=False)

    xfull = nc.dram_tensor("xfull", [PREV * 128, D], F32, kind="ExternalInput")
    xown = nc.dram_tensor("xown", [T, D], F32, kind="ExternalInput")
    wzT = nc.dram_tensor("wzT", [D, D], F32, kind="ExternalInput")
    wupT = nc.dram_tensor("wupT", [D, J], F32, kind="ExternalInput")
    wgateT = nc.dram_tensor("wgateT", [D, J], F32, kind="ExternalInput")
    wdnT = nc.dram_tensor("wdnT", [J, D], F32, kind="ExternalInput")
    lamlog = nc.dram_tensor("lamlog", [1, 4], F32, kind="ExternalInput")
    etain = nc.dram_tensor("etain", [1, 4], F32, kind="ExternalInput")
    gammain = nc.dram_tensor("gammain", [1, D], F32, kind="ExternalInput")
    betain = nc.dram_tensor("betain", [1, D], F32, kind="ExternalInput")
    suugin = nc.dram_tensor("suugin", [1, D], F32, kind="ExternalInput")
    sudpin = nc.dram_tensor("sudpin", [64, 128], F32, kind="ExternalInput")
    wsupin = nc.dram_tensor("wsupin", [64, 128], F32, kind="ExternalInput")
    wsgin = nc.dram_tensor("wsgin", [64, 128], F32, kind="ExternalInput")
    wsdpin = nc.dram_tensor("wsdpin", [1, D], F32, kind="ExternalInput")
    h128in = nc.dram_tensor("h128in", [128, 128], BF16, kind="ExternalInput")
    i128fin = nc.dram_tensor("i128fin", [128, 128], F32, kind="ExternalInput")
    i128bin = nc.dram_tensor("i128bin", [128, 128], BF16, kind="ExternalInput")
    tmsin = nc.dram_tensor("tmsin", [128, 128], F32, kind="ExternalInput")
    mgein = nc.dram_tensor("mgein", [128, 128], F32, kind="ExternalInput")
    riotain = nc.dram_tensor("riotain", [128, 4], F32, kind="ExternalInput")
    grid4in = nc.dram_tensor("grid4in", [4, 128], F32, kind="ExternalInput")
    expoin = nc.dram_tensor("expoin", [128, PREV], F32, kind="ExternalInput")
    pmaskin = nc.dram_tensor("pmaskin", [128, PREV], F32, kind="ExternalInput")

    yout = nc.dram_tensor("yout", [T, D], F32, kind="ExternalOutput")
    x2_dram = nc.dram_tensor("x2_dram", [T, D], F32, kind="Internal")
    wup_bf = nc.dram_tensor("wup_bf", [D, J], BF16, kind="Internal")
    wgate_bf = nc.dram_tensor("wgate_bf", [D, J], BF16, kind="Internal")
    v_dram = nc.dram_tensor("v_dram", [J, T], BF16, kind="Internal")
    v2_dram = nc.dram_tensor("v2_dram", [J, T], BF16, kind="Internal")

    PE, DV, AC = Q("pe"), Q("dv"), Q("ac")
    SY, GP = Q("sy", dma=True), Q("gp", dma=True)
    QS = (PE, DV, AC, SY, GP)

    SYG = [SubQ(SY), SubQ(SY)]
    SYX2 = SubQ(SY)
    SYW = [SubQ(SY), SubQ(SY)]
    SYV = [SubQ(SY), SubQ(SY)]
    SYVQ = SubQ(SY)
    SYV2 = SubQ(SY)
    SYSL = SubQ(SY)
    SYE = [SubQ(SY), SubQ(SY)]
    SYY = [SubQ(SY), SubQ(SY)]
    SYR = SubQ(SY)
    GPW = SubQ(GP)
    GPZ = SubQ(GP)
    GPB = SubQ(GP)
    GPD = SubQ(GP)
    SUBS = [SYG[0], SYG[1], SYX2, SYW[0], SYW[1], SYV[0], SYV[1], SYVQ, SYV2,
            SYSL, SYE[0], SYE[1], SYY[0], SYY[1], SYR, GPW, GPZ, GPB, GPD]

    with contextlib.ExitStack() as top:
        for q, nm in zip(QS, ("s_pe", "s_dv", "s_ac", "s_sy", "s_gp")):
            q.sem = top.enter_context(nc.semaphore(nm))
        for i, sq in enumerate(SUBS):
            sq.sem = top.enter_context(nc.semaphore(f"s_sub{i}"))

        def sb(name, shape, dt=F32):
            return top.enter_context(nc.sbuf_tensor(name, shape, dt))

        h128_s = sb("h128_s", [128, 128], BF16)
        i128f_s = sb("i128f_s", [128, 128], F32)
        i128b_s = sb("i128b_s", [128, 128], BF16)
        tms_s = sb("tms_s", [128, 128], F32)
        mge_s = sb("mge_s", [128, 128], F32)
        riota_s = sb("riota_s", [128, 4], F32)
        grid4_s = sb("grid4_s", [4, 128], F32)
        consts = sb("consts", [128, 8], F32)
        onesr = sb("onesr", [1, 128], F32)
        srow = sb("srow", [1, 16], F32)
        scol = sb("scol", [4, 4], F32)
        lnlam_b = sb("lnlam_b", [128, 4], F32)
        eta_b = sb("eta_b", [128, 4], F32)
        bmat = sb("bmat", [128, 128], BF16)
        bmf = sb("bmf", [128, 128], F32)
        cscr = sb("cscr", [128, 128], F32)
        clcoef = sb("clcoef", [128, 4], BF16)
        aloc = sb("aloc", [4, 128], BF16)
        am = sb("am", [4, TC * 128], BF16)
        a4t = sb("a4t", [4, 128], F32)
        rowload = sb("rowload", [64, 128], F32)
        sudpT = sb("sudpT", [128, 64], F32)
        wsupT = sb("wsupT", [128, 64], F32)
        wsgT = sb("wsgT", [128, 64], F32)
        marow = sb("marow", [4, D], BF16)
        carry = sb("carry", [4, D], F32)
        carry_bf = sb("carry_bf", [4, D], BF16)
        pprev_bf = sb("pprev_bf", [128, 64], BF16)
        expo_s = sb("expo_s", [128, PREV], F32)
        pmask_s = sb("pmask_s", [128, PREV], F32)
        statc = sb("statc", [128, 16], F32)
        w4f = sb("w4f", [128, 4], F32)
        w4b = [sb(f"w4b{i}", [128, 4], BF16) for i in range(2)]
        G = [sb(f"G{i}", [128, D], F32) for i in range(2)]
        G2h = sb("G2h", [128, 1024], F32)
        Hh = [sb(f"H{i}", [128, D], BF16) for i in range(2)]
        zbf = sb("zbf", [128, D], BF16)
        hb = sb("hb", [128, D], F32)
        onesb = sb("onesb", [1, 128], BF16)

        blk = top.enter_context(nc.Block())
        _open_stacks = []

        def _close_stack(s):
            s.close()
            _open_stacks.remove(s)

        def _cut():
            for s in reversed(_open_stacks):
                s.close()
            _open_stacks.clear()
            _flush_all(blk, PE, DV, AC, SY, GP)


        def emit_stats(src_tile_fn, sink_fn, sp, extra_dv_wait=None):
            AC.emit(lambda ac: ac.activation(sink_fn(), src_tile_fn(), AF.Identity,
                                             accum_out=statc[:, sp:sp + 1]))
            AC.wait(AC, AC.n)
            mss = AC.emit(lambda ac: ac.activation(sink_fn(), src_tile_fn(), AF.Square,
                                                   accum_out=statc[:, sp + 2:sp + 3]))
            DV.wait(AC, mss)
            if extra_dv_wait is not None:
                DV.wait(*extra_dv_wait)
            DV.emit(lambda dv: dv.tensor_scalar_mul(statc[:, sp + 1:sp + 2],
                                                    statc[:, sp:sp + 1], -1.0 / D))
            DV.emit(lambda dv: dv.tensor_scalar_mul(statc[:, sp + 3:sp + 4],
                                                    statc[:, sp + 2:sp + 3], 1.0 / D))
            DV.wait(DV, DV.n)
            DV.emit(lambda dv: dv.tensor_mul(statc[:, sp + 5:sp + 6],
                                             statc[:, sp + 1:sp + 2], statc[:, sp + 1:sp + 2]))
            DV.wait(DV, DV.n)
            mvar = DV.emit(lambda dv: dv.tensor_sub(statc[:, sp + 3:sp + 4],
                                                    statc[:, sp + 3:sp + 4], statc[:, sp + 5:sp + 6]))
            AC.wait(DV, mvar)
            AC.emit(lambda ac: ac.activation(statc[:, sp + 4:sp + 5], statc[:, sp + 3:sp + 4],
                                             AF.Ln, bias=consts[:, 0:1]))
            AC.wait(AC, AC.n)
            mrs = AC.emit(lambda ac: ac.activation(statc[:, sp + 4:sp + 5], statc[:, sp + 4:sp + 5],
                                                   AF.Exp, scale=consts[:, 2:3]))
            return mvar, mrs

        for c in range(DC):
            GPW.emit(lambda gp, c=c: gp.dma_start(
                out=wup_bf[c * 128:(c + 1) * 128, :], in_=wupT[c * 128:(c + 1) * 128, :]))
        for c in range(DC):
            GPW.emit(lambda gp, c=c: gp.dma_start(
                out=wgate_bf[c * 128:(c + 1) * 128, :], in_=wgateT[c * 128:(c + 1) * 128, :]))
        m_gp_upgate = GPW.n

        m_sy0 = [None]
        for nm, (dst, src) in {
            "h128": (h128_s, h128in), "i128f": (i128f_s, i128fin), "i128b": (i128b_s, i128bin),
            "tms": (tms_s, tmsin), "mge": (mge_s, mgein), "riota": (riota_s, riotain),
            "grid4": (grid4_s, grid4in), "expo": (expo_s, expoin), "pmask": (pmask_s, pmaskin),
        }.items():
            SY.emit(lambda sy, d=dst, s=src: sy.dma_start(out=d[:, :], in_=s[:, :]))
        SY.emit(lambda sy: sy.dma_start(out=srow[0:1, 0:4], in_=lamlog[:, :]))
        SY.emit(lambda sy: sy.dma_start(out=srow[0:1, 8:12], in_=etain[:, :]))
        m_syc = SY.n

        stP = contextlib.ExitStack()
        _open_stacks.append(stP)
        pstat = stP.enter_context(nc.psum_tensor("pstat", [128, 512], F32))
        pp2 = stP.enter_context(nc.psum_tensor("pp2", [128, 512], F32))
        ptA = [stP.enter_context(nc.psum_tensor(f"ptA{i}", [128, 1024], BF16))
               for i in range(2)]

        for col, val in ((0, EPS), (1, 1.0), (2, -0.5), (3, -1.0)):
            DV.emit(lambda dv, c=col, v=val: dv.memset(consts[:, c:c + 1], v))
        DV.emit(lambda dv: dv.memset(onesr[:, :], 1.0))
        DV.emit(lambda dv: dv.memset(onesb[:, :], 1.0))
        DV.emit(lambda dv: dv.memset(carry[:, :], 0.0))
        DV.emit(lambda dv: dv.memset(carry_bf[:, :], 0.0))
        m_dvc = DV.n

        AC.wait(SY, m_syc)
        AC.wait(DV, m_dvc)
        m1 = AC.emit(lambda ac: ac.activation(srow[0:1, 0:4], srow[0:1, 0:4], AF.Exp,
                                              scale=consts[0:1, 3:4]))
        DV.wait(AC, m1)
        DV.emit(lambda dv: dv.tensor_scalar_add(srow[0:1, 0:4], srow[0:1, 0:4], 1.0))
        DV.wait(DV, DV.n)
        m2 = DV.emit(lambda dv: dv.reciprocal(srow[0:1, 0:4], srow[0:1, 0:4]))
        AC.wait(DV, m2)
        m3 = AC.emit(lambda ac: ac.activation(srow[0:1, 4:8], srow[0:1, 0:4], AF.Ln))
        DV.wait(AC, m3)
        m4 = DV.emit(lambda dv: dv.tensor_scalar_mul(srow[0:1, 12:16], srow[0:1, 4:8], 128.0))
        AC.wait(DV, m4)
        m5 = AC.emit(lambda ac: ac.activation(srow[0:1, 12:16], srow[0:1, 12:16], AF.Exp))

        PE.wait(AC, m5)
        PE.emit(lambda pe: pe.matmul(pstat[:, 0:4], onesr[0:1, :], srow[0:1, 4:8],
                                     start=True, stop=True))
        PE.emit(lambda pe: pe.matmul(pstat[:, 4:8], onesr[0:1, :], srow[0:1, 8:12],
                                     start=True, stop=True))
        for i, c0 in enumerate((0, 4, 8, 12)):
            PE.emit(lambda pe, i=i, c0=c0: pe.transpose(
                pstat[0:4, 8 + i:9 + i], srow[0:1, c0:c0 + 4], i128f_s[0:1, 0:1]))
        m6 = PE.n
        DV.wait(PE, m6)
        DV.emit(lambda dv: dv.tensor_copy(lnlam_b[:, :], pstat[:, 0:4]))
        DV.emit(lambda dv: dv.tensor_copy(eta_b[:, :], pstat[:, 4:8]))
        m7 = DV.emit(lambda dv: dv.tensor_copy(scol[0:4, 0:4], pstat[0:4, 8:12]))
        m_scalars = m7

        SYR.emit(lambda sy: sy.dma_start(out=rowload[:, :], in_=sudpin[:, :]))
        m8 = SYR.n
        PE.wait(SYR, m8)
        PE.wait(DV, m7)
        m9 = PE.emit(lambda pe: pe.transpose(pstat[:, 0:64], rowload[0:64, :], i128f_s[0:64, 0:64]))
        DV.wait(PE, m9)
        m10 = DV.emit(lambda dv: dv.tensor_scalar_mul(
            sudpT[:, :], pstat[:, 0:64], 1.0 / (SCALE * float(np.sqrt(J)))))
        SY.wait(DV, m10)
        SYR.emit(lambda sy: sy.dma_start(out=rowload[:, :], in_=wsupin[:, :]))
        m11 = SYR.n
        PE.wait(SYR, m11)
        PE.wait(DV, m10)
        m12 = PE.emit(lambda pe: pe.transpose(pstat[:, 64:128], rowload[0:64, :], i128f_s[0:64, 0:64]))
        DV.wait(PE, m12)
        m13 = DV.emit(lambda dv: dv.tensor_scalar_mul(wsupT[:, :], pstat[:, 64:128], SCALE))
        SY.wait(PE, m12)
        SYR.emit(lambda sy: sy.dma_start(out=rowload[:, :], in_=wsgin[:, :]))
        m14 = SYR.n
        PE.wait(SYR, m14)
        PE.wait(DV, m13)
        m15 = PE.emit(lambda pe: pe.transpose(pstat[:, 128:192], rowload[0:64, :], i128f_s[0:64, 0:64]))
        DV.wait(PE, m15)
        m16 = DV.emit(lambda dv: dv.tensor_scalar_mul(wsgT[:, :], pstat[:, 128:192], SCALE))

        DV.wait(DV, m_scalars)
        DV.emit(lambda dv: dv.memset(bmf[:, :], 0.0))
        for k in range(4):
            DV.wait(DV, DV.n)
            mk = DV.emit(lambda dv, k=k: dv.tensor_scalar(
                cscr[:, :], tms_s[:, :], lnlam_b[:, k:k + 1], None, ALU.mult))
            AC.wait(DV, mk)
            ma = AC.emit(lambda ac: ac.activation(cscr[:, :], cscr[:, :], AF.Exp))
            DV.wait(AC, ma)
            DV.emit(lambda dv, k=k: dv.tensor_scalar(
                cscr[:, :], cscr[:, :], eta_b[:, k:k + 1], None, ALU.mult))
            DV.wait(DV, DV.n)
            DV.emit(lambda dv: dv.tensor_add(bmf[:, :], bmf[:, :], cscr[:, :]))
        DV.wait(DV, DV.n)
        DV.emit(lambda dv: dv.tensor_mul(bmf[:, :], bmf[:, :], mge_s[:, :]))
        DV.wait(DV, DV.n)
        DV.emit(lambda dv: dv.tensor_add(bmf[:, :], bmf[:, :], i128f_s[:, :]))
        DV.wait(DV, DV.n)
        m_bmat = DV.emit(lambda dv: dv.tensor_copy(bmat[:, :], bmf[:, :]))

        DV.emit(lambda dv: dv.tensor_mul(w4f[:, :], riota_s[:, :], lnlam_b[:, :]))
        DV.wait(DV, DV.n)
        mcl = DV.n
        AC.wait(DV, mcl)
        ma = AC.emit(lambda ac: ac.activation(w4f[:, :], w4f[:, :], AF.Exp))
        DV.wait(AC, ma)
        m_cl = DV.emit(lambda dv: dv.tensor_copy(clcoef[:, :], w4f[:, :]))

        def coef4(dst_ap_fn, off):
            DV.wait(DV, DV.n)
            DV.emit(lambda dv: dv.tensor_scalar_add(a4t[:, :], grid4_s[:, :], float(off)))
            DV.wait(DV, DV.n)
            mx = DV.emit(lambda dv: dv.tensor_scalar(
                a4t[:, :], a4t[:, :], scol[0:4, 1:2], None, ALU.mult))
            AC.wait(DV, mx)
            my = AC.emit(lambda ac: ac.activation(a4t[:, :], a4t[:, :], AF.Exp))
            DV.wait(AC, my)
            return DV.emit(lambda dv: dv.tensor_scalar(
                dst_ap_fn(), a4t[:, :], scol[0:4, 2:3], None, ALU.mult))

        coef4(lambda: aloc[:, :], 1.0)
        for jj in range(TC):
            coef4(lambda jj=jj: am[:, jj * 128:(jj + 1) * 128], 128.0 * jj + 1.0)
        m_coefs = DV.n

        if kphase < 2:
            _cut()
            return nc
        m_prep_ld = []
        m_prep_dv = []
        m_prep_ac = []
        m_prep_pe = []
        for jp in range(PREV):
            b = jp % 2
            if jp >= 2:
                SY.wait(AC, m_prep_ac[jp - 2])
            SY.wait(DV, m_dvc)
            m_ld = SYG[b].emit(lambda sy, jp=jp, b=b: sy.dma_start(
                out=G[b][:, :], in_=xfull[jp * 128:(jp + 1) * 128, :]))
            m_prep_ld.append(m_ld)
            sp = 8 * (jp % 2)
            AC.wait(SYG[b], m_ld)
            if jp >= 2:
                AC.wait(DV, m_prep_dv[jp - 2])
                AC.wait(AC, m_prep_ac[jp - 2])
            mvar, mrs = emit_stats(lambda b=b: G[b][:, :], lambda: zbf[:, :], sp,
                                   extra_dv_wait=(DV, m_coefs) if jp == 0 else None)
            DV.wait(AC, mrs)
            if jp >= 1:
                DV.wait(AC, m_prep_ac[jp - 1])
            DV.emit(lambda dv, jp=jp: dv.tensor_scalar(
                w4f[:, :], lnlam_b[:, :], expo_s[:, jp:jp + 1], None, ALU.mult))
            DV.wait(DV, DV.n)
            mw = DV.n
            AC.wait(DV, mw)
            mwe = AC.emit(lambda ac: ac.activation(w4f[:, :], w4f[:, :], AF.Exp))
            DV.wait(AC, mwe)
            if jp >= 2:
                DV.wait(PE, m_prep_pe[jp - 2])
            DV.emit(lambda dv, sp=sp: dv.tensor_scalar(
                w4f[:, :], w4f[:, :], statc[:, sp + 4:sp + 5], None, ALU.mult))
            DV.wait(DV, DV.n)
            mw4 = DV.emit(lambda dv, jp=jp, b=b: dv.tensor_scalar(
                w4b[b][:, :], w4f[:, :], pmask_s[:, jp:jp + 1], None, ALU.mult))
            m_prep_dv.append(mw4)
            AC.wait(DV, mw4)
            if jp >= 2:
                AC.wait(PE, m_prep_pe[jp - 2])
            mcast = AC.emit(lambda ac, b=b, sp=sp: ac.activation(
                Hh[b][:, :], G[b][:, :], AF.Identity, bias=statc[:, sp + 1:sp + 2]))
            m_prep_ac.append(mcast)
            PE.wait(AC, mcast)
            PE.wait(DV, mw4)

            def pmm(pe, jp=jp, b=b):
                last = None
                for c in range(DC):
                    last = pe.matmul(pp2[:, 4 * c:4 * c + 4],
                                     Hh[b][:, c * 128:(c + 1) * 128],
                                     w4b[b][:, :],
                                     start=(jp == 0 and c == 0),
                                     stop=(jp == PREV - 1 and c == DC - 1))
                return last
            m_pe = PE.emit(pmm)
            m_prep_pe.append(m_pe)
        DV.wait(PE, m_prep_pe[-1])
        m_pev = DV.emit(lambda dv: dv.tensor_copy(pprev_bf[:, :], pp2[:, 0:64]))

        if kphase < 3:
            _cut()
            return nc
        stA = contextlib.ExitStack()
        _open_stacks.append(stA)
        wzT_s = stA.enter_context(nc.sbuf_tensor("wzT_s", [128, DC * D], BF16))
        xlnT_s = stA.enter_context(nc.sbuf_tensor("xlnT_s", [128, DC * T], BF16))
        gam_b = stA.enter_context(nc.sbuf_tensor("gam_b", [128, D], BF16))
        bet_b = stA.enter_context(nc.sbuf_tensor("bet_b", [128, D], BF16))

        for c in range(DC):
            GPZ.emit(lambda gp, c=c: gp.dma_start(
                out=wzT_s[:, c * D:(c + 1) * D], in_=wzT[c * 128:(c + 1) * 128, :]))
        GPB.emit(lambda gp: gp.dma_start(out=gam_b[0:1, :], in_=gammain[:, :]))
        GPB.emit(lambda gp: gp.dma_start(out=bet_b[0:1, :], in_=betain[:, :]))
        m_gp_wz = GPZ.n
        m_gb = GPB.n

        PE.wait(GPB, m_gb)
        PE.wait(DV, m16)
        for row_t in (gam_b, bet_b):
            for s in range(4):
                mmm = PE.emit(lambda pe, r=row_t, s=s: pe.matmul(
                    pstat[:, 0:512], onesb[0:1, :], r[0:1, s * 512:(s + 1) * 512],
                    start=True, stop=True))
                DV.wait(PE, mmm)
                mvv = DV.emit(lambda dv, r=row_t, s=s: dv.tensor_copy(
                    r[:, s * 512:(s + 1) * 512], pstat[:, 0:512]))
                PE.wait(DV, mvv)
        m_gbb = DV.n

        PE.wait(DV, m_pev)
        PE.wait(GPZ, m_gp_wz)
        for s in range(4):
            def mmm(pe, s=s):
                last = None
                for c in range(DC):
                    last = pe.matmul(pstat[0:4, 0:512],
                                     pprev_bf[:, 4 * c:4 * c + 4],
                                     wzT_s[:, c * D + s * 512: c * D + (s + 1) * 512],
                                     start=(c == 0), stop=(c == DC - 1))
                return last
            mp = PE.emit(mmm)
            DV.wait(PE, mp)
            md = DV.emit(lambda dv, s=s: dv.tensor_copy(
                marow[0:4, s * 512:(s + 1) * 512], pstat[0:4, 0:512]))
            PE.wait(DV, md)
        m_M = DV.n

        if kphase < 4:
            _cut()
            return nc
        m_a0_ac = []
        m_prep_end_dv = DV.n
        m_prep_end_ac = AC.n
        m_prep_end_pe = PE.n
        slot_user = [0] * 2
        for j in range(TC):
            b = j % 2
            SY.wait(AC, m_prep_end_ac)
            if j >= 2:
                SY.wait(AC, m_a0_ac[j - 2])
            m_ld = SYG[b].emit(lambda sy, j=j, b=b: sy.dma_start(
                out=G[b][:, :], in_=xown[j * 128:(j + 1) * 128, :]))
            DV.wait(SYG[b], m_ld)
            DV.emit(lambda dv, b=b: dv.reduce_sum(statc[:, 0:1], G[b][:, :], axis=AX.X))
            DV.wait(DV, DV.n)
            DV.emit(lambda dv: dv.tensor_scalar_mul(statc[:, 1:2], statc[:, 0:1], -1.0 / D))
            mdn = DV.n
            AC.wait(DV, mdn)
            msq = AC.emit(lambda ac, b=b: ac.activation(G2h[:, :], G[b][:, 0:1024], AF.Square))
            DV.wait(AC, msq)
            mr0 = DV.emit(lambda dv: dv.reduce_sum(statc[:, 2:3], G2h[:, :], axis=AX.X))
            AC.wait(DV, mr0)
            msq2 = AC.emit(lambda ac, b=b: ac.activation(G2h[:, :], G[b][:, 1024:2048], AF.Square))
            DV.wait(AC, msq2)
            DV.emit(lambda dv: dv.reduce_sum(statc[:, 6:7], G2h[:, :], axis=AX.X))
            DV.wait(DV, DV.n)
            DV.emit(lambda dv: dv.tensor_add(statc[:, 2:3], statc[:, 2:3], statc[:, 6:7]))
            DV.wait(DV, DV.n)
            DV.emit(lambda dv: dv.tensor_scalar_mul(statc[:, 3:4], statc[:, 2:3], 1.0 / D))
            DV.emit(lambda dv: dv.tensor_mul(statc[:, 5:6], statc[:, 1:2], statc[:, 1:2]))
            DV.wait(DV, DV.n)
            mvar = DV.emit(lambda dv: dv.tensor_sub(statc[:, 3:4], statc[:, 3:4], statc[:, 5:6]))
            AC.wait(DV, mvar)
            AC.emit(lambda ac: ac.activation(statc[:, 4:5], statc[:, 3:4], AF.Ln,
                                             bias=consts[:, 0:1]))
            AC.wait(AC, AC.n)
            mrs = AC.emit(lambda ac: ac.activation(statc[:, 4:5], statc[:, 4:5], AF.Exp,
                                                   scale=consts[:, 2:3]))
            if ka0 < 2:
                m_a0_ac.append(AC.n)
                try:
                    m_a0_pe.append(PE.n)
                except NameError:
                    m_a0_pe = [PE.n]
                continue
            DV.wait(AC, mrs)
            if j >= 2:
                DV.wait(PE, m_a0_pe[j - 2])
            else:
                DV.wait(PE, m_prep_end_pe)
            mnorm = DV.emit(lambda dv, b=b: dv.tensor_scalar(
                Hh[b][:, :], G[b][:, :], statc[:, 1:2], statc[:, 4:5], ALU.add, ALU.mult))
            if ka0 < 3:
                m_a0_ac.append(AC.n)
                try:
                    m_a0_pe.append(PE.n)
                except NameError:
                    m_a0_pe = [PE.n]
                continue
            PE.wait(DV, mnorm)
            for c in range(DC):
                slot = c % 2
                PE.wait(AC, slot_user[slot])
                mt = PE.emit(lambda pe, b=b, c=c, slot=slot: pe.transpose(
                    ptA[slot][:, 0:128],
                    Hh[b][:, c * 128:(c + 1) * 128], i128b_s[:, :]))
                AC.wait(PE, mt)
                mcp = AC.emit(lambda ac, c=c, j=j, slot=slot: ac.copy(
                    xlnT_s[:, c * T + j * 128: c * T + (j + 1) * 128],
                    ptA[slot][:, 0:128]))
                slot_user[slot] = mcp
            m_a0_ac.append(AC.n)
            try:
                m_a0_pe.append(PE.n)
            except NameError:
                m_a0_pe = [PE.n]

        m_a0_done_pe = PE.n
        m_a0_end_dv = DV.n

        if kphase < 5:
            _cut()
            return nc
        stAz = contextlib.ExitStack()
        _open_stacks.append(stAz)
        pz = stAz.enter_context(nc.psum_tensor("pz", [128, 1024], F32))
        ph = stAz.enter_context(nc.psum_tensor("ph", [128, 1024], F32))

        m_zev = 0
        m_hev = 0
        m_ccast = 0
        m_clmm = [0, 0]
        m_x2st = []
        for j in range(TC):
            b = j % 2
            for hh in range(2):
                PE.wait(AC, m_a0_ac[j])
                PE.wait(GPZ, m_gp_wz)
                PE.wait(AC, m_zev)
                PE.wait(DV, m_ccast)

                def zmm(pe, j=j, hh=hh):
                    last = None
                    for c in range(DC):
                        for s in range(2):
                            last = pe.matmul(
                                pz[:, s * 512:(s + 1) * 512],
                                xlnT_s[:, c * T + j * 128: c * T + (j + 1) * 128],
                                wzT_s[:, c * D + hh * 1024 + s * 512: c * D + hh * 1024 + (s + 1) * 512],
                                start=(c == 0), stop=(c == DC - 1))
                    return last
                mz = PE.emit(zmm)
                AC.wait(PE, mz)
                AC.wait(PE, m_clmm[hh])
                m_zev = AC.emit(lambda ac, hh=hh: ac.copy(
                    zbf[:, hh * 1024:(hh + 1) * 1024], pz[:, :]))
                PE.wait(AC, m_zev)
                PE.wait(DV, max(m_hev, m_M, m_ccast))

                def hmm(pe, j=j, hh=hh):
                    last = None
                    for s in range(2):
                        sl = slice(hh * 1024 + s * 512, hh * 1024 + (s + 1) * 512)
                        psl = slice(s * 512, (s + 1) * 512)
                        pe.matmul(ph[:, psl], bmat[:, :], zbf[:, sl], start=True, stop=False)
                        pe.matmul(ph[:, psl], aloc[:, :], carry_bf[0:4, sl], start=False, stop=False)
                        last = pe.matmul(ph[:, psl], am[:, j * 128:(j + 1) * 128],
                                         marow[0:4, sl], start=False, stop=True)
                    return last
                mh = PE.emit(hmm)
                DV.wait(PE, mh)
                if j >= 1:
                    DV.wait(SYX2, m_x2st[j - 1])
                m_hev = DV.emit(lambda dv, hh=hh: dv.tensor_copy(
                    hb[:, hh * 1024:(hh + 1) * 1024], ph[:, :]))
                PE.wait(AC, m_zev)
                PE.wait(DV, m_ccast)

                def clmm(pe, hh=hh):
                    last = None
                    for s in range(2):
                        sl = slice(hh * 1024 + s * 512, hh * 1024 + (s + 1) * 512)
                        last = pe.matmul(pz[0:4, s * 512:(s + 1) * 512], clcoef[:, :],
                                         zbf[:, sl], start=True, stop=True)
                    return last
                mcl2 = PE.emit(clmm)
                m_clmm[hh] = mcl2
                DV.wait(PE, mcl2)
                DV.emit(lambda dv, hh=hh: dv.scalar_tensor_tensor(
                    carry[0:4, hh * 1024:(hh + 1) * 1024],
                    carry[0:4, hh * 1024:(hh + 1) * 1024],
                    scol[0:4, 3:4], pz[0:4, 0:1024], ALU.mult, ALU.add))
                DV.wait(DV, DV.n)
                m_ccast = DV.emit(lambda dv, hh=hh: dv.tensor_copy(
                    carry_bf[0:4, hh * 1024:(hh + 1) * 1024],
                    carry[0:4, hh * 1024:(hh + 1) * 1024]))
            sp = 8 * (j % 2)
            AC.wait(DV, m_hev)
            if j >= 2:
                AC.wait(DV, m_x2dv[j - 2])
            mvar, mrs = emit_stats(lambda: hb[:, :], lambda: Hh[0][:, :], sp)
            if j >= 2:
                SY.wait(DV, m_x2dv[j - 2])
            else:
                SY.wait(DV, m_a0_end_dv)
            SY.wait(DV, m_a0_end_dv)
            mxl = SYG[b].emit(lambda sy, j=j, b=b: sy.dma_start(
                out=G[b][:, :], in_=xown[j * 128:(j + 1) * 128, :]))
            DV.wait(AC, mrs)
            DV.emit(lambda dv, sp=sp: dv.tensor_scalar(
                hb[:, :], hb[:, :], statc[:, sp + 1:sp + 2], statc[:, sp + 4:sp + 5],
                ALU.add, ALU.mult))
            DV.wait(DV, DV.n)
            DV.emit(lambda dv: dv.tensor_mul(hb[:, :], hb[:, :], gam_b[:, :]))
            DV.wait(DV, DV.n)
            DV.emit(lambda dv: dv.tensor_add(hb[:, :], hb[:, :], bet_b[:, :]))
            DV.wait(DV, DV.n)
            DV.wait(SYG[b], mxl)
            mx2 = DV.emit(lambda dv, b=b: dv.tensor_add(hb[:, :], hb[:, :], G[b][:, :]))
            try:
                m_x2dv.append(mx2)
            except NameError:
                m_x2dv = [mx2]
            SY.wait(DV, mx2)
            mst = SYX2.emit(lambda sy, j=j: sy.dma_start(
                out=x2_dram[j * 128:(j + 1) * 128, :], in_=hb[:, :]))
            m_x2st.append(mst)

        if kphase < 6:
            _cut()
            return nc
        m_a1z_end = dict(pe=PE.n, dv=DV.n, ac=AC.n, sy=SY.n)
        _close_stack(stAz)
        _close_stack(stP)
        _close_stack(stA)

        stB1 = contextlib.ExitStack()
        _open_stacks.append(stB1)
        uT_s = stB1.enter_context(nc.sbuf_tensor("uT_s", [128, DC * T], BF16))
        su_b = stB1.enter_context(nc.sbuf_tensor("su_b", [128, D], F32))
        wslab = [stB1.enter_context(nc.sbuf_tensor(f"wslab{i}", [128, 16 * 256], BF16)) for i in range(2)]
        gslab = [stB1.enter_context(nc.sbuf_tensor(f"gslab{i}", [128, 16 * 256], BF16)) for i in range(2)]
        btmp = stB1.enter_context(nc.sbuf_tensor("btmp", [128, 1024], BF16))
        silb2 = stB1.enter_context(nc.sbuf_tensor("silb2", [128, 1024], F32))
        vtmp2 = stB1.enter_context(nc.sbuf_tensor("vtmp2", [128, 1024], F32))
        vchB = [stB1.enter_context(nc.sbuf_tensor(f"vchB{i}", [128, T], BF16)) for i in range(2)]

        stB0 = contextlib.ExitStack()
        _open_stacks.append(stB0)
        ptB = [stB0.enter_context(nc.psum_tensor(f"ptB{i}", [128, 1024], BF16))
               for i in range(2)]
        pH0 = stB0.enter_context(nc.psum_tensor("pH0", [128, 512], F32))
        pH1 = stB0.enter_context(nc.psum_tensor("pH1", [128, 512], F32))

        SYR.emit(lambda sy: sy.dma_start(out=su_b[0:1, :], in_=suugin[:, :]))
        m_sul = SYR.n
        PE.wait(SYR, m_sul)
        su_scale = 1.0 / (SCALE * float(np.sqrt(D)))
        for s in range(4):
            mmm = PE.emit(lambda pe, s=s: pe.matmul(
                pH0[:, 0:512], onesr[0:1, :], su_b[0:1, s * 512:(s + 1) * 512],
                start=True, stop=True))
            DV.wait(PE, mmm)
            mvv = DV.emit(lambda dv, s=s: dv.tensor_scalar_mul(
                su_b[:, s * 512:(s + 1) * 512], pH0[:, 0:512], su_scale))
            PE.wait(DV, mvv)
        m_sub = DV.n

        m_b0_ac = []
        slotB = [0] * 2
        m_b0_pe_chunks = []
        for j in range(TC):
            b = j % 2
            SY.wait(SYX2, m_x2st[j])
            if j >= 2:
                SY.wait(DV, m_b0_norm[j - 2])
            else:
                SY.wait(DV, m_a1z_end["dv"])
            mld = SYG[b].emit(lambda sy, j=j, b=b: sy.dma_start(
                out=G[b][:, :], in_=x2_dram[j * 128:(j + 1) * 128, :]))
            AC.wait(SYG[b], mld)
            if j < 2:
                AC.wait(DV, m_a1z_end["dv"])
            sp = 8 * (j % 2)
            if j >= 2:
                AC.wait(DV, m_b0_norm[j - 2])
            mvar, mrs = emit_stats(lambda b=b: G[b][:, :], lambda: zbf[:, :], sp)
            DV.wait(AC, mrs)
            DV.wait(DV, m_sub)
            if j >= 2:
                DV.wait(PE, m_b0_pe_chunks[j - 2])
            else:
                DV.wait(PE, m_a1z_end["pe"])
            for hx in range(2):
                DV.emit(lambda dv, b=b, hx=hx, sp=sp: dv.tensor_scalar(
                    G2h[:, :], G[b][:, hx * 1024:(hx + 1) * 1024],
                    statc[:, sp + 1:sp + 2], statc[:, sp + 4:sp + 5], ALU.add, ALU.mult))
                DV.wait(DV, DV.n)
                DV.emit(lambda dv, b=b, hx=hx: dv.tensor_mul(
                    Hh[b][:, hx * 1024:(hx + 1) * 1024], G2h[:, :],
                    su_b[:, hx * 1024:(hx + 1) * 1024]))
                DV.wait(DV, DV.n)
            mu_ = DV.n
            prev = mu_
            for st_i, dist in enumerate((1, 2, 4, 8)):
                blkw = dist * 128
                ngrp = DC // (2 * dist)

                def bfly(dv, b=b, dist=dist, blkw=blkw, ngrp=ngrp, which=0):
                    vv = Hh[b][:, :].rearrange("p (g two q) -> p g two q", two=2, q=blkw)
                    a = vv[:, :, 0, :]
                    bb = vv[:, :, 1, :]
                    t = btmp[:, 0:ngrp * blkw].rearrange("p (g q) -> p g q", q=blkw)
                    if which == 0:
                        return dv.tensor_add(t, a, bb)
                    elif which == 1:
                        return dv.tensor_sub(bb, a, bb)
                    else:
                        return dv.tensor_copy(a, t)
                DV.wait(DV, prev)
                DV.emit(lambda dv, f=bfly: f(dv, which=0))
                DV.wait(DV, DV.n)
                DV.emit(lambda dv, f=bfly: f(dv, which=1))
                DV.wait(DV, DV.n)
                prev = DV.emit(lambda dv, f=bfly: f(dv, which=2))
            m_b0n = prev
            try:
                m_b0_norm.append(m_b0n)
            except NameError:
                m_b0_norm = [m_b0n]
            PE.wait(DV, m_b0n)
            for c in range(DC):
                slot = c % 2
                PE.wait(AC, slotB[slot])
                mt = PE.emit(lambda pe, b=b, c=c, slot=slot: pe.transpose(
                    ptB[slot][:, 0:128],
                    Hh[b][:, c * 128:(c + 1) * 128], i128b_s[:, :]))
                AC.wait(PE, mt)
                mcp = AC.emit(lambda ac, c=c, j=j, slot=slot: ac.copy(
                    uT_s[:, c * T + j * 128: c * T + (j + 1) * 128],
                    ptB[slot][:, 0:128]))
                slotB[slot] = mcp
            m_b0_ac.append(AC.n)
            m_b0_pe_chunks.append(PE.n)
        m_h128 = []
        PE.wait(AC, m_b0_ac[-1])
        prev_ev = [0, 0]
        for c in range(DC):
            for s in range(4):
                par = (c * 4 + s) % 2
                ptgt = pH0 if par == 0 else pH1
                PE.wait(AC, prev_ev[par])
                mm_ = PE.emit(lambda pe, c=c, s=s, ptgt=ptgt: pe.matmul(
                    ptgt[:, 0:512], h128_s[:, :],
                    uT_s[:, c * T + s * 512: c * T + (s + 1) * 512],
                    start=True, stop=True))
                AC.wait(PE, mm_)
                mev = AC.emit(lambda ac, c=c, s=s, ptgt=ptgt: ac.copy(
                    uT_s[:, c * T + s * 512: c * T + (s + 1) * 512], ptgt[:, 0:512]))
                prev_ev[par] = mev
        m_uT = AC.n

        if kphase < 7:
            _cut()
            return nc
        _close_stack(stB0)
        stB1p = contextlib.ExitStack()
        _open_stacks.append(stB1p)
        pu = [stB1p.enter_context(nc.psum_tensor(f"pu{i}", [128, 1024], F32)) for i in range(2)]
        pg = [stB1p.enter_context(nc.psum_tensor(f"pg{i}", [128, 1024], F32)) for i in range(2)]

        def emit_slab(w, wbuf, gbuf):
            sq = SYW[w % 2]
            for c in range(DC):
                sq.emit(lambda sy, w=w, c=c, wbuf=wbuf: sy.dma_start(
                    out=wbuf[:, c * 256:(c + 1) * 256],
                    in_=wup_bf[c * 128:(c + 1) * 128, w * 256:(w + 1) * 256]))
            for c in range(DC):
                sq.emit(lambda sy, w=w, c=c, gbuf=gbuf: sy.dma_start(
                    out=gbuf[:, c * 256:(c + 1) * 256],
                    in_=wgate_bf[c * 128:(c + 1) * 128, w * 256:(w + 1) * 256]))
            return sq.n

        m_slab = {}
        m_win_pe = {}
        SY.wait(GPW, m_gp_upgate)
        m_slab[0] = emit_slab(0, wslab[0], gslab[0])
        m_vst = []
        m_vdv = {}
        m_stt_prev = 0
        for jc in range(JC):
            w = jc // 2
            if jc % 2 == 0 and w + 1 < JC // 2:
                if w >= 1:
                    SY.wait(PE, m_win_pe[w - 1])
                m_slab[w + 1] = emit_slab(w + 1, wslab[(w + 1) % 2], gslab[(w + 1) % 2])
            for hh in range(2):
                par = (jc * 2 + hh) % 2
                PE.wait(SYW[w % 2], m_slab[w])
                PE.wait(AC, m_uT)
                if (jc * 2 + hh) >= 2:
                    PE.wait(DV, m_vdv[(jc * 2 + hh) - 2])

                def upmm(pe, jc=jc, hh=hh, w=w, par=par):
                    jj = jc % 2
                    last = None
                    for c in range(DC):
                        for s in range(2):
                            last = pe.matmul(
                                pu[par][:, s * 512:(s + 1) * 512],
                                wslab[w % 2][:, c * 256 + jj * 128: c * 256 + (jj + 1) * 128],
                                uT_s[:, c * T + hh * 1024 + s * 512: c * T + hh * 1024 + (s + 1) * 512],
                                start=(c == 0), stop=(c == DC - 1))
                    for c in range(DC):
                        for s in range(2):
                            last = pe.matmul(
                                pg[par][:, s * 512:(s + 1) * 512],
                                gslab[w % 2][:, c * 256 + jj * 128: c * 256 + (jj + 1) * 128],
                                uT_s[:, c * T + hh * 1024 + s * 512: c * T + hh * 1024 + (s + 1) * 512],
                                start=(c == 0), stop=(c == DC - 1))
                    return last
                mup = PE.emit(upmm)
                AC.wait(PE, mup)
                AC.wait(DV, m_stt_prev)
                msil = AC.emit(lambda ac, jc=jc, par=par: ac.activation(
                    silb2[:, :], pg[par][:, :], AF.Silu, scale=wsgT[:, jc:jc + 1]))
                DV.wait(AC, msil)
                if jc >= 2:
                    DV.wait(SYV[jc % 2], m_vst[jc - 2])
                DV.wait(DV, DV.n)
                m_stt_prev = DV.emit(lambda dv, jc=jc, par=par: dv.scalar_tensor_tensor(
                    vtmp2[:, :], pu[par][:, :], wsupT[:, jc:jc + 1], silb2[:, :],
                    ALU.mult, ALU.mult))
                DV.wait(DV, DV.n)
                mvd = DV.emit(lambda dv, jc=jc, hh=hh: dv.tensor_scalar(
                    vchB[jc % 2][:, hh * 1024:(hh + 1) * 1024], vtmp2[:, :],
                    sudpT[:, jc:jc + 1], None, ALU.mult))
                m_vdv[jc * 2 + hh] = mvd
            SY.wait(DV, m_vdv[jc * 2 + 1])
            mst = SYV[jc % 2].emit(lambda sy, jc=jc: sy.dma_start(
                out=v_dram[jc * 128:(jc + 1) * 128, :], in_=vchB[jc % 2][:, :]))
            m_vst.append(mst)
            if jc % 2 == 1:
                m_win_pe[jc // 2] = PE.n
        m_b1_end = dict(pe=PE.n, dv=DV.n, ac=AC.n, sy=SY.n)

        if kphase < 8:
            _cut()
            return nc
        _close_stack(stB1p)
        _close_stack(stB1)
        stB2 = contextlib.ExitStack()
        _open_stacks.append(stB2)
        vq = stB2.enter_context(nc.sbuf_tensor("vq", [128, JC * 512], BF16))
        btmp2 = stB2.enter_context(nc.sbuf_tensor("btmp2", [128, 32 * 512], BF16))
        pq = [stB2.enter_context(nc.psum_tensor(f"pq{i}", [128, 512], F32)) for i in range(2)]

        m_v2st = {}
        for q in range(4):
            if q == 0:
                SY.wait(SYV[0], SYV[0].n)
                SY.wait(SYV[1], SYV[1].n)
            if q >= 1:
                SY.wait(SYV2, m_v2st[q - 1])
            for c in range(JC):
                SYVQ.emit(lambda sy, q=q, c=c: sy.dma_start(
                    out=vq[:, c * 512:(c + 1) * 512],
                    in_=v_dram[c * 128:(c + 1) * 128, q * 512:(q + 1) * 512]))
            mldq = SYVQ.n
            prev_evq = [0, 0]
            mmq_last = None
            for c in range(JC):
                par = c % 2
                PE.wait(SYVQ, mldq)
                PE.wait(AC, prev_evq[par])
                mmq = PE.emit(lambda pe, c=c, par=par: pe.matmul(
                    pq[par][:, :], h128_s[:, :], vq[:, c * 512:(c + 1) * 512],
                    start=True, stop=True))
                AC.wait(PE, mmq)
                mevq = AC.emit(lambda ac, c=c, par=par: ac.copy(
                    vq[:, c * 512:(c + 1) * 512], pq[par][:, :]))
                prev_evq[par] = mevq
            m_h128q = AC.n
            DV.wait(AC, m_h128q)
            prev = DV.n
            for dist in (1, 2, 4, 8, 16, 32):
                blkw = dist * 512
                ngrp = JC // (2 * dist)

                def bfq(dv, dist=dist, blkw=blkw, ngrp=ngrp, which=0):
                    vv = vq[:, :].rearrange("p (g two q2) -> p g two q2", two=2, q2=blkw)
                    a = vv[:, :, 0, :]
                    bb = vv[:, :, 1, :]
                    t = btmp2[:, 0:ngrp * blkw].rearrange("p (g q2) -> p g q2", q2=blkw)
                    if which == 0:
                        return dv.tensor_add(t, a, bb)
                    elif which == 1:
                        return dv.tensor_sub(bb, a, bb)
                    else:
                        return dv.tensor_copy(a, t)
                DV.wait(DV, prev)
                DV.emit(lambda dv, f=bfq: f(dv, which=0))
                DV.wait(DV, DV.n)
                DV.emit(lambda dv, f=bfq: f(dv, which=1))
                DV.wait(DV, DV.n)
                prev = DV.emit(lambda dv, f=bfq: f(dv, which=2))
            SY.wait(DV, prev)
            for c in range(JC):
                SYV2.emit(lambda sy, q=q, c=c: sy.dma_start(
                    out=v2_dram[c * 128:(c + 1) * 128, q * 512:(q + 1) * 512],
                    in_=vq[:, c * 512:(c + 1) * 512]))
            m_v2st[q] = SYV2.n
        m_b2_end_sy = SYV2.n
        m_b2_end_pe = PE.n
        m_b2_end_dv = DV.n

        if kphase < 9:
            _cut()
            return nc
        _close_stack(stB2)
        stB3 = contextlib.ExitStack()
        _open_stacks.append(stB3)
        acc = stB3.enter_context(nc.sbuf_tensor("acc", [128, TC * 512], F32))
        v2sl = stB3.enter_context(nc.sbuf_tensor("v2sl", [128, 16 * T], BF16))
        wdsl = stB3.enter_context(nc.sbuf_tensor("wdsl", [128, 16 * 512], BF16))
        ep = [stB3.enter_context(nc.sbuf_tensor(f"ep{i}", [128, 512], F32)) for i in range(2)]
        epw = stB3.enter_context(nc.sbuf_tensor("epw", [128, 512], F32))
        wsdp_b = stB3.enter_context(nc.sbuf_tensor("wsdp_b", [128, D], F32))
        pd = [stB3.enter_context(nc.psum_tensor(f"pd{i}", [128, 512], F32)) for i in range(4)]

        SYR.emit(lambda sy: sy.dma_start(out=wsdp_b[0:1, :], in_=wsdpin[:, :]))
        m_wsl = SYR.n
        PE.wait(SYR, m_wsl)
        for s in range(4):
            mmm = PE.emit(lambda pe, s=s: pe.matmul(
                pd[0][:, :], onesr[0:1, :], wsdp_b[0:1, s * 512:(s + 1) * 512],
                start=True, stop=True))
            DV.wait(PE, mmm)
            mvv = DV.emit(lambda dv, s=s: dv.tensor_scalar_mul(
                wsdp_b[:, s * 512:(s + 1) * 512], pd[0][:, :], SCALE))
            PE.wait(DV, mvv)
        m_wsb = DV.n

        m_yst = []
        m_padd = {}
        DV.wait(SYV2, m_b2_end_sy)
        for ihq in range(4):
            m_accz = DV.emit(lambda dv: dv.memset(acc[:, :], 0.0))
            m_add = {}
            for g in range(4):
                SY.wait(SYV2, m_b2_end_sy)
                if not (ihq == 0 and g == 0):
                    SY.wait(PE, m_g_pe)
                for jj in range(16):
                    SYSL.emit(lambda sy, g=g, jj=jj: sy.dma_start(
                        out=v2sl[:, jj * T:(jj + 1) * T],
                        in_=v2_dram[(g * 16 + jj) * 128:(g * 16 + jj + 1) * 128, :]))
                mv2 = SYSL.n
                if not (ihq == 0 and g == 0):
                    GP.wait(PE, m_g_pe)
                else:
                    GP.wait(PE, m_b2_end_pe)
                    GP.wait(DV, m_b2_end_dv)
                    GP.wait(SYV2, m_b2_end_sy)
                for jj in range(16):
                    GPD.emit(lambda gp, g=g, ihq=ihq, jj=jj: gp.dma_start(
                        out=wdsl[:, jj * 512:(jj + 1) * 512],
                        in_=wdnT[(g * 16 + jj) * 128:(g * 16 + jj + 1) * 128,
                                 ihq * 512:(ihq + 1) * 512]))
                mwd = GPD.n
                for tch in range(TC):
                    par = tch % 4
                    PE.wait(SYSL, mv2)
                    PE.wait(GPD, mwd)
                    PE.wait(DV, m_padd.get(par, 0))
                    PE.wait(DV, m_accz)

                    def dmm(pe, g=g, tch=tch, par=par):
                        last = None
                        for jj in range(16):
                            last = pe.matmul(
                                pd[par][:, :],
                                v2sl[:, jj * T + tch * 128: jj * T + (tch + 1) * 128],
                                wdsl[:, jj * 512:(jj + 1) * 512],
                                start=(jj == 0), stop=(jj == 15))
                        return last
                    mdm = PE.emit(dmm)
                    DV.wait(PE, mdm)
                    madd = DV.emit(lambda dv, tch=tch, par=par: dv.tensor_add(
                        acc[:, tch * 512:(tch + 1) * 512],
                        acc[:, tch * 512:(tch + 1) * 512], pd[par][:, :]))
                    m_add[(g, par)] = madd
                    m_padd[par] = madd
                m_g_pe = PE.n
            for tch in range(TC):
                b = tch % 2
                if SYY[b].n > 0:
                    SY.wait(SYY[b], SYY[b].n)
                if len(m_yst) >= 2:
                    SY.wait(DV, m_ep_dv[-2])
                mxl = SYE[b].emit(lambda sy, tch=tch, ihq=ihq, b=b: sy.dma_start(
                    out=ep[b][:, :],
                    in_=x2_dram[tch * 128:(tch + 1) * 128, ihq * 512:(ihq + 1) * 512]))
                DV.wait(SYE[b], mxl)
                DV.wait(DV, m_add[(3, tch % 4)])
                DV.emit(lambda dv, tch=tch, ihq=ihq: dv.tensor_mul(
                    epw[:, :], acc[:, tch * 512:(tch + 1) * 512],
                    wsdp_b[:, ihq * 512:(ihq + 1) * 512]))
                DV.wait(DV, DV.n)
                mepd = DV.emit(lambda dv, b=b: dv.tensor_add(
                    ep[b][:, :], ep[b][:, :], epw[:, :]))
                try:
                    m_ep_dv.append(mepd)
                except NameError:
                    m_ep_dv = [mepd]
                SY.wait(DV, mepd)
                myst = SYY[b].emit(lambda sy, tch=tch, ihq=ihq, b=b: sy.dma_start(
                    out=yout[tch * 128:(tch + 1) * 128, ihq * 512:(ihq + 1) * 512],
                    in_=ep[b][:, :]))
                m_yst.append(myst)
        _close_stack(stB3)

        SY.wait(SYY[0], SYY[0].n)
        SY.wait(SYY[1], SYY[1].n)
        SY.wait(SY, SY.n)
        GP.wait(GP, GP.n)

        _flush_all(blk, PE, DV, AC, SY, GP)

    return nc


def _hadamard(n):
    h = np.array([[1.0]], dtype=np.float64)
    while h.shape[0] < n:
        h = np.block([[h, h], [h, -h]])
    return h


_NC_CACHE = [None]


def kernel(**inputs):
    import ml_dtypes
    x = np.asarray(inputs["L0_attn_x"])
    W = np.asarray(inputs["L0_attn_W"])
    lam_logit = np.asarray(inputs["L0_attn_lam_logit"]).reshape(1, 4)
    eta = np.asarray(inputs["L0_attn_eta"]).reshape(1, 4)
    gamma = np.asarray(inputs["L0_attn_gamma"]).reshape(1, D)
    beta = np.asarray(inputs["L0_attn_beta"]).reshape(1, D)
    su_ug = np.asarray(inputs["L0_ffn_SU_ug"]).reshape(1, D)
    su_dp = np.asarray(inputs["L0_ffn_SU_dp"]).reshape(64, 128)
    ws_ug = np.asarray(inputs["L0_ffn_Wscale_ug"])
    ws_dp = np.asarray(inputs["L0_ffn_Wscale_dp"]).reshape(1, D)
    W_up = np.asarray(inputs["L0_ffn_W_up"])
    W_gate = np.asarray(inputs["L0_ffn_W_gate"])
    W_down = np.asarray(inputs["L0_ffn_W_down"])

    xf = np.ascontiguousarray(x.reshape(NCORE * T, D))
    xprev = np.ascontiguousarray(xf[:PREV * 128, :])
    wzT = np.ascontiguousarray(W.T)
    wupT = np.ascontiguousarray(W_up.T)
    wgateT = np.ascontiguousarray(W_gate.T)
    wdnT = np.ascontiguousarray(W_down.T)

    h128 = (_hadamard(128)).astype(ml_dtypes.bfloat16)
    i128b = np.eye(128, dtype=np.float32).astype(ml_dtypes.bfloat16)
    i128f = np.eye(128, dtype=np.float32)
    ti = np.arange(128, dtype=np.float32)
    tms = (ti[None, :] - ti[:, None]).astype(np.float32)
    mge = (ti[None, :] >= ti[:, None]).astype(np.float32)
    riota = np.repeat((127.0 - ti)[:, None], 4, axis=1).astype(np.float32)
    grid4 = np.repeat(ti[None, :], 4, axis=0).astype(np.float32)

    common = {
        "wzT": wzT, "wupT": wupT, "wgateT": wgateT, "wdnT": wdnT,
        "lamlog": lam_logit.astype(np.float32), "etain": eta.astype(np.float32),
        "gammain": gamma.astype(np.float32), "betain": beta.astype(np.float32),
        "suugin": su_ug.astype(np.float32), "sudpin": su_dp.astype(np.float32),
        "wsupin": np.ascontiguousarray(ws_ug[:J].reshape(64, 128)).astype(np.float32),
        "wsgin": np.ascontiguousarray(ws_ug[J:].reshape(64, 128)).astype(np.float32),
        "wsdpin": ws_dp.astype(np.float32),
        "h128in": h128, "i128fin": i128f, "i128bin": i128b,
        "tmsin": tms, "mgein": mge, "riotain": riota, "grid4in": grid4,
        "xfull": xprev,
    }
    in_maps = []
    for c in range(NCORE):
        tau = np.arange(128, dtype=np.float64)[:, None]
        jj = np.arange(PREV, dtype=np.float64)[None, :]
        expo = (2048.0 * c - 1.0) - 128.0 * jj - tau
        live = (jj < 16 * c) & np.ones((128, 1), bool)
        expo = np.where(live, expo, 1.0e6)
        pmask = live.astype(np.float32)
        m = dict(common)
        m["xown"] = np.ascontiguousarray(xf[c * T:(c + 1) * T, :])
        m["expoin"] = expo.astype(np.float32)
        m["pmaskin"] = np.ascontiguousarray(pmask)
        in_maps.append(m)

    if _NC_CACHE[0] is None:
        _NC_CACHE[0] = build_nc()
    nc = _NC_CACHE[0]
    res = run_bass_kernel_spmd(nc, in_maps, core_ids=list(range(NCORE)))
    out = np.concatenate([np.asarray(r["yout"]) for r in res.results], axis=0)
    return out.reshape(1, NCORE * T, D).astype(np.float32)


# version 04
# speedup vs baseline: 1.0406x; 1.0067x over previous
import numpy as np
import contextlib
import concourse.bass as bass
import concourse.mybir as mybir
from concourse.bass_utils import run_bass_kernel_spmd

F32 = mybir.dt.float32
BF16 = mybir.dt.bfloat16
AF = mybir.ActivationFunctionType
ALU = mybir.AluOpType
AX = mybir.AxisListType

NCORE = 8
T = 2048
D = 2048
J = 8192
TC = T // 128
DC = D // 128
JC = J // 128
PREV = (NCORE - 1) * TC
EPS = 1e-5
SCALE = 64.0


class Q:

    def __init__(self, name, dma=False):
        self.name = name
        self.dma = dma
        self.sem = None
        self.ops = []
        self.n = 0

    def emit(self, f, inc=None):
        if inc is None:
            inc = 16 if self.dma else 1
        sem = lambda: self.sem

        def op(eng, f=f, inc=inc):
            ins = f(eng)
            ins.then_inc(self.sem, inc)
        self.ops.append(op)
        self.n += inc
        return self.n

    def raw(self, f):
        self.ops.append(lambda eng, f=f: f(eng))

    def wait(self, other, val):
        if val > 0:
            self.ops.append(lambda eng, o=other, v=val: eng.wait_ge(o.sem, v))


class _CutDone(Exception):
    def __init__(self, env):
        self.env = env


def _flush_all(blk, PE, DV, AC, SY, GP):
    def flush(q, eng):
        q.eng = eng
        for op in q.ops:
            op(eng)
    blk.tensor(lambda eng: flush(PE, eng))
    blk.vector(lambda eng: flush(DV, eng))
    blk.scalar(lambda eng: flush(AC, eng))
    blk.sync(lambda eng: flush(SY, eng))
    blk.gpsimd(lambda eng: flush(GP, eng))


class SubQ:

    def __init__(self, parent):
        self.parent = parent
        self.sem = None
        self.n = 0

    def emit(self, f):
        def op(eng, f=f):
            f(eng).then_inc(self.sem, 16)
        self.parent.ops.append(op)
        self.n += 16
        return self.n


def build_nc():
    import os
    kphase = int(os.environ.get("KPHASE", "99"))
    ka0 = int(os.environ.get("KA0", "99"))
    nc = bass.Bass("TRN2", target_bir_lowering=False)

    xfull = nc.dram_tensor("xfull", [PREV * 128, D], F32, kind="ExternalInput")
    xown = nc.dram_tensor("xown", [T, D], F32, kind="ExternalInput")
    wzT = nc.dram_tensor("wzT", [D, D], F32, kind="ExternalInput")
    wupT = nc.dram_tensor("wupT", [D, J], F32, kind="ExternalInput")
    wgateT = nc.dram_tensor("wgateT", [D, J], F32, kind="ExternalInput")
    wdnT = nc.dram_tensor("wdnT", [J, D], F32, kind="ExternalInput")
    lamlog = nc.dram_tensor("lamlog", [1, 4], F32, kind="ExternalInput")
    etain = nc.dram_tensor("etain", [1, 4], F32, kind="ExternalInput")
    gammain = nc.dram_tensor("gammain", [1, D], F32, kind="ExternalInput")
    betain = nc.dram_tensor("betain", [1, D], F32, kind="ExternalInput")
    suugin = nc.dram_tensor("suugin", [1, D], F32, kind="ExternalInput")
    sudpin = nc.dram_tensor("sudpin", [64, 128], F32, kind="ExternalInput")
    wsupin = nc.dram_tensor("wsupin", [64, 128], F32, kind="ExternalInput")
    wsgin = nc.dram_tensor("wsgin", [64, 128], F32, kind="ExternalInput")
    wsdpin = nc.dram_tensor("wsdpin", [1, D], F32, kind="ExternalInput")
    h128in = nc.dram_tensor("h128in", [128, 128], BF16, kind="ExternalInput")
    i128fin = nc.dram_tensor("i128fin", [128, 128], F32, kind="ExternalInput")
    i128bin = nc.dram_tensor("i128bin", [128, 128], BF16, kind="ExternalInput")
    tmsin = nc.dram_tensor("tmsin", [128, 128], F32, kind="ExternalInput")
    mgein = nc.dram_tensor("mgein", [128, 128], F32, kind="ExternalInput")
    riotain = nc.dram_tensor("riotain", [128, 4], F32, kind="ExternalInput")
    grid4in = nc.dram_tensor("grid4in", [4, 128], F32, kind="ExternalInput")
    expoin = nc.dram_tensor("expoin", [128, PREV], F32, kind="ExternalInput")
    pmaskin = nc.dram_tensor("pmaskin", [128, PREV], F32, kind="ExternalInput")

    yout = nc.dram_tensor("yout", [T, D], F32, kind="ExternalOutput")
    x2_dram = nc.dram_tensor("x2_dram", [T, D], F32, kind="Internal")
    wup_bf = nc.dram_tensor("wup_bf", [D, J], BF16, kind="Internal")
    wgate_bf = nc.dram_tensor("wgate_bf", [D, J], BF16, kind="Internal")
    v_dram = nc.dram_tensor("v_dram", [J, T], BF16, kind="Internal")
    v2_dram = nc.dram_tensor("v2_dram", [J, T], BF16, kind="Internal")

    PE, DV, AC = Q("pe"), Q("dv"), Q("ac")
    SY, GP = Q("sy", dma=True), Q("gp", dma=True)
    QS = (PE, DV, AC, SY, GP)

    SYG = [SubQ(SY), SubQ(SY)]
    SYX2 = SubQ(SY)
    SYW = [SubQ(SY), SubQ(SY)]
    SYV = [SubQ(SY), SubQ(SY)]
    SYVQ = [SubQ(SY), SubQ(SY)]
    SYV2 = [SubQ(SY), SubQ(SY)]
    SYSL2 = [SubQ(SY), SubQ(SY)]
    GPD2 = [SubQ(GP), SubQ(GP)]
    SYE = [SubQ(SY), SubQ(SY)]
    SYY = [SubQ(SY), SubQ(SY)]
    SYR = SubQ(SY)
    GPW = SubQ(GP)
    GPZ = SubQ(GP)
    GPB = SubQ(GP)
    SUBS = [SYG[0], SYG[1], SYX2, SYW[0], SYW[1], SYV[0], SYV[1],
            SYVQ[0], SYVQ[1], SYV2[0], SYV2[1],
            SYSL2[0], SYSL2[1], GPD2[0], GPD2[1],
            SYE[0], SYE[1], SYY[0], SYY[1], SYR, GPW, GPZ, GPB]

    with contextlib.ExitStack() as top:
        for q, nm in zip(QS, ("s_pe", "s_dv", "s_ac", "s_sy", "s_gp")):
            q.sem = top.enter_context(nc.semaphore(nm))
        for i, sq in enumerate(SUBS):
            sq.sem = top.enter_context(nc.semaphore(f"s_sub{i}"))

        def sb(name, shape, dt=F32):
            return top.enter_context(nc.sbuf_tensor(name, shape, dt))

        h128_s = sb("h128_s", [128, 128], BF16)
        i128f_s = sb("i128f_s", [128, 128], F32)
        i128b_s = sb("i128b_s", [128, 128], BF16)
        tms_s = sb("tms_s", [128, 128], F32)
        mge_s = sb("mge_s", [128, 128], F32)
        riota_s = sb("riota_s", [128, 4], F32)
        grid4_s = sb("grid4_s", [4, 128], F32)
        consts = sb("consts", [128, 8], F32)
        onesr = sb("onesr", [1, 128], F32)
        srow = sb("srow", [1, 16], F32)
        scol = sb("scol", [4, 4], F32)
        lnlam_b = sb("lnlam_b", [128, 4], F32)
        eta_b = sb("eta_b", [128, 4], F32)
        bmat = sb("bmat", [128, 128], BF16)
        bmf = sb("bmf", [128, 128], F32)
        cscr = sb("cscr", [128, 128], F32)
        clcoef = sb("clcoef", [128, 4], BF16)
        aloc = sb("aloc", [4, 128], BF16)
        am = sb("am", [4, TC * 128], BF16)
        a4t = sb("a4t", [4, 128], F32)
        rowload = sb("rowload", [64, 128], F32)
        sudpT = sb("sudpT", [128, 64], F32)
        wsupT = sb("wsupT", [128, 64], F32)
        wsgT = sb("wsgT", [128, 64], F32)
        marow = sb("marow", [4, D], BF16)
        carry = sb("carry", [4, D], F32)
        carry_bf = sb("carry_bf", [4, D], BF16)
        pprev_bf = sb("pprev_bf", [128, 64], BF16)
        expo_s = sb("expo_s", [128, PREV], F32)
        pmask_s = sb("pmask_s", [128, PREV], F32)
        statc = sb("statc", [128, 16], F32)
        w4f = sb("w4f", [128, 4], F32)
        w4b = [sb(f"w4b{i}", [128, 4], BF16) for i in range(2)]
        G = [sb(f"G{i}", [128, D], F32) for i in range(2)]
        G2h = sb("G2h", [128, 1024], F32)
        Hh = [sb(f"H{i}", [128, D], BF16) for i in range(2)]
        zbf = sb("zbf", [128, D], BF16)
        hb = sb("hb", [128, D], F32)
        onesb = sb("onesb", [1, 128], BF16)

        blk = top.enter_context(nc.Block())
        _open_stacks = []

        def _close_stack(s):
            s.close()
            _open_stacks.remove(s)

        def _cut():
            for s in reversed(_open_stacks):
                s.close()
            _open_stacks.clear()
            _flush_all(blk, PE, DV, AC, SY, GP)


        def emit_stats(src_tile_fn, sink_fn, sp, extra_dv_wait=None):
            AC.emit(lambda ac: ac.activation(sink_fn(), src_tile_fn(), AF.Identity,
                                             accum_out=statc[:, sp:sp + 1]))
            AC.wait(AC, AC.n)
            mss = AC.emit(lambda ac: ac.activation(sink_fn(), src_tile_fn(), AF.Square,
                                                   accum_out=statc[:, sp + 2:sp + 3]))
            DV.wait(AC, mss)
            if extra_dv_wait is not None:
                DV.wait(*extra_dv_wait)
            DV.emit(lambda dv: dv.tensor_scalar_mul(statc[:, sp + 1:sp + 2],
                                                    statc[:, sp:sp + 1], -1.0 / D))
            DV.emit(lambda dv: dv.tensor_scalar_mul(statc[:, sp + 3:sp + 4],
                                                    statc[:, sp + 2:sp + 3], 1.0 / D))
            DV.wait(DV, DV.n)
            DV.emit(lambda dv: dv.tensor_mul(statc[:, sp + 5:sp + 6],
                                             statc[:, sp + 1:sp + 2], statc[:, sp + 1:sp + 2]))
            DV.wait(DV, DV.n)
            mvar = DV.emit(lambda dv: dv.tensor_sub(statc[:, sp + 3:sp + 4],
                                                    statc[:, sp + 3:sp + 4], statc[:, sp + 5:sp + 6]))
            AC.wait(DV, mvar)
            AC.emit(lambda ac: ac.activation(statc[:, sp + 4:sp + 5], statc[:, sp + 3:sp + 4],
                                             AF.Ln, bias=consts[:, 0:1]))
            AC.wait(AC, AC.n)
            mrs = AC.emit(lambda ac: ac.activation(statc[:, sp + 4:sp + 5], statc[:, sp + 4:sp + 5],
                                                   AF.Exp, scale=consts[:, 2:3]))
            return mvar, mrs

        for c in range(DC):
            GPW.emit(lambda gp, c=c: gp.dma_start(
                out=wup_bf[c * 128:(c + 1) * 128, :], in_=wupT[c * 128:(c + 1) * 128, :]))
        for c in range(DC):
            GPW.emit(lambda gp, c=c: gp.dma_start(
                out=wgate_bf[c * 128:(c + 1) * 128, :], in_=wgateT[c * 128:(c + 1) * 128, :]))
        m_gp_upgate = GPW.n

        m_sy0 = [None]
        for nm, (dst, src) in {
            "h128": (h128_s, h128in), "i128f": (i128f_s, i128fin), "i128b": (i128b_s, i128bin),
            "tms": (tms_s, tmsin), "mge": (mge_s, mgein), "riota": (riota_s, riotain),
            "grid4": (grid4_s, grid4in), "expo": (expo_s, expoin), "pmask": (pmask_s, pmaskin),
        }.items():
            SY.emit(lambda sy, d=dst, s=src: sy.dma_start(out=d[:, :], in_=s[:, :]))
        SY.emit(lambda sy: sy.dma_start(out=srow[0:1, 0:4], in_=lamlog[:, :]))
        SY.emit(lambda sy: sy.dma_start(out=srow[0:1, 8:12], in_=etain[:, :]))
        m_syc = SY.n

        stP = contextlib.ExitStack()
        _open_stacks.append(stP)
        pstat = stP.enter_context(nc.psum_tensor("pstat", [128, 512], F32))
        pp2 = stP.enter_context(nc.psum_tensor("pp2", [128, 512], F32))
        ptA = [stP.enter_context(nc.psum_tensor(f"ptA{i}", [128, 1024], BF16))
               for i in range(2)]

        for col, val in ((0, EPS), (1, 1.0), (2, -0.5), (3, -1.0)):
            DV.emit(lambda dv, c=col, v=val: dv.memset(consts[:, c:c + 1], v))
        DV.emit(lambda dv: dv.memset(onesr[:, :], 1.0))
        DV.emit(lambda dv: dv.memset(onesb[:, :], 1.0))
        DV.emit(lambda dv: dv.memset(carry[:, :], 0.0))
        DV.emit(lambda dv: dv.memset(carry_bf[:, :], 0.0))
        m_dvc = DV.n

        AC.wait(SY, m_syc)
        AC.wait(DV, m_dvc)
        m1 = AC.emit(lambda ac: ac.activation(srow[0:1, 0:4], srow[0:1, 0:4], AF.Exp,
                                              scale=consts[0:1, 3:4]))
        DV.wait(AC, m1)
        DV.emit(lambda dv: dv.tensor_scalar_add(srow[0:1, 0:4], srow[0:1, 0:4], 1.0))
        DV.wait(DV, DV.n)
        m2 = DV.emit(lambda dv: dv.reciprocal(srow[0:1, 0:4], srow[0:1, 0:4]))
        AC.wait(DV, m2)
        m3 = AC.emit(lambda ac: ac.activation(srow[0:1, 4:8], srow[0:1, 0:4], AF.Ln))
        DV.wait(AC, m3)
        m4 = DV.emit(lambda dv: dv.tensor_scalar_mul(srow[0:1, 12:16], srow[0:1, 4:8], 128.0))
        AC.wait(DV, m4)
        m5 = AC.emit(lambda ac: ac.activation(srow[0:1, 12:16], srow[0:1, 12:16], AF.Exp))

        PE.wait(AC, m5)
        PE.emit(lambda pe: pe.matmul(pstat[:, 0:4], onesr[0:1, :], srow[0:1, 4:8],
                                     start=True, stop=True))
        PE.emit(lambda pe: pe.matmul(pstat[:, 4:8], onesr[0:1, :], srow[0:1, 8:12],
                                     start=True, stop=True))
        for i, c0 in enumerate((0, 4, 8, 12)):
            PE.emit(lambda pe, i=i, c0=c0: pe.transpose(
                pstat[0:4, 8 + i:9 + i], srow[0:1, c0:c0 + 4], i128f_s[0:1, 0:1]))
        m6 = PE.n
        DV.wait(PE, m6)
        DV.emit(lambda dv: dv.tensor_copy(lnlam_b[:, :], pstat[:, 0:4]))
        DV.emit(lambda dv: dv.tensor_copy(eta_b[:, :], pstat[:, 4:8]))
        m7 = DV.emit(lambda dv: dv.tensor_copy(scol[0:4, 0:4], pstat[0:4, 8:12]))
        m_scalars = m7

        SYR.emit(lambda sy: sy.dma_start(out=rowload[:, :], in_=sudpin[:, :]))
        m8 = SYR.n
        PE.wait(SYR, m8)
        PE.wait(DV, m7)
        m9 = PE.emit(lambda pe: pe.transpose(pstat[:, 0:64], rowload[0:64, :], i128f_s[0:64, 0:64]))
        DV.wait(PE, m9)
        m10 = DV.emit(lambda dv: dv.tensor_scalar_mul(
            sudpT[:, :], pstat[:, 0:64], 1.0 / (SCALE * float(np.sqrt(J)))))
        SY.wait(DV, m10)
        SYR.emit(lambda sy: sy.dma_start(out=rowload[:, :], in_=wsupin[:, :]))
        m11 = SYR.n
        PE.wait(SYR, m11)
        PE.wait(DV, m10)
        m12 = PE.emit(lambda pe: pe.transpose(pstat[:, 64:128], rowload[0:64, :], i128f_s[0:64, 0:64]))
        DV.wait(PE, m12)
        m13 = DV.emit(lambda dv: dv.tensor_scalar_mul(wsupT[:, :], pstat[:, 64:128], SCALE))
        SY.wait(PE, m12)
        SYR.emit(lambda sy: sy.dma_start(out=rowload[:, :], in_=wsgin[:, :]))
        m14 = SYR.n
        PE.wait(SYR, m14)
        PE.wait(DV, m13)
        m15 = PE.emit(lambda pe: pe.transpose(pstat[:, 128:192], rowload[0:64, :], i128f_s[0:64, 0:64]))
        DV.wait(PE, m15)
        m16 = DV.emit(lambda dv: dv.tensor_scalar_mul(wsgT[:, :], pstat[:, 128:192], SCALE))

        DV.wait(DV, m_scalars)
        DV.emit(lambda dv: dv.memset(bmf[:, :], 0.0))
        for k in range(4):
            DV.wait(DV, DV.n)
            mk = DV.emit(lambda dv, k=k: dv.tensor_scalar(
                cscr[:, :], tms_s[:, :], lnlam_b[:, k:k + 1], None, ALU.mult))
            AC.wait(DV, mk)
            ma = AC.emit(lambda ac: ac.activation(cscr[:, :], cscr[:, :], AF.Exp))
            DV.wait(AC, ma)
            DV.emit(lambda dv, k=k: dv.tensor_scalar(
                cscr[:, :], cscr[:, :], eta_b[:, k:k + 1], None, ALU.mult))
            DV.wait(DV, DV.n)
            DV.emit(lambda dv: dv.tensor_add(bmf[:, :], bmf[:, :], cscr[:, :]))
        DV.wait(DV, DV.n)
        DV.emit(lambda dv: dv.tensor_mul(bmf[:, :], bmf[:, :], mge_s[:, :]))
        DV.wait(DV, DV.n)
        DV.emit(lambda dv: dv.tensor_add(bmf[:, :], bmf[:, :], i128f_s[:, :]))
        DV.wait(DV, DV.n)
        m_bmat = DV.emit(lambda dv: dv.tensor_copy(bmat[:, :], bmf[:, :]))

        DV.emit(lambda dv: dv.tensor_mul(w4f[:, :], riota_s[:, :], lnlam_b[:, :]))
        DV.wait(DV, DV.n)
        mcl = DV.n
        AC.wait(DV, mcl)
        ma = AC.emit(lambda ac: ac.activation(w4f[:, :], w4f[:, :], AF.Exp))
        DV.wait(AC, ma)
        m_cl = DV.emit(lambda dv: dv.tensor_copy(clcoef[:, :], w4f[:, :]))

        def coef4(dst_ap_fn, off):
            DV.wait(DV, DV.n)
            DV.emit(lambda dv: dv.tensor_scalar_add(a4t[:, :], grid4_s[:, :], float(off)))
            DV.wait(DV, DV.n)
            mx = DV.emit(lambda dv: dv.tensor_scalar(
                a4t[:, :], a4t[:, :], scol[0:4, 1:2], None, ALU.mult))
            AC.wait(DV, mx)
            my = AC.emit(lambda ac: ac.activation(a4t[:, :], a4t[:, :], AF.Exp))
            DV.wait(AC, my)
            return DV.emit(lambda dv: dv.tensor_scalar(
                dst_ap_fn(), a4t[:, :], scol[0:4, 2:3], None, ALU.mult))

        coef4(lambda: aloc[:, :], 1.0)
        for jj in range(TC):
            coef4(lambda jj=jj: am[:, jj * 128:(jj + 1) * 128], 128.0 * jj + 1.0)
        m_coefs = DV.n

        if kphase < 2:
            _cut()
            return nc
        m_prep_ld = []
        m_prep_dv = []
        m_prep_ac = []
        m_prep_pe = []
        for jp in range(PREV):
            b = jp % 2
            if jp >= 2:
                SY.wait(DV, m_prep_ac[jp - 2])
            SY.wait(DV, m_dvc)
            m_ld = SYG[b].emit(lambda sy, jp=jp, b=b: sy.dma_start(
                out=G[b][:, :], in_=xfull[jp * 128:(jp + 1) * 128, :]))
            m_prep_ld.append(m_ld)
            sp = 8 * (jp % 2)
            AC.wait(SYG[b], m_ld)
            if jp >= 2:
                AC.wait(DV, m_prep_dv[jp - 2])
                AC.wait(DV, m_prep_ac[jp - 2])
            mvar, mrs = emit_stats(lambda b=b: G[b][:, :], lambda: zbf[:, :], sp,
                                   extra_dv_wait=(DV, m_coefs) if jp == 0 else None)
            DV.wait(AC, mrs)
            DV.emit(lambda dv, jp=jp: dv.tensor_scalar(
                w4f[:, :], lnlam_b[:, :], expo_s[:, jp:jp + 1], None, ALU.mult))
            DV.wait(DV, DV.n)
            mw = DV.n
            AC.wait(DV, mw)
            mwe = AC.emit(lambda ac: ac.activation(w4f[:, :], w4f[:, :], AF.Exp))
            DV.wait(AC, mwe)
            if jp >= 2:
                DV.wait(PE, m_prep_pe[jp - 2])
            DV.emit(lambda dv, sp=sp: dv.tensor_scalar(
                w4f[:, :], w4f[:, :], statc[:, sp + 4:sp + 5], None, ALU.mult))
            DV.wait(DV, DV.n)
            mw4 = DV.emit(lambda dv, jp=jp, b=b: dv.tensor_scalar(
                w4b[b][:, :], w4f[:, :], pmask_s[:, jp:jp + 1], None, ALU.mult))
            m_prep_dv.append(mw4)
            DV.wait(DV, mw4)
            if jp >= 2:
                DV.wait(PE, m_prep_pe[jp - 2])
            mcast = DV.emit(lambda dv, b=b, sp=sp: dv.tensor_scalar(
                Hh[b][:, :], G[b][:, :], statc[:, sp + 1:sp + 2], None, ALU.add))
            m_prep_ac.append(mcast)
            PE.wait(DV, mcast)

            def pmm(pe, jp=jp, b=b):
                last = None
                for c in range(DC):
                    last = pe.matmul(pp2[:, 4 * c:4 * c + 4],
                                     Hh[b][:, c * 128:(c + 1) * 128],
                                     w4b[b][:, :],
                                     start=(jp == 0 and c == 0),
                                     stop=(jp == PREV - 1 and c == DC - 1))
                return last
            m_pe = PE.emit(pmm)
            m_prep_pe.append(m_pe)
        DV.wait(PE, m_prep_pe[-1])
        m_pev = DV.emit(lambda dv: dv.tensor_copy(pprev_bf[:, :], pp2[:, 0:64]))

        if kphase < 3:
            _cut()
            return nc
        stA = contextlib.ExitStack()
        _open_stacks.append(stA)
        wzT_s = stA.enter_context(nc.sbuf_tensor("wzT_s", [128, DC * D], BF16))
        xlnT_s = stA.enter_context(nc.sbuf_tensor("xlnT_s", [128, DC * T], BF16))
        gam_b = stA.enter_context(nc.sbuf_tensor("gam_b", [128, D], BF16))
        bet_b = stA.enter_context(nc.sbuf_tensor("bet_b", [128, D], BF16))

        for c in range(DC):
            GPZ.emit(lambda gp, c=c: gp.dma_start(
                out=wzT_s[:, c * D:(c + 1) * D], in_=wzT[c * 128:(c + 1) * 128, :]))
        GPB.emit(lambda gp: gp.dma_start(out=gam_b[0:1, :], in_=gammain[:, :]))
        GPB.emit(lambda gp: gp.dma_start(out=bet_b[0:1, :], in_=betain[:, :]))
        m_gp_wz = GPZ.n
        m_gb = GPB.n

        PE.wait(GPB, m_gb)
        PE.wait(DV, m16)
        for row_t in (gam_b, bet_b):
            for s in range(4):
                mmm = PE.emit(lambda pe, r=row_t, s=s: pe.matmul(
                    pstat[:, 0:512], onesb[0:1, :], r[0:1, s * 512:(s + 1) * 512],
                    start=True, stop=True))
                DV.wait(PE, mmm)
                mvv = DV.emit(lambda dv, r=row_t, s=s: dv.tensor_copy(
                    r[:, s * 512:(s + 1) * 512], pstat[:, 0:512]))
                PE.wait(DV, mvv)
        m_gbb = DV.n

        PE.wait(DV, m_pev)
        PE.wait(GPZ, m_gp_wz)
        for s in range(4):
            def mmm(pe, s=s):
                last = None
                for c in range(DC):
                    last = pe.matmul(pstat[0:4, 0:512],
                                     pprev_bf[:, 4 * c:4 * c + 4],
                                     wzT_s[:, c * D + s * 512: c * D + (s + 1) * 512],
                                     start=(c == 0), stop=(c == DC - 1))
                return last
            mp = PE.emit(mmm)
            DV.wait(PE, mp)
            md = DV.emit(lambda dv, s=s: dv.tensor_copy(
                marow[0:4, s * 512:(s + 1) * 512], pstat[0:4, 0:512]))
            PE.wait(DV, md)
        m_M = DV.n

        if kphase < 4:
            _cut()
            return nc
        m_a0_ac = []
        m_prep_end_dv = DV.n
        m_prep_end_ac = AC.n
        m_prep_end_pe = PE.n
        slot_user = [0] * 2
        for j in range(TC):
            b = j % 2
            SY.wait(AC, m_prep_end_ac)
            SY.wait(DV, m_prep_end_dv)
            if j >= 2:
                SY.wait(AC, m_a0_ac[j - 2])
            m_ld = SYG[b].emit(lambda sy, j=j, b=b: sy.dma_start(
                out=G[b][:, :], in_=xown[j * 128:(j + 1) * 128, :]))
            DV.wait(SYG[b], m_ld)
            DV.emit(lambda dv, b=b: dv.reduce_sum(statc[:, 0:1], G[b][:, :], axis=AX.X))
            DV.wait(DV, DV.n)
            DV.emit(lambda dv: dv.tensor_scalar_mul(statc[:, 1:2], statc[:, 0:1], -1.0 / D))
            mdn = DV.n
            AC.wait(DV, mdn)
            msq = AC.emit(lambda ac, b=b: ac.activation(G2h[:, :], G[b][:, 0:1024], AF.Square))
            DV.wait(AC, msq)
            mr0 = DV.emit(lambda dv: dv.reduce_sum(statc[:, 2:3], G2h[:, :], axis=AX.X))
            AC.wait(DV, mr0)
            msq2 = AC.emit(lambda ac, b=b: ac.activation(G2h[:, :], G[b][:, 1024:2048], AF.Square))
            DV.wait(AC, msq2)
            DV.emit(lambda dv: dv.reduce_sum(statc[:, 6:7], G2h[:, :], axis=AX.X))
            DV.wait(DV, DV.n)
            DV.emit(lambda dv: dv.tensor_add(statc[:, 2:3], statc[:, 2:3], statc[:, 6:7]))
            DV.wait(DV, DV.n)
            DV.emit(lambda dv: dv.tensor_scalar_mul(statc[:, 3:4], statc[:, 2:3], 1.0 / D))
            DV.emit(lambda dv: dv.tensor_mul(statc[:, 5:6], statc[:, 1:2], statc[:, 1:2]))
            DV.wait(DV, DV.n)
            mvar = DV.emit(lambda dv: dv.tensor_sub(statc[:, 3:4], statc[:, 3:4], statc[:, 5:6]))
            AC.wait(DV, mvar)
            AC.emit(lambda ac: ac.activation(statc[:, 4:5], statc[:, 3:4], AF.Ln,
                                             bias=consts[:, 0:1]))
            AC.wait(AC, AC.n)
            mrs = AC.emit(lambda ac: ac.activation(statc[:, 4:5], statc[:, 4:5], AF.Exp,
                                                   scale=consts[:, 2:3]))
            if ka0 < 2:
                m_a0_ac.append(AC.n)
                try:
                    m_a0_pe.append(PE.n)
                except NameError:
                    m_a0_pe = [PE.n]
                continue
            DV.wait(AC, mrs)
            if j >= 2:
                DV.wait(PE, m_a0_pe[j - 2])
            else:
                DV.wait(PE, m_prep_end_pe)
            mnorm = DV.emit(lambda dv, b=b: dv.tensor_scalar(
                Hh[b][:, :], G[b][:, :], statc[:, 1:2], statc[:, 4:5], ALU.add, ALU.mult))
            if ka0 < 3:
                m_a0_ac.append(AC.n)
                try:
                    m_a0_pe.append(PE.n)
                except NameError:
                    m_a0_pe = [PE.n]
                continue
            PE.wait(DV, mnorm)
            for c in range(DC):
                slot = c % 2
                PE.wait(AC, slot_user[slot])
                mt = PE.emit(lambda pe, b=b, c=c, slot=slot: pe.transpose(
                    ptA[slot][:, 0:128],
                    Hh[b][:, c * 128:(c + 1) * 128], i128b_s[:, :]))
                AC.wait(PE, mt)
                mcp = AC.emit(lambda ac, c=c, j=j, slot=slot: ac.copy(
                    xlnT_s[:, c * T + j * 128: c * T + (j + 1) * 128],
                    ptA[slot][:, 0:128]))
                slot_user[slot] = mcp
            m_a0_ac.append(AC.n)
            try:
                m_a0_pe.append(PE.n)
            except NameError:
                m_a0_pe = [PE.n]

        m_a0_done_pe = PE.n
        m_a0_end_dv = DV.n

        if kphase < 5:
            _cut()
            return nc
        stAz = contextlib.ExitStack()
        _open_stacks.append(stAz)
        pz = stAz.enter_context(nc.psum_tensor("pz", [128, 1024], F32))
        ph = stAz.enter_context(nc.psum_tensor("ph", [128, 1024], F32))

        m_zev = 0
        m_hev = 0
        m_ccast = 0
        m_clmm = [0, 0]
        m_x2st = []
        for j in range(TC):
            b = j % 2
            for hh in range(2):
                PE.wait(AC, m_a0_ac[j])
                PE.wait(GPZ, m_gp_wz)
                PE.wait(AC, m_zev)
                PE.wait(DV, m_ccast)

                def zmm(pe, j=j, hh=hh):
                    last = None
                    for c in range(DC):
                        for s in range(2):
                            last = pe.matmul(
                                pz[:, s * 512:(s + 1) * 512],
                                xlnT_s[:, c * T + j * 128: c * T + (j + 1) * 128],
                                wzT_s[:, c * D + hh * 1024 + s * 512: c * D + hh * 1024 + (s + 1) * 512],
                                start=(c == 0), stop=(c == DC - 1))
                    return last
                mz = PE.emit(zmm)
                AC.wait(PE, mz)
                AC.wait(PE, m_clmm[hh])
                m_zev = AC.emit(lambda ac, hh=hh: ac.copy(
                    zbf[:, hh * 1024:(hh + 1) * 1024], pz[:, :]))
                PE.wait(AC, m_zev)
                PE.wait(DV, max(m_hev, m_M, m_ccast))

                def hmm(pe, j=j, hh=hh):
                    last = None
                    for s in range(2):
                        sl = slice(hh * 1024 + s * 512, hh * 1024 + (s + 1) * 512)
                        psl = slice(s * 512, (s + 1) * 512)
                        pe.matmul(ph[:, psl], bmat[:, :], zbf[:, sl], start=True, stop=False)
                        pe.matmul(ph[:, psl], aloc[:, :], carry_bf[0:4, sl], start=False, stop=False)
                        last = pe.matmul(ph[:, psl], am[:, j * 128:(j + 1) * 128],
                                         marow[0:4, sl], start=False, stop=True)
                    return last
                mh = PE.emit(hmm)
                DV.wait(PE, mh)
                if j >= 1:
                    DV.wait(SYX2, m_x2st[j - 1])
                m_hev = DV.emit(lambda dv, hh=hh: dv.tensor_copy(
                    hb[:, hh * 1024:(hh + 1) * 1024], ph[:, :]))
                PE.wait(AC, m_zev)
                PE.wait(DV, m_ccast)

                def clmm(pe, hh=hh):
                    last = None
                    for s in range(2):
                        sl = slice(hh * 1024 + s * 512, hh * 1024 + (s + 1) * 512)
                        last = pe.matmul(pz[0:4, s * 512:(s + 1) * 512], clcoef[:, :],
                                         zbf[:, sl], start=True, stop=True)
                    return last
                mcl2 = PE.emit(clmm)
                m_clmm[hh] = mcl2
                DV.wait(PE, mcl2)
                DV.emit(lambda dv, hh=hh: dv.scalar_tensor_tensor(
                    carry[0:4, hh * 1024:(hh + 1) * 1024],
                    carry[0:4, hh * 1024:(hh + 1) * 1024],
                    scol[0:4, 3:4], pz[0:4, 0:1024], ALU.mult, ALU.add))
                DV.wait(DV, DV.n)
                m_ccast = DV.emit(lambda dv, hh=hh: dv.tensor_copy(
                    carry_bf[0:4, hh * 1024:(hh + 1) * 1024],
                    carry[0:4, hh * 1024:(hh + 1) * 1024]))
            sp = 8 * (j % 2)
            AC.wait(DV, m_hev)
            if j >= 2:
                AC.wait(DV, m_x2dv[j - 2])
            mvar, mrs = emit_stats(lambda: hb[:, :], lambda: Hh[0][:, :], sp)
            if j >= 2:
                SY.wait(DV, m_x2dv[j - 2])
            else:
                SY.wait(DV, m_a0_end_dv)
            SY.wait(DV, m_a0_end_dv)
            mxl = SYG[b].emit(lambda sy, j=j, b=b: sy.dma_start(
                out=G[b][:, :], in_=xown[j * 128:(j + 1) * 128, :]))
            DV.wait(AC, mrs)
            DV.emit(lambda dv, sp=sp: dv.tensor_scalar(
                hb[:, :], hb[:, :], statc[:, sp + 1:sp + 2], statc[:, sp + 4:sp + 5],
                ALU.add, ALU.mult))
            DV.wait(DV, DV.n)
            DV.emit(lambda dv: dv.tensor_mul(hb[:, :], hb[:, :], gam_b[:, :]))
            DV.wait(DV, DV.n)
            DV.emit(lambda dv: dv.tensor_add(hb[:, :], hb[:, :], bet_b[:, :]))
            DV.wait(DV, DV.n)
            DV.wait(SYG[b], mxl)
            mx2 = DV.emit(lambda dv, b=b: dv.tensor_add(hb[:, :], hb[:, :], G[b][:, :]))
            try:
                m_x2dv.append(mx2)
            except NameError:
                m_x2dv = [mx2]
            SY.wait(DV, mx2)
            mst = SYX2.emit(lambda sy, j=j: sy.dma_start(
                out=x2_dram[j * 128:(j + 1) * 128, :], in_=hb[:, :]))
            m_x2st.append(mst)

        if kphase < 6:
            _cut()
            return nc
        m_a1z_end = dict(pe=PE.n, dv=DV.n, ac=AC.n, sy=SY.n)
        _close_stack(stAz)
        _close_stack(stP)
        _close_stack(stA)

        stB1 = contextlib.ExitStack()
        _open_stacks.append(stB1)
        uT_s = stB1.enter_context(nc.sbuf_tensor("uT_s", [128, DC * T], BF16))
        su_b = stB1.enter_context(nc.sbuf_tensor("su_b", [128, D], F32))
        wslab = [stB1.enter_context(nc.sbuf_tensor(f"wslab{i}", [128, 16 * 256], BF16)) for i in range(2)]
        gslab = [stB1.enter_context(nc.sbuf_tensor(f"gslab{i}", [128, 16 * 256], BF16)) for i in range(2)]
        btmp = stB1.enter_context(nc.sbuf_tensor("btmp", [128, 1024], BF16))
        silb2 = stB1.enter_context(nc.sbuf_tensor("silb2", [128, 1024], F32))
        vtmp2 = stB1.enter_context(nc.sbuf_tensor("vtmp2", [128, 1024], F32))
        vchB = [stB1.enter_context(nc.sbuf_tensor(f"vchB{i}", [128, T], BF16)) for i in range(2)]

        stB0 = contextlib.ExitStack()
        _open_stacks.append(stB0)
        ptB = [stB0.enter_context(nc.psum_tensor(f"ptB{i}", [128, 1024], BF16))
               for i in range(2)]
        pH0 = stB0.enter_context(nc.psum_tensor("pH0", [128, 512], F32))
        pH1 = stB0.enter_context(nc.psum_tensor("pH1", [128, 512], F32))

        SYR.emit(lambda sy: sy.dma_start(out=su_b[0:1, :], in_=suugin[:, :]))
        m_sul = SYR.n
        PE.wait(SYR, m_sul)
        su_scale = 1.0 / (SCALE * float(np.sqrt(D)))
        for s in range(4):
            mmm = PE.emit(lambda pe, s=s: pe.matmul(
                pH0[:, 0:512], onesr[0:1, :], su_b[0:1, s * 512:(s + 1) * 512],
                start=True, stop=True))
            DV.wait(PE, mmm)
            mvv = DV.emit(lambda dv, s=s: dv.tensor_scalar_mul(
                su_b[:, s * 512:(s + 1) * 512], pH0[:, 0:512], su_scale))
            PE.wait(DV, mvv)
        m_sub = DV.n

        m_b0_ac = []
        slotB = [0] * 2
        m_b0_pe_chunks = []
        for j in range(TC):
            b = j % 2
            SY.wait(SYX2, m_x2st[j])
            if j >= 2:
                SY.wait(DV, m_b0_norm[j - 2])
            else:
                SY.wait(DV, m_a1z_end["dv"])
            mld = SYG[b].emit(lambda sy, j=j, b=b: sy.dma_start(
                out=G[b][:, :], in_=x2_dram[j * 128:(j + 1) * 128, :]))
            AC.wait(SYG[b], mld)
            if j < 2:
                AC.wait(DV, m_a1z_end["dv"])
            sp = 8 * (j % 2)
            if j >= 2:
                AC.wait(DV, m_b0_norm[j - 2])
            mvar, mrs = emit_stats(lambda b=b: G[b][:, :], lambda: zbf[:, :], sp)
            DV.wait(AC, mrs)
            DV.wait(DV, m_sub)
            if j >= 2:
                DV.wait(PE, m_b0_pe_chunks[j - 2])
            else:
                DV.wait(PE, m_a1z_end["pe"])
            for hx in range(2):
                DV.emit(lambda dv, b=b, hx=hx, sp=sp: dv.tensor_scalar(
                    G2h[:, :], G[b][:, hx * 1024:(hx + 1) * 1024],
                    statc[:, sp + 1:sp + 2], statc[:, sp + 4:sp + 5], ALU.add, ALU.mult))
                DV.wait(DV, DV.n)
                DV.emit(lambda dv, b=b, hx=hx: dv.tensor_mul(
                    Hh[b][:, hx * 1024:(hx + 1) * 1024], G2h[:, :],
                    su_b[:, hx * 1024:(hx + 1) * 1024]))
                DV.wait(DV, DV.n)
            mu_ = DV.n
            prev = mu_
            for st_i, dist in enumerate((1, 2, 4, 8)):
                blkw = dist * 128
                ngrp = DC // (2 * dist)

                def bfly(dv, b=b, dist=dist, blkw=blkw, ngrp=ngrp, which=0):
                    vv = Hh[b][:, :].rearrange("p (g two q) -> p g two q", two=2, q=blkw)
                    a = vv[:, :, 0, :]
                    bb = vv[:, :, 1, :]
                    t = btmp[:, 0:ngrp * blkw].rearrange("p (g q) -> p g q", q=blkw)
                    if which == 0:
                        return dv.tensor_add(t, a, bb)
                    elif which == 1:
                        return dv.tensor_sub(bb, a, bb)
                    else:
                        return dv.tensor_copy(a, t)
                DV.wait(DV, prev)
                DV.emit(lambda dv, f=bfly: f(dv, which=0))
                DV.wait(DV, DV.n)
                DV.emit(lambda dv, f=bfly: f(dv, which=1))
                DV.wait(DV, DV.n)
                prev = DV.emit(lambda dv, f=bfly: f(dv, which=2))
            m_b0n = prev
            try:
                m_b0_norm.append(m_b0n)
            except NameError:
                m_b0_norm = [m_b0n]
            PE.wait(DV, m_b0n)
            for c in range(DC):
                slot = c % 2
                PE.wait(AC, slotB[slot])
                mt = PE.emit(lambda pe, b=b, c=c, slot=slot: pe.transpose(
                    ptB[slot][:, 0:128],
                    Hh[b][:, c * 128:(c + 1) * 128], i128b_s[:, :]))
                AC.wait(PE, mt)
                mcp = AC.emit(lambda ac, c=c, j=j, slot=slot: ac.copy(
                    uT_s[:, c * T + j * 128: c * T + (j + 1) * 128],
                    ptB[slot][:, 0:128]))
                slotB[slot] = mcp
            m_b0_ac.append(AC.n)
            m_b0_pe_chunks.append(PE.n)
        m_h128 = []
        PE.wait(AC, m_b0_ac[-1])
        prev_ev = [0, 0]
        for c in range(DC):
            for s in range(4):
                par = (c * 4 + s) % 2
                ptgt = pH0 if par == 0 else pH1
                PE.wait(AC, prev_ev[par])
                mm_ = PE.emit(lambda pe, c=c, s=s, ptgt=ptgt: pe.matmul(
                    ptgt[:, 0:512], h128_s[:, :],
                    uT_s[:, c * T + s * 512: c * T + (s + 1) * 512],
                    start=True, stop=True))
                AC.wait(PE, mm_)
                mev = AC.emit(lambda ac, c=c, s=s, ptgt=ptgt: ac.copy(
                    uT_s[:, c * T + s * 512: c * T + (s + 1) * 512], ptgt[:, 0:512]))
                prev_ev[par] = mev
        m_uT = AC.n

        if kphase < 7:
            _cut()
            return nc
        _close_stack(stB0)
        stB1p = contextlib.ExitStack()
        _open_stacks.append(stB1p)
        pu = [stB1p.enter_context(nc.psum_tensor(f"pu{i}", [128, 1024], F32)) for i in range(2)]
        pg = [stB1p.enter_context(nc.psum_tensor(f"pg{i}", [128, 1024], F32)) for i in range(2)]

        def emit_slab(w, wbuf, gbuf):
            sq = SYW[w % 2]
            for c in range(DC):
                sq.emit(lambda sy, w=w, c=c, wbuf=wbuf: sy.dma_start(
                    out=wbuf[:, c * 256:(c + 1) * 256],
                    in_=wup_bf[c * 128:(c + 1) * 128, w * 256:(w + 1) * 256]))
            for c in range(DC):
                sq.emit(lambda sy, w=w, c=c, gbuf=gbuf: sy.dma_start(
                    out=gbuf[:, c * 256:(c + 1) * 256],
                    in_=wgate_bf[c * 128:(c + 1) * 128, w * 256:(w + 1) * 256]))
            return sq.n

        m_slab = {}
        m_win_pe = {}
        SY.wait(GPW, m_gp_upgate)
        m_slab[0] = emit_slab(0, wslab[0], gslab[0])
        m_vst = []
        m_vdv = {}
        m_stt_prev = 0
        for jc in range(JC):
            w = jc // 2
            if jc % 2 == 0 and w + 1 < JC // 2:
                if w >= 1:
                    SY.wait(PE, m_win_pe[w - 1])
                m_slab[w + 1] = emit_slab(w + 1, wslab[(w + 1) % 2], gslab[(w + 1) % 2])
            for hh in range(2):
                par = (jc * 2 + hh) % 2
                PE.wait(SYW[w % 2], m_slab[w])
                PE.wait(AC, m_uT)
                if (jc * 2 + hh) >= 2:
                    PE.wait(DV, m_vdv[(jc * 2 + hh) - 2])

                def upmm(pe, jc=jc, hh=hh, w=w, par=par):
                    jj = jc % 2
                    last = None
                    for c in range(DC):
                        for s in range(2):
                            last = pe.matmul(
                                pu[par][:, s * 512:(s + 1) * 512],
                                wslab[w % 2][:, c * 256 + jj * 128: c * 256 + (jj + 1) * 128],
                                uT_s[:, c * T + hh * 1024 + s * 512: c * T + hh * 1024 + (s + 1) * 512],
                                start=(c == 0), stop=(c == DC - 1))
                    for c in range(DC):
                        for s in range(2):
                            last = pe.matmul(
                                pg[par][:, s * 512:(s + 1) * 512],
                                gslab[w % 2][:, c * 256 + jj * 128: c * 256 + (jj + 1) * 128],
                                uT_s[:, c * T + hh * 1024 + s * 512: c * T + hh * 1024 + (s + 1) * 512],
                                start=(c == 0), stop=(c == DC - 1))
                    return last
                mup = PE.emit(upmm)
                AC.wait(PE, mup)
                AC.wait(DV, m_stt_prev)
                msil = AC.emit(lambda ac, jc=jc, par=par: ac.activation(
                    silb2[:, :], pg[par][:, :], AF.Silu, scale=wsgT[:, jc:jc + 1]))
                DV.wait(AC, msil)
                if jc >= 2:
                    DV.wait(SYV[jc % 2], m_vst[jc - 2])
                DV.wait(DV, DV.n)
                m_stt_prev = DV.emit(lambda dv, jc=jc, par=par: dv.scalar_tensor_tensor(
                    vtmp2[:, :], pu[par][:, :], wsupT[:, jc:jc + 1], silb2[:, :],
                    ALU.mult, ALU.mult))
                DV.wait(DV, DV.n)
                mvd = DV.emit(lambda dv, jc=jc, hh=hh: dv.tensor_scalar(
                    vchB[jc % 2][:, hh * 1024:(hh + 1) * 1024], vtmp2[:, :],
                    sudpT[:, jc:jc + 1], None, ALU.mult))
                m_vdv[jc * 2 + hh] = mvd
            SY.wait(DV, m_vdv[jc * 2 + 1])
            mst = SYV[jc % 2].emit(lambda sy, jc=jc: sy.dma_start(
                out=v_dram[jc * 128:(jc + 1) * 128, :], in_=vchB[jc % 2][:, :]))
            m_vst.append(mst)
            if jc % 2 == 1:
                m_win_pe[jc // 2] = PE.n
        m_b1_end = dict(pe=PE.n, dv=DV.n, ac=AC.n, sy=SY.n)

        if kphase < 8:
            _cut()
            return nc
        _close_stack(stB1p)
        _close_stack(stB1)
        stB2 = contextlib.ExitStack()
        _open_stacks.append(stB2)
        vq2 = [stB2.enter_context(nc.sbuf_tensor(f"vq{i}", [128, JC * 256], BF16))
               for i in range(2)]
        btmp2 = stB2.enter_context(nc.sbuf_tensor("btmp2", [128, 32 * 256], BF16))
        pq = [stB2.enter_context(nc.psum_tensor(f"pq{i}", [128, 512], F32)) for i in range(2)]

        m_v2st = {}
        m_bf_end = {}
        prev_evq = [0, 0]
        for e in range(8):
            p = e % 2
            if e == 0:
                SY.wait(SYV[0], SYV[0].n)
                SY.wait(SYV[1], SYV[1].n)
            if e >= 2:
                SY.wait(SYV2[p], m_v2st[e - 2])
            for c in range(JC):
                SYVQ[p].emit(lambda sy, e=e, c=c, p=p: sy.dma_start(
                    out=vq2[p][:, c * 256:(c + 1) * 256],
                    in_=v_dram[c * 128:(c + 1) * 128, e * 256:(e + 1) * 256]))
            mldq = SYVQ[p].n
            for c in range(JC):
                par = c % 2
                PE.wait(SYVQ[p], mldq)
                PE.wait(AC, prev_evq[par])
                if e >= 2:
                    PE.wait(DV, m_bf_end[e - 2])
                mmq = PE.emit(lambda pe, c=c, par=par, p=p: pe.matmul(
                    pq[par][:, 0:256], h128_s[:, :], vq2[p][:, c * 256:(c + 1) * 256],
                    start=True, stop=True))
                AC.wait(PE, mmq)
                mevq = AC.emit(lambda ac, c=c, par=par, p=p: ac.copy(
                    vq2[p][:, c * 256:(c + 1) * 256], pq[par][:, 0:256]))
                prev_evq[par] = mevq
            m_h128q = AC.n
            DV.wait(AC, m_h128q)
            prev = DV.n
            for dist in (1, 2, 4, 8, 16, 32):
                blkw = dist * 256
                ngrp = JC // (2 * dist)

                def bfq(dv, dist=dist, blkw=blkw, ngrp=ngrp, p=p, which=0):
                    vv = vq2[p][:, :].rearrange("q (g two q2) -> q g two q2", two=2, q2=blkw)
                    a = vv[:, :, 0, :]
                    bb = vv[:, :, 1, :]
                    t = btmp2[:, 0:ngrp * blkw].rearrange("q (g q2) -> q g q2", q2=blkw)
                    if which == 0:
                        return dv.tensor_add(t, a, bb)
                    elif which == 1:
                        return dv.tensor_sub(bb, a, bb)
                    else:
                        return dv.tensor_copy(a, t)
                DV.wait(DV, prev)
                DV.emit(lambda dv, f=bfq: f(dv, which=0))
                DV.wait(DV, DV.n)
                DV.emit(lambda dv, f=bfq: f(dv, which=1))
                DV.wait(DV, DV.n)
                prev = DV.emit(lambda dv, f=bfq: f(dv, which=2))
            m_bf_end[e] = prev
            SY.wait(DV, prev)
            for c in range(JC):
                SYV2[p].emit(lambda sy, e=e, c=c, p=p: sy.dma_start(
                    out=v2_dram[c * 128:(c + 1) * 128, e * 256:(e + 1) * 256],
                    in_=vq2[p][:, c * 256:(c + 1) * 256]))
            m_v2st[e] = SYV2[p].n
        m_b2_end_sy = (SYV2[0].n, SYV2[1].n)
        m_b2_end_pe = PE.n
        m_b2_end_dv = DV.n

        _close_stack(stB2)
        stB3 = contextlib.ExitStack()
        _open_stacks.append(stB3)
        acc = stB3.enter_context(nc.sbuf_tensor("acc", [128, TC * 512], F32))
        v2sl = [stB3.enter_context(nc.sbuf_tensor(f"v2sl{i}", [128, 8 * T], BF16))
                for i in range(2)]
        wdsl = [stB3.enter_context(nc.sbuf_tensor(f"wdsl{i}", [128, 8 * 512], BF16))
                for i in range(2)]
        ep = [stB3.enter_context(nc.sbuf_tensor(f"ep{i}", [128, 512], F32)) for i in range(2)]
        epw = stB3.enter_context(nc.sbuf_tensor("epw", [128, 512], F32))
        wsdp_b = stB3.enter_context(nc.sbuf_tensor("wsdp_b", [128, D], F32))
        pd = [stB3.enter_context(nc.psum_tensor(f"pd{i}", [128, 512], F32)) for i in range(4)]

        SYR.emit(lambda sy: sy.dma_start(out=wsdp_b[0:1, :], in_=wsdpin[:, :]))
        m_wsl = SYR.n
        PE.wait(SYR, m_wsl)
        PE.wait(SYV2[0], m_b2_end_sy[0])
        PE.wait(SYV2[1], m_b2_end_sy[1])
        for s in range(4):
            mmm = PE.emit(lambda pe, s=s: pe.matmul(
                pd[0][:, :], onesr[0:1, :], wsdp_b[0:1, s * 512:(s + 1) * 512],
                start=True, stop=True))
            DV.wait(PE, mmm)
            mvv = DV.emit(lambda dv, s=s: dv.tensor_scalar_mul(
                wsdp_b[:, s * 512:(s + 1) * 512], pd[0][:, :], SCALE))
            PE.wait(DV, mvv)
        m_wsb = DV.n

        m_yst = []
        m_padd = {}
        m_sg_pe = {}
        DV.wait(SYV2[0], m_b2_end_sy[0])
        DV.wait(SYV2[1], m_b2_end_sy[1])
        first = True
        for ihq in range(4):
            m_accz = DV.emit(lambda dv: dv.memset(acc[:, :], 0.0))
            for sg in range(8):
                sp_ = (ihq * 8 + sg) % 2
                if first:
                    SY.wait(SYV2[0], m_b2_end_sy[0])
                    SY.wait(SYV2[1], m_b2_end_sy[1])
                    GP.wait(PE, m_b2_end_pe)
                    GP.wait(DV, m_b2_end_dv)
                    GP.wait(SYV2[0], m_b2_end_sy[0])
                    GP.wait(SYV2[1], m_b2_end_sy[1])
                    first = False
                else:
                    SY.wait(PE, m_sg_pe.get(sp_, 0))
                    GP.wait(PE, m_sg_pe.get(sp_, 0))
                for jj in range(8):
                    jcg = sg * 8 + jj
                    SYSL2[sp_].emit(lambda sy, jcg=jcg, jj=jj, sp_=sp_: sy.dma_start(
                        out=v2sl[sp_][:, jj * T:(jj + 1) * T],
                        in_=v2_dram[jcg * 128:(jcg + 1) * 128, :]))
                mv2 = SYSL2[sp_].n
                for jj in range(8):
                    jcg = sg * 8 + jj
                    GPD2[sp_].emit(lambda gp, jcg=jcg, jj=jj, ihq=ihq, sp_=sp_: gp.dma_start(
                        out=wdsl[sp_][:, jj * 512:(jj + 1) * 512],
                        in_=wdnT[jcg * 128:(jcg + 1) * 128, ihq * 512:(ihq + 1) * 512]))
                mwd = GPD2[sp_].n
                for tch in range(TC):
                    par = tch % 4
                    PE.wait(SYSL2[sp_], mv2)
                    PE.wait(GPD2[sp_], mwd)
                    PE.wait(DV, m_padd.get(par, 0))
                    PE.wait(DV, m_accz)

                    def dmm(pe, tch=tch, par=par, sp_=sp_):
                        last = None
                        for jj in range(8):
                            last = pe.matmul(
                                pd[par][:, :],
                                v2sl[sp_][:, jj * T + tch * 128: jj * T + (tch + 1) * 128],
                                wdsl[sp_][:, jj * 512:(jj + 1) * 512],
                                start=(jj == 0), stop=(jj == 7))
                        return last
                    mdm = PE.emit(dmm)
                    DV.wait(PE, mdm)
                    madd = DV.emit(lambda dv, tch=tch, par=par: dv.tensor_add(
                        acc[:, tch * 512:(tch + 1) * 512],
                        acc[:, tch * 512:(tch + 1) * 512], pd[par][:, :]))
                    m_padd[par] = madd
                m_sg_pe[sp_] = PE.n
            for tch in range(TC):
                b = tch % 2
                if SYY[b].n > 0:
                    SY.wait(SYY[b], SYY[b].n)
                if len(m_yst) >= 2:
                    SY.wait(DV, m_ep_dv[-2])
                mxl = SYE[b].emit(lambda sy, tch=tch, ihq=ihq, b=b: sy.dma_start(
                    out=ep[b][:, :],
                    in_=x2_dram[tch * 128:(tch + 1) * 128, ihq * 512:(ihq + 1) * 512]))
                DV.wait(SYE[b], mxl)
                DV.wait(DV, m_padd.get(tch % 4, 0))
                DV.wait(DV, DV.n)
                DV.emit(lambda dv, tch=tch, ihq=ihq: dv.tensor_mul(
                    epw[:, :], acc[:, tch * 512:(tch + 1) * 512],
                    wsdp_b[:, ihq * 512:(ihq + 1) * 512]))
                DV.wait(DV, DV.n)
                mepd = DV.emit(lambda dv, b=b: dv.tensor_add(
                    ep[b][:, :], ep[b][:, :], epw[:, :]))
                try:
                    m_ep_dv.append(mepd)
                except NameError:
                    m_ep_dv = [mepd]
                SY.wait(DV, mepd)
                myst = SYY[b].emit(lambda sy, tch=tch, ihq=ihq, b=b: sy.dma_start(
                    out=yout[tch * 128:(tch + 1) * 128, ihq * 512:(ihq + 1) * 512],
                    in_=ep[b][:, :]))
                m_yst.append(myst)
        _close_stack(stB3)

        SY.wait(SYY[0], SYY[0].n)
        SY.wait(SYY[1], SYY[1].n)
        SY.wait(SY, SY.n)
        GP.wait(GP, GP.n)

        _flush_all(blk, PE, DV, AC, SY, GP)

    return nc


def _hadamard(n):
    h = np.array([[1.0]], dtype=np.float64)
    while h.shape[0] < n:
        h = np.block([[h, h], [h, -h]])
    return h


_NC_CACHE = [None]


def kernel(**inputs):
    import ml_dtypes
    x = np.asarray(inputs["L0_attn_x"])
    W = np.asarray(inputs["L0_attn_W"])
    lam_logit = np.asarray(inputs["L0_attn_lam_logit"]).reshape(1, 4)
    eta = np.asarray(inputs["L0_attn_eta"]).reshape(1, 4)
    gamma = np.asarray(inputs["L0_attn_gamma"]).reshape(1, D)
    beta = np.asarray(inputs["L0_attn_beta"]).reshape(1, D)
    su_ug = np.asarray(inputs["L0_ffn_SU_ug"]).reshape(1, D)
    su_dp = np.asarray(inputs["L0_ffn_SU_dp"]).reshape(64, 128)
    ws_ug = np.asarray(inputs["L0_ffn_Wscale_ug"])
    ws_dp = np.asarray(inputs["L0_ffn_Wscale_dp"]).reshape(1, D)
    W_up = np.asarray(inputs["L0_ffn_W_up"])
    W_gate = np.asarray(inputs["L0_ffn_W_gate"])
    W_down = np.asarray(inputs["L0_ffn_W_down"])

    xf = np.ascontiguousarray(x.reshape(NCORE * T, D))
    xprev = np.ascontiguousarray(xf[:PREV * 128, :])
    wzT = np.ascontiguousarray(W.T)
    wupT = np.ascontiguousarray(W_up.T)
    wgateT = np.ascontiguousarray(W_gate.T)
    wdnT = np.ascontiguousarray(W_down.T)

    h128 = (_hadamard(128)).astype(ml_dtypes.bfloat16)
    i128b = np.eye(128, dtype=np.float32).astype(ml_dtypes.bfloat16)
    i128f = np.eye(128, dtype=np.float32)
    ti = np.arange(128, dtype=np.float32)
    tms = (ti[None, :] - ti[:, None]).astype(np.float32)
    mge = (ti[None, :] >= ti[:, None]).astype(np.float32)
    riota = np.repeat((127.0 - ti)[:, None], 4, axis=1).astype(np.float32)
    grid4 = np.repeat(ti[None, :], 4, axis=0).astype(np.float32)

    common = {
        "wzT": wzT, "wupT": wupT, "wgateT": wgateT, "wdnT": wdnT,
        "lamlog": lam_logit.astype(np.float32), "etain": eta.astype(np.float32),
        "gammain": gamma.astype(np.float32), "betain": beta.astype(np.float32),
        "suugin": su_ug.astype(np.float32), "sudpin": su_dp.astype(np.float32),
        "wsupin": np.ascontiguousarray(ws_ug[:J].reshape(64, 128)).astype(np.float32),
        "wsgin": np.ascontiguousarray(ws_ug[J:].reshape(64, 128)).astype(np.float32),
        "wsdpin": ws_dp.astype(np.float32),
        "h128in": h128, "i128fin": i128f, "i128bin": i128b,
        "tmsin": tms, "mgein": mge, "riotain": riota, "grid4in": grid4,
        "xfull": xprev,
    }
    in_maps = []
    for c in range(NCORE):
        tau = np.arange(128, dtype=np.float64)[:, None]
        jj = np.arange(PREV, dtype=np.float64)[None, :]
        expo = (2048.0 * c - 1.0) - 128.0 * jj - tau
        live = (jj < 16 * c) & np.ones((128, 1), bool)
        expo = np.where(live, expo, 1.0e6)
        pmask = live.astype(np.float32)
        m = dict(common)
        m["xown"] = np.ascontiguousarray(xf[c * T:(c + 1) * T, :])
        m["expoin"] = expo.astype(np.float32)
        m["pmaskin"] = np.ascontiguousarray(pmask)
        in_maps.append(m)

    if _NC_CACHE[0] is None:
        _NC_CACHE[0] = build_nc()
    nc = _NC_CACHE[0]
    res = run_bass_kernel_spmd(nc, in_maps, core_ids=list(range(NCORE)))
    out = np.concatenate([np.asarray(r["yout"]) for r in res.results], axis=0)
    return out.reshape(1, NCORE * T, D).astype(np.float32)


# version 05
# speedup vs baseline: 1.1698x; 1.1241x over previous
import numpy as np
import contextlib
import concourse.bass as bass
import concourse.mybir as mybir
from concourse.bass_utils import run_bass_kernel_spmd

F32 = mybir.dt.float32
BF16 = mybir.dt.bfloat16
AF = mybir.ActivationFunctionType
ALU = mybir.AluOpType
AX = mybir.AxisListType

NCORE = 8
T = 2048
D = 2048
J = 8192
TC = T // 128
DC = D // 128
JC = J // 128
PREV = (NCORE - 1) * TC
EPS = 1e-5
SCALE = 64.0


class Q:

    def __init__(self, name, dma=False):
        self.name = name
        self.dma = dma
        self.sem = None
        self.ops = []
        self.n = 0

    def emit(self, f, inc=None):
        if inc is None:
            inc = 16 if self.dma else 1
        sem = lambda: self.sem

        def op(eng, f=f, inc=inc):
            ins = f(eng)
            ins.then_inc(self.sem, inc)
        self.ops.append(op)
        self.n += inc
        return self.n

    def raw(self, f):
        self.ops.append(lambda eng, f=f: f(eng))

    def wait(self, other, val):
        if val > 0:
            self.ops.append(lambda eng, o=other, v=val: eng.wait_ge(o.sem, v))


class _CutDone(Exception):
    def __init__(self, env):
        self.env = env


def _flush_all(blk, PE, DV, AC, SY, GP):
    def flush(q, eng):
        q.eng = eng
        for op in q.ops:
            op(eng)
    blk.tensor(lambda eng: flush(PE, eng))
    blk.vector(lambda eng: flush(DV, eng))
    blk.scalar(lambda eng: flush(AC, eng))
    blk.sync(lambda eng: flush(SY, eng))
    blk.gpsimd(lambda eng: flush(GP, eng))


class SubQ:

    def __init__(self, parent):
        self.parent = parent
        self.sem = None
        self.n = 0

    def emit(self, f):
        def op(eng, f=f):
            f(eng).then_inc(self.sem, 16)
        self.parent.ops.append(op)
        self.n += 16
        return self.n


def build_nc():
    import os
    kphase = int(os.environ.get("KPHASE", "99"))
    ka0 = int(os.environ.get("KA0", "99"))
    nc = bass.Bass("TRN2", target_bir_lowering=False)

    xfull = nc.dram_tensor("xfull", [PREV * 128, D], F32, kind="ExternalInput")
    xown = nc.dram_tensor("xown", [T, D], F32, kind="ExternalInput")
    wzT = nc.dram_tensor("wzT", [D, D], F32, kind="ExternalInput")
    wupT = nc.dram_tensor("wupT", [D, J], F32, kind="ExternalInput")
    wgateT = nc.dram_tensor("wgateT", [D, J], F32, kind="ExternalInput")
    wdnT = nc.dram_tensor("wdnT", [J, D], F32, kind="ExternalInput")
    lamlog = nc.dram_tensor("lamlog", [1, 4], F32, kind="ExternalInput")
    etain = nc.dram_tensor("etain", [1, 4], F32, kind="ExternalInput")
    gammain = nc.dram_tensor("gammain", [1, D], F32, kind="ExternalInput")
    betain = nc.dram_tensor("betain", [1, D], F32, kind="ExternalInput")
    suugin = nc.dram_tensor("suugin", [1, D], F32, kind="ExternalInput")
    sudpin = nc.dram_tensor("sudpin", [64, 128], F32, kind="ExternalInput")
    wsupin = nc.dram_tensor("wsupin", [64, 128], F32, kind="ExternalInput")
    wsgin = nc.dram_tensor("wsgin", [64, 128], F32, kind="ExternalInput")
    wsdpin = nc.dram_tensor("wsdpin", [1, D], F32, kind="ExternalInput")
    h128in = nc.dram_tensor("h128in", [128, 128], BF16, kind="ExternalInput")
    i128fin = nc.dram_tensor("i128fin", [128, 128], F32, kind="ExternalInput")
    i128bin = nc.dram_tensor("i128bin", [128, 128], BF16, kind="ExternalInput")
    tmsin = nc.dram_tensor("tmsin", [128, 128], F32, kind="ExternalInput")
    mgein = nc.dram_tensor("mgein", [128, 128], F32, kind="ExternalInput")
    riotain = nc.dram_tensor("riotain", [128, 4], F32, kind="ExternalInput")
    grid4in = nc.dram_tensor("grid4in", [4, 128], F32, kind="ExternalInput")
    expoin = nc.dram_tensor("expoin", [128, PREV], F32, kind="ExternalInput")
    pmaskin = nc.dram_tensor("pmaskin", [128, PREV], F32, kind="ExternalInput")

    yout = nc.dram_tensor("yout", [T, D], F32, kind="ExternalOutput")
    x2_dram = nc.dram_tensor("x2_dram", [T, D], F32, kind="Internal")
    wup_bf = nc.dram_tensor("wup_bf", [D, J], BF16, kind="Internal")
    wgate_bf = nc.dram_tensor("wgate_bf", [D, J], BF16, kind="Internal")
    v_dram = nc.dram_tensor("v_dram", [J, T], BF16, kind="Internal")
    v2_dram = nc.dram_tensor("v2_dram", [J, T], BF16, kind="Internal")

    PE, DV, AC = Q("pe"), Q("dv"), Q("ac")
    SY, GP = Q("sy", dma=True), Q("gp", dma=True)
    QS = (PE, DV, AC, SY, GP)

    SYG = [SubQ(SY), SubQ(SY)]
    SYX2 = SubQ(SY)
    SYW = [SubQ(SY), SubQ(SY)]
    SYV = [SubQ(SY), SubQ(SY)]
    SYVQ = [SubQ(SY), SubQ(SY)]
    SYV2 = [SubQ(SY), SubQ(SY)]
    SYSL2 = [SubQ(SY), SubQ(SY)]
    GPD2 = [SubQ(GP), SubQ(GP)]
    SYE = [SubQ(SY), SubQ(SY)]
    SYY = [SubQ(SY), SubQ(SY)]
    SYR = SubQ(SY)
    GPW = SubQ(GP)
    GPZ = SubQ(GP)
    GPB = SubQ(GP)
    SUBS = [SYG[0], SYG[1], SYX2, SYW[0], SYW[1], SYV[0], SYV[1],
            SYVQ[0], SYVQ[1], SYV2[0], SYV2[1],
            SYSL2[0], SYSL2[1], GPD2[0], GPD2[1],
            SYE[0], SYE[1], SYY[0], SYY[1], SYR, GPW, GPZ, GPB]

    with contextlib.ExitStack() as top:
        for q, nm in zip(QS, ("s_pe", "s_dv", "s_ac", "s_sy", "s_gp")):
            q.sem = top.enter_context(nc.semaphore(nm))
        for i, sq in enumerate(SUBS):
            sq.sem = top.enter_context(nc.semaphore(f"s_sub{i}"))

        def sb(name, shape, dt=F32):
            return top.enter_context(nc.sbuf_tensor(name, shape, dt))

        h128_s = sb("h128_s", [128, 128], BF16)
        i128f_s = sb("i128f_s", [128, 128], F32)
        i128b_s = sb("i128b_s", [128, 128], BF16)
        tms_s = sb("tms_s", [128, 128], F32)
        mge_s = sb("mge_s", [128, 128], F32)
        riota_s = sb("riota_s", [128, 4], F32)
        grid4_s = sb("grid4_s", [4, 128], F32)
        consts = sb("consts", [128, 8], F32)
        onesr = sb("onesr", [1, 128], F32)
        srow = sb("srow", [1, 16], F32)
        scol = sb("scol", [4, 4], F32)
        lnlam_b = sb("lnlam_b", [128, 4], F32)
        eta_b = sb("eta_b", [128, 4], F32)
        bmat = sb("bmat", [128, 128], BF16)
        bmf = sb("bmf", [128, 128], F32)
        cscr = sb("cscr", [128, 128], F32)
        clcoef = sb("clcoef", [128, 4], BF16)
        aloc = sb("aloc", [4, 128], BF16)
        am = sb("am", [4, TC * 128], BF16)
        a4t = sb("a4t", [4, 128], F32)
        rowload = sb("rowload", [64, 128], F32)
        sudpT = sb("sudpT", [128, 64], F32)
        wsupT = sb("wsupT", [128, 64], F32)
        wsgT = sb("wsgT", [128, 64], F32)
        marow = sb("marow", [4, D], BF16)
        carry = sb("carry", [4, D], F32)
        carry_bf = sb("carry_bf", [4, D], BF16)
        pprev_bf = sb("pprev_bf", [128, 64], BF16)
        expo_s = sb("expo_s", [128, PREV], F32)
        pmask_s = sb("pmask_s", [128, PREV], F32)
        statc = sb("statc", [128, 16], F32)
        w4f = sb("w4f", [128, 4], F32)
        w4b = [sb(f"w4b{i}", [128, 4], BF16) for i in range(2)]
        G = [sb(f"G{i}", [128, D], F32) for i in range(2)]
        G2h = sb("G2h", [128, 1024], F32)
        onesb = sb("onesb", [1, 128], BF16)

        blk = top.enter_context(nc.Block())
        _open_stacks = []

        def _close_stack(s):
            s.close()
            _open_stacks.remove(s)

        def _cut():
            for s in reversed(_open_stacks):
                s.close()
            _open_stacks.clear()
            _flush_all(blk, PE, DV, AC, SY, GP)


        def emit_stats(src_tile_fn, sink_fn, sp, extra_dv_wait=None):
            if extra_dv_wait is not None:
                for w_ in extra_dv_wait:
                    DV.wait(*w_)
            DV.emit(lambda dv: dv.reduce_sum(statc[:, sp:sp + 1], src_tile_fn(), axis=AX.X))
            DV.wait(DV, DV.n)
            DV.emit(lambda dv: dv.tensor_scalar_mul(statc[:, sp + 1:sp + 2],
                                                    statc[:, sp:sp + 1], -1.0 / D))
            AC.wait(AC, AC.n)
            mss = AC.emit(lambda ac: ac.activation(sink_fn(), src_tile_fn(), AF.Square,
                                                   accum_out=statc[:, sp + 2:sp + 3]))
            DV.wait(AC, mss)
            DV.emit(lambda dv: dv.tensor_scalar_mul(statc[:, sp + 3:sp + 4],
                                                    statc[:, sp + 2:sp + 3], 1.0 / D))
            DV.wait(DV, DV.n)
            DV.emit(lambda dv: dv.tensor_mul(statc[:, sp + 5:sp + 6],
                                             statc[:, sp + 1:sp + 2], statc[:, sp + 1:sp + 2]))
            DV.wait(DV, DV.n)
            mvar = DV.emit(lambda dv: dv.tensor_sub(statc[:, sp + 3:sp + 4],
                                                    statc[:, sp + 3:sp + 4], statc[:, sp + 5:sp + 6]))
            AC.wait(DV, mvar)
            AC.emit(lambda ac: ac.activation(statc[:, sp + 4:sp + 5], statc[:, sp + 3:sp + 4],
                                             AF.Ln, bias=consts[:, 0:1]))
            AC.wait(AC, AC.n)
            mrs = AC.emit(lambda ac: ac.activation(statc[:, sp + 4:sp + 5], statc[:, sp + 4:sp + 5],
                                                   AF.Exp, scale=consts[:, 2:3]))
            return mvar, mrs

        for c in range(DC):
            GPW.emit(lambda gp, c=c: gp.dma_start(
                out=wup_bf[c * 128:(c + 1) * 128, :], in_=wupT[c * 128:(c + 1) * 128, :]))
        for c in range(DC):
            GPW.emit(lambda gp, c=c: gp.dma_start(
                out=wgate_bf[c * 128:(c + 1) * 128, :], in_=wgateT[c * 128:(c + 1) * 128, :]))
        m_gp_upgate = GPW.n

        m_sy0 = [None]
        for nm, (dst, src) in {
            "h128": (h128_s, h128in), "i128f": (i128f_s, i128fin), "i128b": (i128b_s, i128bin),
            "tms": (tms_s, tmsin), "mge": (mge_s, mgein), "riota": (riota_s, riotain),
            "grid4": (grid4_s, grid4in), "expo": (expo_s, expoin), "pmask": (pmask_s, pmaskin),
        }.items():
            SY.emit(lambda sy, d=dst, s=src: sy.dma_start(out=d[:, :], in_=s[:, :]))
        SY.emit(lambda sy: sy.dma_start(out=srow[0:1, 0:4], in_=lamlog[:, :]))
        SY.emit(lambda sy: sy.dma_start(out=srow[0:1, 8:12], in_=etain[:, :]))
        m_syc = SY.n

        stP = contextlib.ExitStack()
        _open_stacks.append(stP)
        pstat = stP.enter_context(nc.psum_tensor("pstat", [128, 512], F32))
        pp2 = stP.enter_context(nc.psum_tensor("pp2", [128, 512], F32))
        ptA = [stP.enter_context(nc.psum_tensor(f"ptA{i}", [128, 1024], BF16))
               for i in range(2)]

        for col, val in ((0, EPS), (1, 1.0), (2, -0.5), (3, -1.0)):
            DV.emit(lambda dv, c=col, v=val: dv.memset(consts[:, c:c + 1], v))
        DV.emit(lambda dv: dv.memset(onesr[:, :], 1.0))
        DV.emit(lambda dv: dv.memset(onesb[:, :], 1.0))
        DV.emit(lambda dv: dv.memset(carry[:, :], 0.0))
        DV.emit(lambda dv: dv.memset(carry_bf[:, :], 0.0))
        m_dvc = DV.n

        AC.wait(SY, m_syc)
        AC.wait(DV, m_dvc)
        m1 = AC.emit(lambda ac: ac.activation(srow[0:1, 0:4], srow[0:1, 0:4], AF.Exp,
                                              scale=consts[0:1, 3:4]))
        DV.wait(AC, m1)
        DV.emit(lambda dv: dv.tensor_scalar_add(srow[0:1, 0:4], srow[0:1, 0:4], 1.0))
        DV.wait(DV, DV.n)
        m2 = DV.emit(lambda dv: dv.reciprocal(srow[0:1, 0:4], srow[0:1, 0:4]))
        AC.wait(DV, m2)
        m3 = AC.emit(lambda ac: ac.activation(srow[0:1, 4:8], srow[0:1, 0:4], AF.Ln))
        DV.wait(AC, m3)
        m4 = DV.emit(lambda dv: dv.tensor_scalar_mul(srow[0:1, 12:16], srow[0:1, 4:8], 128.0))
        AC.wait(DV, m4)
        m5 = AC.emit(lambda ac: ac.activation(srow[0:1, 12:16], srow[0:1, 12:16], AF.Exp))

        PE.wait(AC, m5)
        PE.emit(lambda pe: pe.matmul(pstat[:, 0:4], onesr[0:1, :], srow[0:1, 4:8],
                                     start=True, stop=True))
        PE.emit(lambda pe: pe.matmul(pstat[:, 4:8], onesr[0:1, :], srow[0:1, 8:12],
                                     start=True, stop=True))
        for i, c0 in enumerate((0, 4, 8, 12)):
            PE.emit(lambda pe, i=i, c0=c0: pe.transpose(
                pstat[0:4, 8 + i:9 + i], srow[0:1, c0:c0 + 4], i128f_s[0:1, 0:1]))
        m6 = PE.n
        DV.wait(PE, m6)
        DV.emit(lambda dv: dv.tensor_copy(lnlam_b[:, :], pstat[:, 0:4]))
        DV.emit(lambda dv: dv.tensor_copy(eta_b[:, :], pstat[:, 4:8]))
        m7 = DV.emit(lambda dv: dv.tensor_copy(scol[0:4, 0:4], pstat[0:4, 8:12]))
        m_scalars = m7

        SYR.emit(lambda sy: sy.dma_start(out=rowload[:, :], in_=sudpin[:, :]))
        m8 = SYR.n
        PE.wait(SYR, m8)
        PE.wait(DV, m7)
        m9 = PE.emit(lambda pe: pe.transpose(pstat[:, 0:64], rowload[0:64, :], i128f_s[0:64, 0:64]))
        DV.wait(PE, m9)
        m10 = DV.emit(lambda dv: dv.tensor_scalar_mul(
            sudpT[:, :], pstat[:, 0:64], 1.0 / (SCALE * float(np.sqrt(J)))))
        SY.wait(DV, m10)
        SYR.emit(lambda sy: sy.dma_start(out=rowload[:, :], in_=wsupin[:, :]))
        m11 = SYR.n
        PE.wait(SYR, m11)
        PE.wait(DV, m10)
        m12 = PE.emit(lambda pe: pe.transpose(pstat[:, 64:128], rowload[0:64, :], i128f_s[0:64, 0:64]))
        DV.wait(PE, m12)
        m13 = DV.emit(lambda dv: dv.tensor_scalar_mul(wsupT[:, :], pstat[:, 64:128], SCALE))
        SY.wait(PE, m12)
        SYR.emit(lambda sy: sy.dma_start(out=rowload[:, :], in_=wsgin[:, :]))
        m14 = SYR.n
        PE.wait(SYR, m14)
        PE.wait(DV, m13)
        m15 = PE.emit(lambda pe: pe.transpose(pstat[:, 128:192], rowload[0:64, :], i128f_s[0:64, 0:64]))
        DV.wait(PE, m15)
        m16 = DV.emit(lambda dv: dv.tensor_scalar_mul(wsgT[:, :], pstat[:, 128:192], SCALE))

        DV.wait(DV, m_scalars)
        DV.emit(lambda dv: dv.memset(bmf[:, :], 0.0))
        for k in range(4):
            DV.wait(DV, DV.n)
            mk = DV.emit(lambda dv, k=k: dv.tensor_scalar(
                cscr[:, :], tms_s[:, :], lnlam_b[:, k:k + 1], None, ALU.mult))
            AC.wait(DV, mk)
            ma = AC.emit(lambda ac: ac.activation(cscr[:, :], cscr[:, :], AF.Exp))
            DV.wait(AC, ma)
            DV.emit(lambda dv, k=k: dv.tensor_scalar(
                cscr[:, :], cscr[:, :], eta_b[:, k:k + 1], None, ALU.mult))
            DV.wait(DV, DV.n)
            DV.emit(lambda dv: dv.tensor_add(bmf[:, :], bmf[:, :], cscr[:, :]))
        DV.wait(DV, DV.n)
        DV.emit(lambda dv: dv.tensor_mul(bmf[:, :], bmf[:, :], mge_s[:, :]))
        DV.wait(DV, DV.n)
        DV.emit(lambda dv: dv.tensor_add(bmf[:, :], bmf[:, :], i128f_s[:, :]))
        DV.wait(DV, DV.n)
        m_bmat = DV.emit(lambda dv: dv.tensor_copy(bmat[:, :], bmf[:, :]))

        DV.emit(lambda dv: dv.tensor_mul(w4f[:, :], riota_s[:, :], lnlam_b[:, :]))
        DV.wait(DV, DV.n)
        mcl = DV.n
        AC.wait(DV, mcl)
        ma = AC.emit(lambda ac: ac.activation(w4f[:, :], w4f[:, :], AF.Exp))
        DV.wait(AC, ma)
        m_cl = DV.emit(lambda dv: dv.tensor_copy(clcoef[:, :], w4f[:, :]))

        def coef4(dst_ap_fn, off):
            DV.wait(DV, DV.n)
            DV.emit(lambda dv: dv.tensor_scalar_add(a4t[:, :], grid4_s[:, :], float(off)))
            DV.wait(DV, DV.n)
            mx = DV.emit(lambda dv: dv.tensor_scalar(
                a4t[:, :], a4t[:, :], scol[0:4, 1:2], None, ALU.mult))
            AC.wait(DV, mx)
            my = AC.emit(lambda ac: ac.activation(a4t[:, :], a4t[:, :], AF.Exp))
            DV.wait(AC, my)
            return DV.emit(lambda dv: dv.tensor_scalar(
                dst_ap_fn(), a4t[:, :], scol[0:4, 2:3], None, ALU.mult))

        coef4(lambda: aloc[:, :], 1.0)
        for jj in range(TC):
            coef4(lambda jj=jj: am[:, jj * 128:(jj + 1) * 128], 128.0 * jj + 1.0)
        m_coefs = DV.n

        if kphase < 2:
            _cut()
            return nc
        m_prep_ld = []
        m_prep_dv = []
        m_prep_ac = []
        m_prep_pe = []
        for jp in range(PREV):
            b = jp % 2
            if jp >= 2:
                SY.wait(DV, m_prep_ac[jp - 2])
            SY.wait(DV, m_dvc)
            m_ld = SYG[b].emit(lambda sy, jp=jp, b=b: sy.dma_start(
                out=G[b][:, :], in_=xfull[jp * 128:(jp + 1) * 128, :]))
            m_prep_ld.append(m_ld)
            sp = 8 * (jp % 2)
            AC.wait(SYG[b], m_ld)
            if jp >= 2:
                AC.wait(DV, m_prep_dv[jp - 2])
                AC.wait(DV, m_prep_ac[jp - 2])
            edw = [(SYG[b], m_ld)]
            if jp == 0:
                edw.append((DV, m_coefs))
            mvar, mrs = emit_stats(lambda b=b: G[b][:, :], lambda: zbf[:, :], sp,
                                   extra_dv_wait=edw)
            DV.wait(AC, mrs)
            DV.emit(lambda dv, jp=jp: dv.tensor_scalar(
                w4f[:, :], lnlam_b[:, :], expo_s[:, jp:jp + 1], None, ALU.mult))
            DV.wait(DV, DV.n)
            mw = DV.n
            AC.wait(DV, mw)
            mwe = AC.emit(lambda ac: ac.activation(w4f[:, :], w4f[:, :], AF.Exp))
            DV.wait(AC, mwe)
            if jp >= 2:
                DV.wait(PE, m_prep_pe[jp - 2])
            DV.emit(lambda dv, sp=sp: dv.tensor_scalar(
                w4f[:, :], w4f[:, :], statc[:, sp + 4:sp + 5], None, ALU.mult))
            DV.wait(DV, DV.n)
            mw4 = DV.emit(lambda dv, jp=jp, b=b: dv.tensor_scalar(
                w4b[b][:, :], w4f[:, :], pmask_s[:, jp:jp + 1], None, ALU.mult))
            m_prep_dv.append(mw4)
            DV.wait(DV, mw4)
            if jp >= 2:
                DV.wait(PE, m_prep_pe[jp - 2])
            mcast = DV.emit(lambda dv, b=b, sp=sp: dv.tensor_scalar(
                Hh[b][:, :], G[b][:, :], statc[:, sp + 1:sp + 2], None, ALU.add))
            m_prep_ac.append(mcast)
            PE.wait(DV, mcast)

            def pmm(pe, jp=jp, b=b):
                last = None
                for c in range(DC):
                    last = pe.matmul(pp2[:, 4 * c:4 * c + 4],
                                     Hh[b][:, c * 128:(c + 1) * 128],
                                     w4b[b][:, :],
                                     start=(jp == 0 and c == 0),
                                     stop=(jp == PREV - 1 and c == DC - 1))
                return last
            m_pe = PE.emit(pmm)
            m_prep_pe.append(m_pe)
        DV.wait(PE, m_prep_pe[-1])
        m_pev = DV.emit(lambda dv: dv.tensor_copy(pprev_bf[:, :], pp2[:, 0:64]))

        if kphase < 3:
            _cut()
            return nc
        stA = contextlib.ExitStack()
        _open_stacks.append(stA)
        wzT_s = stA.enter_context(nc.sbuf_tensor("wzT_s", [128, DC * D], BF16))
        xlnT_s = stA.enter_context(nc.sbuf_tensor("xlnT_s", [128, DC * T], BF16))
        gam_b = stA.enter_context(nc.sbuf_tensor("gam_b", [128, D], BF16))
        bet_b = stA.enter_context(nc.sbuf_tensor("bet_b", [128, D], BF16))
        zbf = stA.enter_context(nc.sbuf_tensor("zbf", [128, D], BF16))
        hb = stA.enter_context(nc.sbuf_tensor("hb", [128, D], F32))
        Hh = [stA.enter_context(nc.sbuf_tensor(f"H{i}", [128, D], BF16)) for i in range(2)]

        for c in range(DC):
            GPZ.emit(lambda gp, c=c: gp.dma_start(
                out=wzT_s[:, c * D:(c + 1) * D], in_=wzT[c * 128:(c + 1) * 128, :]))
        GPB.emit(lambda gp: gp.dma_start(out=gam_b[0:1, :], in_=gammain[:, :]))
        GPB.emit(lambda gp: gp.dma_start(out=bet_b[0:1, :], in_=betain[:, :]))
        m_gp_wz = GPZ.n
        m_gb = GPB.n

        PE.wait(GPB, m_gb)
        PE.wait(DV, m16)
        for row_t in (gam_b, bet_b):
            for s in range(4):
                mmm = PE.emit(lambda pe, r=row_t, s=s: pe.matmul(
                    pstat[:, 0:512], onesb[0:1, :], r[0:1, s * 512:(s + 1) * 512],
                    start=True, stop=True))
                DV.wait(PE, mmm)
                mvv = DV.emit(lambda dv, r=row_t, s=s: dv.tensor_copy(
                    r[:, s * 512:(s + 1) * 512], pstat[:, 0:512]))
                PE.wait(DV, mvv)
        m_gbb = DV.n

        PE.wait(DV, m_pev)
        PE.wait(GPZ, m_gp_wz)
        for s in range(4):
            def mmm(pe, s=s):
                last = None
                for c in range(DC):
                    last = pe.matmul(pstat[0:4, 0:512],
                                     pprev_bf[:, 4 * c:4 * c + 4],
                                     wzT_s[:, c * D + s * 512: c * D + (s + 1) * 512],
                                     start=(c == 0), stop=(c == DC - 1))
                return last
            mp = PE.emit(mmm)
            DV.wait(PE, mp)
            md = DV.emit(lambda dv, s=s: dv.tensor_copy(
                marow[0:4, s * 512:(s + 1) * 512], pstat[0:4, 0:512]))
            PE.wait(DV, md)
        m_M = DV.n

        if kphase < 4:
            _cut()
            return nc
        m_a0_ac = []
        m_prep_end_dv = DV.n
        m_prep_end_ac = AC.n
        m_prep_end_pe = PE.n
        slot_user = [0] * 2
        for j in range(TC):
            b = j % 2
            SY.wait(AC, m_prep_end_ac)
            SY.wait(DV, m_prep_end_dv)
            if j >= 2:
                SY.wait(AC, m_a0_ac[j - 2])
            m_ld = SYG[b].emit(lambda sy, j=j, b=b: sy.dma_start(
                out=G[b][:, :], in_=xown[j * 128:(j + 1) * 128, :]))
            DV.wait(SYG[b], m_ld)
            DV.emit(lambda dv, b=b: dv.reduce_sum(statc[:, 0:1], G[b][:, :], axis=AX.X))
            DV.wait(DV, DV.n)
            DV.emit(lambda dv: dv.tensor_scalar_mul(statc[:, 1:2], statc[:, 0:1], -1.0 / D))
            mdn = DV.n
            AC.wait(DV, mdn)
            msq = AC.emit(lambda ac, b=b: ac.activation(G2h[:, :], G[b][:, 0:1024], AF.Square))
            DV.wait(AC, msq)
            mr0 = DV.emit(lambda dv: dv.reduce_sum(statc[:, 2:3], G2h[:, :], axis=AX.X))
            AC.wait(DV, mr0)
            msq2 = AC.emit(lambda ac, b=b: ac.activation(G2h[:, :], G[b][:, 1024:2048], AF.Square))
            DV.wait(AC, msq2)
            DV.emit(lambda dv: dv.reduce_sum(statc[:, 6:7], G2h[:, :], axis=AX.X))
            DV.wait(DV, DV.n)
            DV.emit(lambda dv: dv.tensor_add(statc[:, 2:3], statc[:, 2:3], statc[:, 6:7]))
            DV.wait(DV, DV.n)
            DV.emit(lambda dv: dv.tensor_scalar_mul(statc[:, 3:4], statc[:, 2:3], 1.0 / D))
            DV.emit(lambda dv: dv.tensor_mul(statc[:, 5:6], statc[:, 1:2], statc[:, 1:2]))
            DV.wait(DV, DV.n)
            mvar = DV.emit(lambda dv: dv.tensor_sub(statc[:, 3:4], statc[:, 3:4], statc[:, 5:6]))
            AC.wait(DV, mvar)
            AC.emit(lambda ac: ac.activation(statc[:, 4:5], statc[:, 3:4], AF.Ln,
                                             bias=consts[:, 0:1]))
            AC.wait(AC, AC.n)
            mrs = AC.emit(lambda ac: ac.activation(statc[:, 4:5], statc[:, 4:5], AF.Exp,
                                                   scale=consts[:, 2:3]))
            if ka0 < 2:
                m_a0_ac.append(AC.n)
                try:
                    m_a0_pe.append(PE.n)
                except NameError:
                    m_a0_pe = [PE.n]
                continue
            DV.wait(AC, mrs)
            if j >= 2:
                DV.wait(PE, m_a0_pe[j - 2])
            else:
                DV.wait(PE, m_prep_end_pe)
            mnorm = DV.emit(lambda dv, b=b: dv.tensor_scalar(
                Hh[b][:, :], G[b][:, :], statc[:, 1:2], statc[:, 4:5], ALU.add, ALU.mult))
            if ka0 < 3:
                m_a0_ac.append(AC.n)
                try:
                    m_a0_pe.append(PE.n)
                except NameError:
                    m_a0_pe = [PE.n]
                continue
            PE.wait(DV, mnorm)
            for c in range(DC):
                slot = c % 2
                PE.wait(AC, slot_user[slot])
                mt = PE.emit(lambda pe, b=b, c=c, slot=slot: pe.transpose(
                    ptA[slot][:, 0:128],
                    Hh[b][:, c * 128:(c + 1) * 128], i128b_s[:, :]))
                AC.wait(PE, mt)
                mcp = AC.emit(lambda ac, c=c, j=j, slot=slot: ac.copy(
                    xlnT_s[:, c * T + j * 128: c * T + (j + 1) * 128],
                    ptA[slot][:, 0:128]))
                slot_user[slot] = mcp
            m_a0_ac.append(AC.n)
            try:
                m_a0_pe.append(PE.n)
            except NameError:
                m_a0_pe = [PE.n]

        m_a0_done_pe = PE.n
        m_a0_end_dv = DV.n

        if kphase < 5:
            _cut()
            return nc
        stAz = contextlib.ExitStack()
        _open_stacks.append(stAz)
        pz = stAz.enter_context(nc.psum_tensor("pz", [128, 1024], F32))
        ph = stAz.enter_context(nc.psum_tensor("ph", [128, 1024], F32))

        m_zev = 0
        m_hev = 0
        m_ccast = 0
        m_clmm = [0, 0]
        m_x2st = []
        for j in range(TC):
            b = j % 2
            for hh in range(2):
                PE.wait(AC, m_a0_ac[j])
                PE.wait(GPZ, m_gp_wz)
                PE.wait(AC, m_zev)
                PE.wait(DV, m_ccast)

                def zmm(pe, j=j, hh=hh):
                    last = None
                    for c in range(DC):
                        for s in range(2):
                            last = pe.matmul(
                                pz[:, s * 512:(s + 1) * 512],
                                xlnT_s[:, c * T + j * 128: c * T + (j + 1) * 128],
                                wzT_s[:, c * D + hh * 1024 + s * 512: c * D + hh * 1024 + (s + 1) * 512],
                                start=(c == 0), stop=(c == DC - 1))
                    return last
                mz = PE.emit(zmm)
                AC.wait(PE, mz)
                AC.wait(PE, m_clmm[hh])
                m_zev = AC.emit(lambda ac, hh=hh: ac.copy(
                    zbf[:, hh * 1024:(hh + 1) * 1024], pz[:, :]))
                PE.wait(AC, m_zev)
                PE.wait(DV, max(m_hev, m_M, m_ccast))

                def hmm(pe, j=j, hh=hh):
                    last = None
                    for s in range(2):
                        sl = slice(hh * 1024 + s * 512, hh * 1024 + (s + 1) * 512)
                        psl = slice(s * 512, (s + 1) * 512)
                        pe.matmul(ph[:, psl], bmat[:, :], zbf[:, sl], start=True, stop=False)
                        pe.matmul(ph[:, psl], aloc[:, :], carry_bf[0:4, sl], start=False, stop=False)
                        last = pe.matmul(ph[:, psl], am[:, j * 128:(j + 1) * 128],
                                         marow[0:4, sl], start=False, stop=True)
                    return last
                mh = PE.emit(hmm)
                DV.wait(PE, mh)
                if j >= 1:
                    DV.wait(SYX2, m_x2st[j - 1])
                m_hev = DV.emit(lambda dv, hh=hh: dv.tensor_copy(
                    hb[:, hh * 1024:(hh + 1) * 1024], ph[:, :]))
                PE.wait(AC, m_zev)
                PE.wait(DV, m_ccast)

                def clmm(pe, hh=hh):
                    last = None
                    for s in range(2):
                        sl = slice(hh * 1024 + s * 512, hh * 1024 + (s + 1) * 512)
                        last = pe.matmul(pz[0:4, s * 512:(s + 1) * 512], clcoef[:, :],
                                         zbf[:, sl], start=True, stop=True)
                    return last
                mcl2 = PE.emit(clmm)
                m_clmm[hh] = mcl2
                DV.wait(PE, mcl2)
                DV.emit(lambda dv, hh=hh: dv.scalar_tensor_tensor(
                    carry[0:4, hh * 1024:(hh + 1) * 1024],
                    carry[0:4, hh * 1024:(hh + 1) * 1024],
                    scol[0:4, 3:4], pz[0:4, 0:1024], ALU.mult, ALU.add))
                DV.wait(DV, DV.n)
                m_ccast = DV.emit(lambda dv, hh=hh: dv.tensor_copy(
                    carry_bf[0:4, hh * 1024:(hh + 1) * 1024],
                    carry[0:4, hh * 1024:(hh + 1) * 1024]))
            sp = 8 * (j % 2)
            AC.wait(DV, m_hev)
            if j >= 2:
                AC.wait(DV, m_x2dv[j - 2])
            mvar, mrs = emit_stats(lambda: hb[:, :], lambda: Hh[0][:, :], sp)
            if j >= 2:
                SY.wait(DV, m_x2dv[j - 2])
            else:
                SY.wait(DV, m_a0_end_dv)
            SY.wait(DV, m_a0_end_dv)
            mxl = SYG[b].emit(lambda sy, j=j, b=b: sy.dma_start(
                out=G[b][:, :], in_=xown[j * 128:(j + 1) * 128, :]))
            DV.wait(AC, mrs)
            DV.emit(lambda dv, sp=sp: dv.tensor_scalar(
                hb[:, :], hb[:, :], statc[:, sp + 1:sp + 2], statc[:, sp + 4:sp + 5],
                ALU.add, ALU.mult))
            DV.wait(DV, DV.n)
            DV.emit(lambda dv: dv.tensor_mul(hb[:, :], hb[:, :], gam_b[:, :]))
            DV.wait(DV, DV.n)
            DV.emit(lambda dv: dv.tensor_add(hb[:, :], hb[:, :], bet_b[:, :]))
            DV.wait(DV, DV.n)
            DV.wait(SYG[b], mxl)
            mx2 = DV.emit(lambda dv, b=b: dv.tensor_add(hb[:, :], hb[:, :], G[b][:, :]))
            try:
                m_x2dv.append(mx2)
            except NameError:
                m_x2dv = [mx2]
            SY.wait(DV, mx2)
            mst = SYX2.emit(lambda sy, j=j: sy.dma_start(
                out=x2_dram[j * 128:(j + 1) * 128, :], in_=hb[:, :]))
            m_x2st.append(mst)

        if kphase < 6:
            _cut()
            return nc
        m_a1z_end = dict(pe=PE.n, dv=DV.n, ac=AC.n, sy=SY.n)
        _close_stack(stAz)
        _close_stack(stP)
        _close_stack(stA)

        stB1 = contextlib.ExitStack()
        _open_stacks.append(stB1)
        uT_s = stB1.enter_context(nc.sbuf_tensor("uT_s", [128, DC * T], BF16))
        HhB = [stB1.enter_context(nc.sbuf_tensor(f"HB{i}", [128, D], BF16)) for i in range(2)]
        su_b = stB1.enter_context(nc.sbuf_tensor("su_b", [128, D], F32))
        wslab = [stB1.enter_context(nc.sbuf_tensor(f"wslab{i}", [128, 16 * 256], BF16)) for i in range(2)]
        gslab = [stB1.enter_context(nc.sbuf_tensor(f"gslab{i}", [128, 16 * 256], BF16)) for i in range(2)]
        btmp = stB1.enter_context(nc.sbuf_tensor("btmp", [128, 1024], BF16))
        silb2 = stB1.enter_context(nc.sbuf_tensor("silb2", [128, 1024], F32))
        vtmp2 = stB1.enter_context(nc.sbuf_tensor("vtmp2", [128, 1024], F32))
        vchB = [stB1.enter_context(nc.sbuf_tensor(f"vchB{i}", [128, T], BF16)) for i in range(2)]

        stB0 = contextlib.ExitStack()
        _open_stacks.append(stB0)
        ptB = [stB0.enter_context(nc.psum_tensor(f"ptB{i}", [128, 1024], BF16))
               for i in range(2)]
        pH0 = stB0.enter_context(nc.psum_tensor("pH0", [128, 512], F32))
        pH1 = stB0.enter_context(nc.psum_tensor("pH1", [128, 512], F32))

        SYR.emit(lambda sy: sy.dma_start(out=su_b[0:1, :], in_=suugin[:, :]))
        m_sul = SYR.n
        PE.wait(SYR, m_sul)
        su_scale = 1.0 / (SCALE * float(np.sqrt(D)))
        for s in range(4):
            mmm = PE.emit(lambda pe, s=s: pe.matmul(
                pH0[:, 0:512], onesr[0:1, :], su_b[0:1, s * 512:(s + 1) * 512],
                start=True, stop=True))
            DV.wait(PE, mmm)
            mvv = DV.emit(lambda dv, s=s: dv.tensor_scalar_mul(
                su_b[:, s * 512:(s + 1) * 512], pH0[:, 0:512], su_scale))
            PE.wait(DV, mvv)
        m_sub = DV.n

        m_b0_ac = []
        slotB = [0] * 2
        m_b0_pe_chunks = []
        for j in range(TC):
            b = j % 2
            SY.wait(SYX2, m_x2st[j])
            if j >= 2:
                SY.wait(DV, m_b0_norm[j - 2])
            else:
                SY.wait(DV, m_a1z_end["dv"])
            mld = SYG[b].emit(lambda sy, j=j, b=b: sy.dma_start(
                out=G[b][:, :], in_=x2_dram[j * 128:(j + 1) * 128, :]))
            AC.wait(SYG[b], mld)
            if j < 2:
                AC.wait(DV, m_a1z_end["dv"])
            sp = 8 * (j % 2)
            if j >= 2:
                AC.wait(DV, m_b0_norm[j - 2])
            mvar, mrs = emit_stats(lambda b=b: G[b][:, :], lambda: vchB[0][:, :], sp,
                                   extra_dv_wait=[(SYG[b], mld)])
            DV.wait(AC, mrs)
            DV.wait(DV, m_sub)
            if j >= 2:
                DV.wait(PE, m_b0_pe_chunks[j - 2])
            else:
                DV.wait(PE, m_a1z_end["pe"])
            for hx in range(2):
                DV.emit(lambda dv, b=b, hx=hx, sp=sp: dv.tensor_scalar(
                    G2h[:, :], G[b][:, hx * 1024:(hx + 1) * 1024],
                    statc[:, sp + 1:sp + 2], statc[:, sp + 4:sp + 5], ALU.add, ALU.mult))
                DV.wait(DV, DV.n)
                DV.emit(lambda dv, b=b, hx=hx: dv.tensor_mul(
                    HhB[b][:, hx * 1024:(hx + 1) * 1024], G2h[:, :],
                    su_b[:, hx * 1024:(hx + 1) * 1024]))
                DV.wait(DV, DV.n)
            mu_ = DV.n
            prev = mu_
            for st_i, dist in enumerate((1, 2, 4, 8)):
                blkw = dist * 128
                ngrp = DC // (2 * dist)

                def bfly(dv, b=b, dist=dist, blkw=blkw, ngrp=ngrp, which=0):
                    vv = HhB[b][:, :].rearrange("p (g two q) -> p g two q", two=2, q=blkw)
                    a = vv[:, :, 0, :]
                    bb = vv[:, :, 1, :]
                    t = btmp[:, 0:ngrp * blkw].rearrange("p (g q) -> p g q", q=blkw)
                    if which == 0:
                        return dv.tensor_add(t, a, bb)
                    elif which == 1:
                        return dv.tensor_sub(bb, a, bb)
                    else:
                        return dv.tensor_copy(a, t)
                DV.wait(DV, prev)
                DV.emit(lambda dv, f=bfly: f(dv, which=0))
                DV.wait(DV, DV.n)
                DV.emit(lambda dv, f=bfly: f(dv, which=1))
                DV.wait(DV, DV.n)
                prev = DV.emit(lambda dv, f=bfly: f(dv, which=2))
            m_b0n = prev
            try:
                m_b0_norm.append(m_b0n)
            except NameError:
                m_b0_norm = [m_b0n]
            PE.wait(DV, m_b0n)
            for c in range(DC):
                slot = c % 2
                PE.wait(AC, slotB[slot])
                mt = PE.emit(lambda pe, b=b, c=c, slot=slot: pe.transpose(
                    ptB[slot][:, 0:128],
                    HhB[b][:, c * 128:(c + 1) * 128], i128b_s[:, :]))
                AC.wait(PE, mt)
                mcp = AC.emit(lambda ac, c=c, j=j, slot=slot: ac.copy(
                    uT_s[:, c * T + j * 128: c * T + (j + 1) * 128],
                    ptB[slot][:, 0:128]))
                slotB[slot] = mcp
            m_b0_ac.append(AC.n)
            m_b0_pe_chunks.append(PE.n)
        m_h128 = []
        PE.wait(AC, m_b0_ac[-1])
        prev_ev = [0, 0]
        for c in range(DC):
            for s in range(4):
                par = (c * 4 + s) % 2
                ptgt = pH0 if par == 0 else pH1
                PE.wait(AC, prev_ev[par])
                mm_ = PE.emit(lambda pe, c=c, s=s, ptgt=ptgt: pe.matmul(
                    ptgt[:, 0:512], h128_s[:, :],
                    uT_s[:, c * T + s * 512: c * T + (s + 1) * 512],
                    start=True, stop=True))
                AC.wait(PE, mm_)
                mev = AC.emit(lambda ac, c=c, s=s, ptgt=ptgt: ac.copy(
                    uT_s[:, c * T + s * 512: c * T + (s + 1) * 512], ptgt[:, 0:512]))
                prev_ev[par] = mev
        m_uT = AC.n

        if kphase < 7:
            _cut()
            return nc
        _close_stack(stB0)
        stB1p = contextlib.ExitStack()
        _open_stacks.append(stB1p)
        pu = [stB1p.enter_context(nc.psum_tensor(f"pu{i}", [128, 1024], F32)) for i in range(2)]
        pg = [stB1p.enter_context(nc.psum_tensor(f"pg{i}", [128, 1024], F32)) for i in range(2)]

        def emit_slab(w, wbuf, gbuf):
            sq = SYW[w % 2]
            for c in range(DC):
                sq.emit(lambda sy, w=w, c=c, wbuf=wbuf: sy.dma_start(
                    out=wbuf[:, c * 256:(c + 1) * 256],
                    in_=wup_bf[c * 128:(c + 1) * 128, w * 256:(w + 1) * 256]))
            for c in range(DC):
                sq.emit(lambda sy, w=w, c=c, gbuf=gbuf: sy.dma_start(
                    out=gbuf[:, c * 256:(c + 1) * 256],
                    in_=wgate_bf[c * 128:(c + 1) * 128, w * 256:(w + 1) * 256]))
            return sq.n

        m_slab = {}
        m_win_pe = {}
        SY.wait(GPW, m_gp_upgate)
        m_slab[0] = emit_slab(0, wslab[0], gslab[0])
        m_vst = []
        m_vdv = {}
        m_stt_prev = 0
        for jc in range(JC):
            w = jc // 2
            if jc % 2 == 0 and w + 1 < JC // 2:
                if w >= 1:
                    SY.wait(PE, m_win_pe[w - 1])
                m_slab[w + 1] = emit_slab(w + 1, wslab[(w + 1) % 2], gslab[(w + 1) % 2])
            for hh in range(2):
                par = (jc * 2 + hh) % 2
                PE.wait(SYW[w % 2], m_slab[w])
                PE.wait(AC, m_uT)
                if (jc * 2 + hh) >= 2:
                    PE.wait(DV, m_vdv[(jc * 2 + hh) - 2])

                def upmm(pe, jc=jc, hh=hh, w=w, par=par):
                    jj = jc % 2
                    last = None
                    for c in range(DC):
                        for s in range(2):
                            last = pe.matmul(
                                pu[par][:, s * 512:(s + 1) * 512],
                                wslab[w % 2][:, c * 256 + jj * 128: c * 256 + (jj + 1) * 128],
                                uT_s[:, c * T + hh * 1024 + s * 512: c * T + hh * 1024 + (s + 1) * 512],
                                start=(c == 0), stop=(c == DC - 1))
                    for c in range(DC):
                        for s in range(2):
                            last = pe.matmul(
                                pg[par][:, s * 512:(s + 1) * 512],
                                gslab[w % 2][:, c * 256 + jj * 128: c * 256 + (jj + 1) * 128],
                                uT_s[:, c * T + hh * 1024 + s * 512: c * T + hh * 1024 + (s + 1) * 512],
                                start=(c == 0), stop=(c == DC - 1))
                    return last
                mup = PE.emit(upmm)
                AC.wait(PE, mup)
                AC.wait(DV, m_stt_prev)
                msil = AC.emit(lambda ac, jc=jc, par=par: ac.activation(
                    silb2[:, :], pg[par][:, :], AF.Silu, scale=wsgT[:, jc:jc + 1]))
                DV.wait(AC, msil)
                if jc >= 2:
                    DV.wait(SYV[jc % 2], m_vst[jc - 2])
                DV.wait(DV, DV.n)
                m_stt_prev = DV.emit(lambda dv, jc=jc, par=par: dv.scalar_tensor_tensor(
                    vtmp2[:, :], pu[par][:, :], wsupT[:, jc:jc + 1], silb2[:, :],
                    ALU.mult, ALU.mult))
                DV.wait(DV, DV.n)
                mvd = DV.emit(lambda dv, jc=jc, hh=hh: dv.tensor_scalar(
                    vchB[jc % 2][:, hh * 1024:(hh + 1) * 1024], vtmp2[:, :],
                    sudpT[:, jc:jc + 1], None, ALU.mult))
                m_vdv[jc * 2 + hh] = mvd
            SY.wait(DV, m_vdv[jc * 2 + 1])
            mst = SYV[jc % 2].emit(lambda sy, jc=jc: sy.dma_start(
                out=v_dram[jc * 128:(jc + 1) * 128, :], in_=vchB[jc % 2][:, :]))
            m_vst.append(mst)
            if jc % 2 == 1:
                m_win_pe[jc // 2] = PE.n
        m_b1_end = dict(pe=PE.n, dv=DV.n, ac=AC.n, sy=SY.n)

        if kphase < 8:
            _cut()
            return nc
        _close_stack(stB1p)
        _close_stack(stB1)
        stB2 = contextlib.ExitStack()
        _open_stacks.append(stB2)
        vq2 = [stB2.enter_context(nc.sbuf_tensor(f"vq{i}", [128, JC * 256], BF16))
               for i in range(2)]
        btmp2 = stB2.enter_context(nc.sbuf_tensor("btmp2", [128, 32 * 256], BF16))
        pq = [stB2.enter_context(nc.psum_tensor(f"pq{i}", [128, 512], F32)) for i in range(2)]

        m_v2st = {}
        m_bf_end = {}
        prev_evq = [0, 0]
        for e in range(8):
            p = e % 2
            if e == 0:
                SY.wait(SYV[0], SYV[0].n)
                SY.wait(SYV[1], SYV[1].n)
            if e >= 2:
                SY.wait(SYV2[p], m_v2st[e - 2])
            for cg in range(8):
                def ldq(sy, e=e, cg=cg, p=p):
                    src_ap = v_dram[cg * 1024:(cg + 1) * 1024, e * 256:(e + 1) * 256]
                    src_ap = src_ap.rearrange("(c q) t -> q c t", q=128)
                    dst_ap = vq2[p][:, cg * 8 * 256:(cg + 1) * 8 * 256].rearrange(
                        "q (c t) -> q c t", t=256)
                    return sy.dma_start(out=dst_ap, in_=src_ap)
                SYVQ[p].emit(ldq)
            mldq = SYVQ[p].n
            for c in range(JC):
                par = c % 2
                PE.wait(SYVQ[p], mldq)
                PE.wait(AC, prev_evq[par])
                if e >= 2:
                    PE.wait(DV, m_bf_end[e - 2])
                mmq = PE.emit(lambda pe, c=c, par=par, p=p: pe.matmul(
                    pq[par][:, 0:256], h128_s[:, :], vq2[p][:, c * 256:(c + 1) * 256],
                    start=True, stop=True))
                AC.wait(PE, mmq)
                mevq = AC.emit(lambda ac, c=c, par=par, p=p: ac.copy(
                    vq2[p][:, c * 256:(c + 1) * 256], pq[par][:, 0:256]))
                prev_evq[par] = mevq
            m_h128q = AC.n
            DV.wait(AC, m_h128q)
            prev = DV.n
            for dist in (1, 2, 4, 8, 16, 32):
                blkw = dist * 256
                ngrp = JC // (2 * dist)

                def bfq(dv, dist=dist, blkw=blkw, ngrp=ngrp, p=p, which=0):
                    vv = vq2[p][:, :].rearrange("q (g two q2) -> q g two q2", two=2, q2=blkw)
                    a = vv[:, :, 0, :]
                    bb = vv[:, :, 1, :]
                    t = btmp2[:, 0:ngrp * blkw].rearrange("q (g q2) -> q g q2", q2=blkw)
                    if which == 0:
                        return dv.tensor_add(t, a, bb)
                    elif which == 1:
                        return dv.tensor_sub(bb, a, bb)
                    else:
                        return dv.tensor_copy(a, t)
                DV.wait(DV, prev)
                DV.emit(lambda dv, f=bfq: f(dv, which=0))
                DV.wait(DV, DV.n)
                DV.emit(lambda dv, f=bfq: f(dv, which=1))
                DV.wait(DV, DV.n)
                prev = DV.emit(lambda dv, f=bfq: f(dv, which=2))
            m_bf_end[e] = prev
            SY.wait(DV, prev)
            for cg in range(8):
                def stq(sy, e=e, cg=cg, p=p):
                    dst_ap = v2_dram[cg * 1024:(cg + 1) * 1024, e * 256:(e + 1) * 256]
                    dst_ap = dst_ap.rearrange("(c q) t -> q c t", q=128)
                    src_ap = vq2[p][:, cg * 8 * 256:(cg + 1) * 8 * 256].rearrange(
                        "q (c t) -> q c t", t=256)
                    return sy.dma_start(out=dst_ap, in_=src_ap)
                SYV2[p].emit(stq)
            m_v2st[e] = SYV2[p].n
        m_b2_end_sy = (SYV2[0].n, SYV2[1].n)
        m_b2_end_pe = PE.n
        m_b2_end_dv = DV.n

        _close_stack(stB2)
        stB3 = contextlib.ExitStack()
        _open_stacks.append(stB3)
        accs = [stB3.enter_context(nc.sbuf_tensor(f"acc{i}", [128, TC * 512], F32))
                for i in range(2)]
        v2sl = [stB3.enter_context(nc.sbuf_tensor(f"v2sl{i}", [128, 8 * T], BF16))
                for i in range(2)]
        wdsl = [stB3.enter_context(nc.sbuf_tensor(f"wdsl{i}", [128, 8 * 512], BF16))
                for i in range(2)]
        ep = [stB3.enter_context(nc.sbuf_tensor(f"ep{i}", [128, 512], F32)) for i in range(2)]
        wsdp_b = stB3.enter_context(nc.sbuf_tensor("wsdp_b", [128, D], BF16))
        pd = [stB3.enter_context(nc.psum_tensor(f"pd{i}", [128, 512], F32)) for i in range(4)]
        pep = pd[3]

        GP.wait(PE, m_b2_end_pe)
        GP.wait(DV, m_b2_end_dv)
        GP.wait(SYV2[0], m_b2_end_sy[0])
        GP.wait(SYV2[1], m_b2_end_sy[1])
        GPB.emit(lambda gp: gp.dma_start(out=wsdp_b[0:1, :], in_=wsdpin[:, :]))
        m_wsl = GPB.n
        PE.wait(GPB, m_wsl)
        PE.wait(SYV2[0], m_b2_end_sy[0])
        PE.wait(SYV2[1], m_b2_end_sy[1])
        for s in range(4):
            mmm = PE.emit(lambda pe, s=s: pe.matmul(
                pd[0][:, :], onesb[0:1, :], wsdp_b[0:1, s * 512:(s + 1) * 512],
                start=True, stop=True))
            DV.wait(PE, mmm)
            mvv = DV.emit(lambda dv, s=s: dv.tensor_scalar_mul(
                wsdp_b[:, s * 512:(s + 1) * 512], pd[0][:, :], SCALE))
            PE.wait(DV, mvv)
        m_wsb = DV.n

        m_yst = []
        m_padd = {}
        m_sg_pe = {}
        m_acc_ep = {}
        DV.wait(SYV2[0], m_b2_end_sy[0])
        DV.wait(SYV2[1], m_b2_end_sy[1])
        first = True

        def emit_epilogue(ihq):
            ap = ihq % 2
            for tch in range(TC):
                b = tch % 2
                if SYY[b].n > 0:
                    SY.wait(SYY[b], SYY[b].n)
                if len(m_yst) >= 2:
                    SY.wait(DV, m_ep_dv[ihq * TC + tch - 2])
                mxl = SYE[b].emit(lambda sy, tch=tch, ihq=ihq, b=b: sy.dma_start(
                    out=ep[b][:, :],
                    in_=x2_dram[tch * 128:(tch + 1) * 128, ihq * 512:(ihq + 1) * 512]))
                DV.wait(SYE[b], mxl)
                DV.wait(DV, DV.n)
                DV.emit(lambda dv, tch=tch, ihq=ihq, ap=ap: dv.tensor_mul(
                    pep[:, :], accs[ap][:, tch * 512:(tch + 1) * 512],
                    wsdp_b[:, ihq * 512:(ihq + 1) * 512]))
                DV.wait(DV, DV.n)
                mepd = DV.emit(lambda dv, b=b: dv.tensor_add(
                    ep[b][:, :], ep[b][:, :], pep[:, :]))
                m_ep_dv.append(mepd)
                SY.wait(DV, mepd)
                myst = SYY[b].emit(lambda sy, tch=tch, ihq=ihq, b=b: sy.dma_start(
                    out=yout[tch * 128:(tch + 1) * 128, ihq * 512:(ihq + 1) * 512],
                    in_=ep[b][:, :]))
                m_yst.append(myst)
            m_acc_ep[ap] = DV.n

        m_ep_dv = []
        for ihq in range(4):
            ap = ihq % 2
            m_accz = DV.emit(lambda dv, ap=ap: dv.memset(accs[ap][:, :], 0.0))
            m_last_add = {}
            for sg in range(8):
                sp_ = (ihq * 8 + sg) % 2
                if first:
                    SY.wait(SYV2[0], m_b2_end_sy[0])
                    SY.wait(SYV2[1], m_b2_end_sy[1])
                    GP.wait(PE, m_b2_end_pe)
                    GP.wait(DV, m_b2_end_dv)
                    GP.wait(SYV2[0], m_b2_end_sy[0])
                    GP.wait(SYV2[1], m_b2_end_sy[1])
                    first = False
                else:
                    SY.wait(PE, m_sg_pe.get(sp_, 0))
                    GP.wait(PE, m_sg_pe.get(sp_, 0))
                for jj in range(8):
                    jcg = sg * 8 + jj
                    SYSL2[sp_].emit(lambda sy, jcg=jcg, jj=jj, sp_=sp_: sy.dma_start(
                        out=v2sl[sp_][:, jj * T:(jj + 1) * T],
                        in_=v2_dram[jcg * 128:(jcg + 1) * 128, :]))
                mv2 = SYSL2[sp_].n
                for jj in range(8):
                    jcg = sg * 8 + jj
                    GPD2[sp_].emit(lambda gp, jcg=jcg, jj=jj, ihq=ihq, sp_=sp_: gp.dma_start(
                        out=wdsl[sp_][:, jj * 512:(jj + 1) * 512],
                        in_=wdnT[jcg * 128:(jcg + 1) * 128, ihq * 512:(ihq + 1) * 512]))
                mwd = GPD2[sp_].n
                for tch in range(TC):
                    par = tch % 3
                    PE.wait(SYSL2[sp_], mv2)
                    PE.wait(GPD2[sp_], mwd)
                    PE.wait(DV, m_padd.get(par, 0))
                    PE.wait(DV, m_accz)

                    def dmm(pe, tch=tch, par=par, sp_=sp_):
                        last = None
                        for jj in range(8):
                            last = pe.matmul(
                                pd[par][:, :],
                                v2sl[sp_][:, jj * T + tch * 128: jj * T + (tch + 1) * 128],
                                wdsl[sp_][:, jj * 512:(jj + 1) * 512],
                                start=(jj == 0), stop=(jj == 7))
                        return last
                    mdm = PE.emit(dmm)
                    DV.wait(PE, mdm)
                    madd = DV.emit(lambda dv, tch=tch, par=par, ap=ap: dv.tensor_add(
                        accs[ap][:, tch * 512:(tch + 1) * 512],
                        accs[ap][:, tch * 512:(tch + 1) * 512], pd[par][:, :]))
                    m_padd[par] = madd
                    m_last_add[tch] = madd
                m_sg_pe[sp_] = PE.n
                if sg == 2 and ihq > 0:
                    emit_epilogue(ihq - 1)
            m_q_adds = DV.n
        emit_epilogue(3)
        _close_stack(stB3)

        SY.wait(SYY[0], SYY[0].n)
        SY.wait(SYY[1], SYY[1].n)
        SY.wait(SY, SY.n)
        GP.wait(GP, GP.n)

        _flush_all(blk, PE, DV, AC, SY, GP)

    return nc


def _hadamard(n):
    h = np.array([[1.0]], dtype=np.float64)
    while h.shape[0] < n:
        h = np.block([[h, h], [h, -h]])
    return h


_NC_CACHE = [None]


def kernel(**inputs):
    import ml_dtypes
    x = np.asarray(inputs["L0_attn_x"])
    W = np.asarray(inputs["L0_attn_W"])
    lam_logit = np.asarray(inputs["L0_attn_lam_logit"]).reshape(1, 4)
    eta = np.asarray(inputs["L0_attn_eta"]).reshape(1, 4)
    gamma = np.asarray(inputs["L0_attn_gamma"]).reshape(1, D)
    beta = np.asarray(inputs["L0_attn_beta"]).reshape(1, D)
    su_ug = np.asarray(inputs["L0_ffn_SU_ug"]).reshape(1, D)
    su_dp = np.asarray(inputs["L0_ffn_SU_dp"]).reshape(64, 128)
    ws_ug = np.asarray(inputs["L0_ffn_Wscale_ug"])
    ws_dp = np.asarray(inputs["L0_ffn_Wscale_dp"]).reshape(1, D)
    W_up = np.asarray(inputs["L0_ffn_W_up"])
    W_gate = np.asarray(inputs["L0_ffn_W_gate"])
    W_down = np.asarray(inputs["L0_ffn_W_down"])

    xf = np.ascontiguousarray(x.reshape(NCORE * T, D))
    xprev = np.ascontiguousarray(xf[:PREV * 128, :])
    wzT = np.ascontiguousarray(W.T)
    wupT = np.ascontiguousarray(W_up.T)
    wgateT = np.ascontiguousarray(W_gate.T)
    wdnT = np.ascontiguousarray(W_down.T)

    h128 = (_hadamard(128)).astype(ml_dtypes.bfloat16)
    i128b = np.eye(128, dtype=np.float32).astype(ml_dtypes.bfloat16)
    i128f = np.eye(128, dtype=np.float32)
    ti = np.arange(128, dtype=np.float32)
    tms = (ti[None, :] - ti[:, None]).astype(np.float32)
    mge = (ti[None, :] >= ti[:, None]).astype(np.float32)
    riota = np.repeat((127.0 - ti)[:, None], 4, axis=1).astype(np.float32)
    grid4 = np.repeat(ti[None, :], 4, axis=0).astype(np.float32)

    common = {
        "wzT": wzT, "wupT": wupT, "wgateT": wgateT, "wdnT": wdnT,
        "lamlog": lam_logit.astype(np.float32), "etain": eta.astype(np.float32),
        "gammain": gamma.astype(np.float32), "betain": beta.astype(np.float32),
        "suugin": su_ug.astype(np.float32), "sudpin": su_dp.astype(np.float32),
        "wsupin": np.ascontiguousarray(ws_ug[:J].reshape(64, 128)).astype(np.float32),
        "wsgin": np.ascontiguousarray(ws_ug[J:].reshape(64, 128)).astype(np.float32),
        "wsdpin": ws_dp.astype(np.float32),
        "h128in": h128, "i128fin": i128f, "i128bin": i128b,
        "tmsin": tms, "mgein": mge, "riotain": riota, "grid4in": grid4,
        "xfull": xprev,
    }
    in_maps = []
    for c in range(NCORE):
        tau = np.arange(128, dtype=np.float64)[:, None]
        jj = np.arange(PREV, dtype=np.float64)[None, :]
        expo = (2048.0 * c - 1.0) - 128.0 * jj - tau
        live = (jj < 16 * c) & np.ones((128, 1), bool)
        expo = np.where(live, expo, 1.0e6)
        pmask = live.astype(np.float32)
        m = dict(common)
        m["xown"] = np.ascontiguousarray(xf[c * T:(c + 1) * T, :])
        m["expoin"] = expo.astype(np.float32)
        m["pmaskin"] = np.ascontiguousarray(pmask)
        in_maps.append(m)

    if _NC_CACHE[0] is None:
        _NC_CACHE[0] = build_nc()
    nc = _NC_CACHE[0]
    res = run_bass_kernel_spmd(nc, in_maps, core_ids=list(range(NCORE)))
    out = np.concatenate([np.asarray(r["yout"]) for r in res.results], axis=0)
    return out.reshape(1, NCORE * T, D).astype(np.float32)


# version 06
# speedup vs baseline: 1.1718x; 1.0017x over previous
import numpy as np
import contextlib
import concourse.bass as bass
import concourse.mybir as mybir
from concourse.bass_utils import run_bass_kernel_spmd

F32 = mybir.dt.float32
BF16 = mybir.dt.bfloat16
AF = mybir.ActivationFunctionType
ALU = mybir.AluOpType
AX = mybir.AxisListType

NCORE = 8
T = 2048
D = 2048
J = 8192
TC = T // 128
DC = D // 128
JC = J // 128
PREV = (NCORE - 1) * TC
EPS = 1e-5
SCALE = 64.0


class Q:

    def __init__(self, name, dma=False):
        self.name = name
        self.dma = dma
        self.sem = None
        self.ops = []
        self.n = 0

    def emit(self, f, inc=None):
        if inc is None:
            inc = 16 if self.dma else 1
        sem = lambda: self.sem

        def op(eng, f=f, inc=inc):
            ins = f(eng)
            ins.then_inc(self.sem, inc)
        self.ops.append(op)
        self.n += inc
        return self.n

    def raw(self, f):
        self.ops.append(lambda eng, f=f: f(eng))

    def wait(self, other, val):
        if val > 0:
            self.ops.append(lambda eng, o=other, v=val: eng.wait_ge(o.sem, v))


class _CutDone(Exception):
    def __init__(self, env):
        self.env = env


def _flush_all(blk, PE, DV, AC, SY, GP):
    def flush(q, eng):
        q.eng = eng
        for op in q.ops:
            op(eng)
    blk.tensor(lambda eng: flush(PE, eng))
    blk.vector(lambda eng: flush(DV, eng))
    blk.scalar(lambda eng: flush(AC, eng))
    blk.sync(lambda eng: flush(SY, eng))
    blk.gpsimd(lambda eng: flush(GP, eng))


class SubQ:

    def __init__(self, parent):
        self.parent = parent
        self.sem = None
        self.n = 0

    def emit(self, f):
        def op(eng, f=f):
            f(eng).then_inc(self.sem, 16)
        self.parent.ops.append(op)
        self.n += 16
        return self.n


def build_nc():
    import os
    kphase = int(os.environ.get("KPHASE", "99"))
    ka0 = int(os.environ.get("KA0", "99"))
    nc = bass.Bass("TRN2", target_bir_lowering=False)

    xfull = nc.dram_tensor("xfull", [PREV * 128, D], F32, kind="ExternalInput")
    xown = nc.dram_tensor("xown", [T, D], F32, kind="ExternalInput")
    wzT = nc.dram_tensor("wzT", [D, D], F32, kind="ExternalInput")
    wupT = nc.dram_tensor("wupT", [D, J], F32, kind="ExternalInput")
    wgateT = nc.dram_tensor("wgateT", [D, J], F32, kind="ExternalInput")
    wdnT = nc.dram_tensor("wdnT", [J, D], F32, kind="ExternalInput")
    lamlog = nc.dram_tensor("lamlog", [1, 4], F32, kind="ExternalInput")
    etain = nc.dram_tensor("etain", [1, 4], F32, kind="ExternalInput")
    gammain = nc.dram_tensor("gammain", [1, D], F32, kind="ExternalInput")
    betain = nc.dram_tensor("betain", [1, D], F32, kind="ExternalInput")
    suugin = nc.dram_tensor("suugin", [1, D], F32, kind="ExternalInput")
    sudpin = nc.dram_tensor("sudpin", [64, 128], F32, kind="ExternalInput")
    wsupin = nc.dram_tensor("wsupin", [64, 128], F32, kind="ExternalInput")
    wsgin = nc.dram_tensor("wsgin", [64, 128], F32, kind="ExternalInput")
    wsdpin = nc.dram_tensor("wsdpin", [1, D], F32, kind="ExternalInput")
    h128in = nc.dram_tensor("h128in", [128, 128], BF16, kind="ExternalInput")
    i128fin = nc.dram_tensor("i128fin", [128, 128], F32, kind="ExternalInput")
    i128bin = nc.dram_tensor("i128bin", [128, 128], BF16, kind="ExternalInput")
    tmsin = nc.dram_tensor("tmsin", [128, 128], F32, kind="ExternalInput")
    mgein = nc.dram_tensor("mgein", [128, 128], F32, kind="ExternalInput")
    riotain = nc.dram_tensor("riotain", [128, 4], F32, kind="ExternalInput")
    grid4in = nc.dram_tensor("grid4in", [4, 128], F32, kind="ExternalInput")
    expoin = nc.dram_tensor("expoin", [128, PREV], F32, kind="ExternalInput")
    pmaskin = nc.dram_tensor("pmaskin", [128, PREV], F32, kind="ExternalInput")

    yout = nc.dram_tensor("yout", [T, D], F32, kind="ExternalOutput")
    x2_dram = nc.dram_tensor("x2_dram", [T, D], F32, kind="Internal")
    wup_bf = nc.dram_tensor("wup_bf", [D, J], BF16, kind="Internal")
    wgate_bf = nc.dram_tensor("wgate_bf", [D, J], BF16, kind="Internal")
    v_dram = nc.dram_tensor("v_dram", [J, T], BF16, kind="Internal")
    v2_dram = nc.dram_tensor("v2_dram", [J, T], BF16, kind="Internal")

    PE, DV, AC = Q("pe"), Q("dv"), Q("ac")
    SY, GP = Q("sy", dma=True), Q("gp", dma=True)
    QS = (PE, DV, AC, SY, GP)

    SYG = [SubQ(SY), SubQ(SY)]
    SYX2 = SubQ(SY)
    SYW = [SubQ(SY), SubQ(SY)]
    SYV = [SubQ(SY), SubQ(SY)]
    SYVQ = [SubQ(SY), SubQ(SY)]
    SYV2 = [SubQ(SY), SubQ(SY)]
    SYSL2 = [SubQ(SY), SubQ(SY)]
    GPD2 = [SubQ(GP), SubQ(GP)]
    SYE = [SubQ(SY), SubQ(SY)]
    SYY = [SubQ(SY), SubQ(SY)]
    SYR = SubQ(SY)
    GPW = SubQ(GP)
    GPZ = SubQ(GP)
    GPB = SubQ(GP)
    SUBS = [SYG[0], SYG[1], SYX2, SYW[0], SYW[1], SYV[0], SYV[1],
            SYVQ[0], SYVQ[1], SYV2[0], SYV2[1],
            SYSL2[0], SYSL2[1], GPD2[0], GPD2[1],
            SYE[0], SYE[1], SYY[0], SYY[1], SYR, GPW, GPZ, GPB]

    with contextlib.ExitStack() as top:
        for q, nm in zip(QS, ("s_pe", "s_dv", "s_ac", "s_sy", "s_gp")):
            q.sem = top.enter_context(nc.semaphore(nm))
        for i, sq in enumerate(SUBS):
            sq.sem = top.enter_context(nc.semaphore(f"s_sub{i}"))

        def sb(name, shape, dt=F32):
            return top.enter_context(nc.sbuf_tensor(name, shape, dt))

        h128_s = sb("h128_s", [128, 128], BF16)
        i128f_s = sb("i128f_s", [128, 128], F32)
        i128b_s = sb("i128b_s", [128, 128], BF16)
        tms_s = sb("tms_s", [128, 128], F32)
        mge_s = sb("mge_s", [128, 128], F32)
        riota_s = sb("riota_s", [128, 4], F32)
        grid4_s = sb("grid4_s", [4, 128], F32)
        consts = sb("consts", [128, 8], F32)
        onesr = sb("onesr", [1, 128], F32)
        srow = sb("srow", [1, 16], F32)
        scol = sb("scol", [4, 4], F32)
        lnlam_b = sb("lnlam_b", [128, 4], F32)
        eta_b = sb("eta_b", [128, 4], F32)
        bmat = sb("bmat", [128, 128], BF16)
        bmf = sb("bmf", [128, 128], F32)
        cscr = sb("cscr", [128, 128], F32)
        clcoef = sb("clcoef", [128, 4], BF16)
        aloc = sb("aloc", [4, 128], BF16)
        am = sb("am", [4, TC * 128], BF16)
        a4t = sb("a4t", [4, 128], F32)
        rowload = sb("rowload", [64, 128], F32)
        sudpT = sb("sudpT", [128, 64], F32)
        wsupT = sb("wsupT", [128, 64], F32)
        wsgT = sb("wsgT", [128, 64], F32)
        marow = sb("marow", [4, D], BF16)
        carry = sb("carry", [4, D], F32)
        carry_bf = sb("carry_bf", [4, D], BF16)
        pprev_bf = sb("pprev_bf", [128, 64], BF16)
        expo_s = sb("expo_s", [128, PREV], F32)
        pmask_s = sb("pmask_s", [128, PREV], F32)
        statc = sb("statc", [128, 16], F32)
        w4f = sb("w4f", [128, 4], F32)
        w4b = [sb(f"w4b{i}", [128, 4], BF16) for i in range(2)]
        G = [sb(f"G{i}", [128, D], F32) for i in range(2)]
        G2h = sb("G2h", [128, 1024], F32)
        onesb = sb("onesb", [1, 128], BF16)

        blk = top.enter_context(nc.Block())
        _open_stacks = []

        def _close_stack(s):
            s.close()
            _open_stacks.remove(s)

        def _cut():
            for s in reversed(_open_stacks):
                s.close()
            _open_stacks.clear()
            _flush_all(blk, PE, DV, AC, SY, GP)


        def emit_stats(src_tile_fn, sink_fn, sp, extra_dv_wait=None):
            if extra_dv_wait is not None:
                for w_ in extra_dv_wait:
                    DV.wait(*w_)
            DV.emit(lambda dv: dv.reduce_sum(statc[:, sp:sp + 1], src_tile_fn(), axis=AX.X))
            DV.wait(DV, DV.n)
            DV.emit(lambda dv: dv.tensor_scalar_mul(statc[:, sp + 1:sp + 2],
                                                    statc[:, sp:sp + 1], -1.0 / D))
            AC.wait(AC, AC.n)
            mss = AC.emit(lambda ac: ac.activation(sink_fn(), src_tile_fn(), AF.Square,
                                                   accum_out=statc[:, sp + 2:sp + 3]))
            DV.wait(AC, mss)
            DV.emit(lambda dv: dv.tensor_scalar_mul(statc[:, sp + 3:sp + 4],
                                                    statc[:, sp + 2:sp + 3], 1.0 / D))
            DV.wait(DV, DV.n)
            DV.emit(lambda dv: dv.tensor_mul(statc[:, sp + 5:sp + 6],
                                             statc[:, sp + 1:sp + 2], statc[:, sp + 1:sp + 2]))
            DV.wait(DV, DV.n)
            mvar = DV.emit(lambda dv: dv.tensor_sub(statc[:, sp + 3:sp + 4],
                                                    statc[:, sp + 3:sp + 4], statc[:, sp + 5:sp + 6]))
            AC.wait(DV, mvar)
            AC.emit(lambda ac: ac.activation(statc[:, sp + 4:sp + 5], statc[:, sp + 3:sp + 4],
                                             AF.Ln, bias=consts[:, 0:1]))
            AC.wait(AC, AC.n)
            mrs = AC.emit(lambda ac: ac.activation(statc[:, sp + 4:sp + 5], statc[:, sp + 4:sp + 5],
                                                   AF.Exp, scale=consts[:, 2:3]))
            return mvar, mrs

        for c in range(DC):
            GPW.emit(lambda gp, c=c: gp.dma_start(
                out=wup_bf[c * 128:(c + 1) * 128, :], in_=wupT[c * 128:(c + 1) * 128, :]))
        for c in range(DC):
            GPW.emit(lambda gp, c=c: gp.dma_start(
                out=wgate_bf[c * 128:(c + 1) * 128, :], in_=wgateT[c * 128:(c + 1) * 128, :]))
        m_gp_upgate = GPW.n

        m_sy0 = [None]
        for nm, (dst, src) in {
            "h128": (h128_s, h128in), "i128f": (i128f_s, i128fin), "i128b": (i128b_s, i128bin),
            "tms": (tms_s, tmsin), "mge": (mge_s, mgein), "riota": (riota_s, riotain),
            "grid4": (grid4_s, grid4in), "expo": (expo_s, expoin), "pmask": (pmask_s, pmaskin),
        }.items():
            SY.emit(lambda sy, d=dst, s=src: sy.dma_start(out=d[:, :], in_=s[:, :]))
        SY.emit(lambda sy: sy.dma_start(out=srow[0:1, 0:4], in_=lamlog[:, :]))
        SY.emit(lambda sy: sy.dma_start(out=srow[0:1, 8:12], in_=etain[:, :]))
        m_syc = SY.n

        stP2 = contextlib.ExitStack()
        _open_stacks.append(stP2)
        ptA = [stP2.enter_context(nc.psum_tensor(f"ptA{i}", [128, 1024], BF16))
               for i in range(2)]
        stP = contextlib.ExitStack()
        _open_stacks.append(stP)
        pstat = stP.enter_context(nc.psum_tensor("pstat", [128, 512], F32))
        pp2 = stP.enter_context(nc.psum_tensor("pp2", [128, 512], F32))

        for col, val in ((0, EPS), (1, 1.0), (2, -0.5), (3, -1.0)):
            DV.emit(lambda dv, c=col, v=val: dv.memset(consts[:, c:c + 1], v))
        DV.emit(lambda dv: dv.memset(onesr[:, :], 1.0))
        DV.emit(lambda dv: dv.memset(onesb[:, :], 1.0))
        DV.emit(lambda dv: dv.memset(carry[:, :], 0.0))
        DV.emit(lambda dv: dv.memset(carry_bf[:, :], 0.0))
        m_dvc = DV.n

        AC.wait(SY, m_syc)
        AC.wait(DV, m_dvc)
        m1 = AC.emit(lambda ac: ac.activation(srow[0:1, 0:4], srow[0:1, 0:4], AF.Exp,
                                              scale=consts[0:1, 3:4]))
        DV.wait(AC, m1)
        DV.emit(lambda dv: dv.tensor_scalar_add(srow[0:1, 0:4], srow[0:1, 0:4], 1.0))
        DV.wait(DV, DV.n)
        m2 = DV.emit(lambda dv: dv.reciprocal(srow[0:1, 0:4], srow[0:1, 0:4]))
        AC.wait(DV, m2)
        m3 = AC.emit(lambda ac: ac.activation(srow[0:1, 4:8], srow[0:1, 0:4], AF.Ln))
        DV.wait(AC, m3)
        m4 = DV.emit(lambda dv: dv.tensor_scalar_mul(srow[0:1, 12:16], srow[0:1, 4:8], 128.0))
        AC.wait(DV, m4)
        m5 = AC.emit(lambda ac: ac.activation(srow[0:1, 12:16], srow[0:1, 12:16], AF.Exp))

        PE.wait(AC, m5)
        PE.emit(lambda pe: pe.matmul(pstat[:, 0:4], onesr[0:1, :], srow[0:1, 4:8],
                                     start=True, stop=True))
        PE.emit(lambda pe: pe.matmul(pstat[:, 4:8], onesr[0:1, :], srow[0:1, 8:12],
                                     start=True, stop=True))
        for i, c0 in enumerate((0, 4, 8, 12)):
            PE.emit(lambda pe, i=i, c0=c0: pe.transpose(
                pstat[0:4, 8 + i:9 + i], srow[0:1, c0:c0 + 4], i128f_s[0:1, 0:1]))
        m6 = PE.n
        DV.wait(PE, m6)
        DV.emit(lambda dv: dv.tensor_copy(lnlam_b[:, :], pstat[:, 0:4]))
        DV.emit(lambda dv: dv.tensor_copy(eta_b[:, :], pstat[:, 4:8]))
        m7 = DV.emit(lambda dv: dv.tensor_copy(scol[0:4, 0:4], pstat[0:4, 8:12]))
        m_scalars = m7

        SYR.emit(lambda sy: sy.dma_start(out=rowload[:, :], in_=sudpin[:, :]))
        m8 = SYR.n
        PE.wait(SYR, m8)
        PE.wait(DV, m7)
        m9 = PE.emit(lambda pe: pe.transpose(pstat[:, 0:64], rowload[0:64, :], i128f_s[0:64, 0:64]))
        DV.wait(PE, m9)
        m10 = DV.emit(lambda dv: dv.tensor_scalar_mul(
            sudpT[:, :], pstat[:, 0:64], 1.0 / (SCALE * float(np.sqrt(J)))))
        SY.wait(DV, m10)
        SYR.emit(lambda sy: sy.dma_start(out=rowload[:, :], in_=wsupin[:, :]))
        m11 = SYR.n
        PE.wait(SYR, m11)
        PE.wait(DV, m10)
        m12 = PE.emit(lambda pe: pe.transpose(pstat[:, 64:128], rowload[0:64, :], i128f_s[0:64, 0:64]))
        DV.wait(PE, m12)
        m13 = DV.emit(lambda dv: dv.tensor_scalar_mul(wsupT[:, :], pstat[:, 64:128], SCALE))
        SY.wait(PE, m12)
        SYR.emit(lambda sy: sy.dma_start(out=rowload[:, :], in_=wsgin[:, :]))
        m14 = SYR.n
        PE.wait(SYR, m14)
        PE.wait(DV, m13)
        m15 = PE.emit(lambda pe: pe.transpose(pstat[:, 128:192], rowload[0:64, :], i128f_s[0:64, 0:64]))
        DV.wait(PE, m15)
        m16 = DV.emit(lambda dv: dv.tensor_scalar_mul(wsgT[:, :], pstat[:, 128:192], SCALE))

        DV.wait(DV, m_scalars)
        DV.emit(lambda dv: dv.memset(bmf[:, :], 0.0))
        for k in range(4):
            DV.wait(DV, DV.n)
            mk = DV.emit(lambda dv, k=k: dv.tensor_scalar(
                cscr[:, :], tms_s[:, :], lnlam_b[:, k:k + 1], None, ALU.mult))
            AC.wait(DV, mk)
            ma = AC.emit(lambda ac: ac.activation(cscr[:, :], cscr[:, :], AF.Exp))
            DV.wait(AC, ma)
            DV.emit(lambda dv, k=k: dv.tensor_scalar(
                cscr[:, :], cscr[:, :], eta_b[:, k:k + 1], None, ALU.mult))
            DV.wait(DV, DV.n)
            DV.emit(lambda dv: dv.tensor_add(bmf[:, :], bmf[:, :], cscr[:, :]))
        DV.wait(DV, DV.n)
        DV.emit(lambda dv: dv.tensor_mul(bmf[:, :], bmf[:, :], mge_s[:, :]))
        DV.wait(DV, DV.n)
        DV.emit(lambda dv: dv.tensor_add(bmf[:, :], bmf[:, :], i128f_s[:, :]))
        DV.wait(DV, DV.n)
        m_bmat = DV.emit(lambda dv: dv.tensor_copy(bmat[:, :], bmf[:, :]))

        DV.emit(lambda dv: dv.tensor_mul(w4f[:, :], riota_s[:, :], lnlam_b[:, :]))
        DV.wait(DV, DV.n)
        mcl = DV.n
        AC.wait(DV, mcl)
        ma = AC.emit(lambda ac: ac.activation(w4f[:, :], w4f[:, :], AF.Exp))
        DV.wait(AC, ma)
        m_cl = DV.emit(lambda dv: dv.tensor_copy(clcoef[:, :], w4f[:, :]))

        def coef4(dst_ap_fn, off):
            DV.wait(DV, DV.n)
            DV.emit(lambda dv: dv.tensor_scalar_add(a4t[:, :], grid4_s[:, :], float(off)))
            DV.wait(DV, DV.n)
            mx = DV.emit(lambda dv: dv.tensor_scalar(
                a4t[:, :], a4t[:, :], scol[0:4, 1:2], None, ALU.mult))
            AC.wait(DV, mx)
            my = AC.emit(lambda ac: ac.activation(a4t[:, :], a4t[:, :], AF.Exp))
            DV.wait(AC, my)
            return DV.emit(lambda dv: dv.tensor_scalar(
                dst_ap_fn(), a4t[:, :], scol[0:4, 2:3], None, ALU.mult))

        coef4(lambda: aloc[:, :], 1.0)
        for jj in range(TC):
            coef4(lambda jj=jj: am[:, jj * 128:(jj + 1) * 128], 128.0 * jj + 1.0)
        m_coefs = DV.n

        if kphase < 2:
            _cut()
            return nc
        m_prep_ld = []
        m_prep_dv = []
        m_prep_ac = []
        m_prep_pe = []
        for jp in range(PREV):
            b = jp % 2
            if jp >= 2:
                SY.wait(DV, m_prep_ac[jp - 2])
            SY.wait(DV, m_dvc)
            m_ld = SYG[b].emit(lambda sy, jp=jp, b=b: sy.dma_start(
                out=G[b][:, :], in_=xfull[jp * 128:(jp + 1) * 128, :]))
            m_prep_ld.append(m_ld)
            sp = 8 * (jp % 2)
            AC.wait(SYG[b], m_ld)
            if jp >= 2:
                AC.wait(DV, m_prep_dv[jp - 2])
                AC.wait(DV, m_prep_ac[jp - 2])
            edw = [(SYG[b], m_ld)]
            if jp == 0:
                edw.append((DV, m_coefs))
            mvar, mrs = emit_stats(lambda b=b: G[b][:, :], lambda: zbf[:, :], sp,
                                   extra_dv_wait=edw)
            DV.wait(AC, mrs)
            DV.emit(lambda dv, jp=jp: dv.tensor_scalar(
                w4f[:, :], lnlam_b[:, :], expo_s[:, jp:jp + 1], None, ALU.mult))
            DV.wait(DV, DV.n)
            mw = DV.n
            AC.wait(DV, mw)
            mwe = AC.emit(lambda ac: ac.activation(w4f[:, :], w4f[:, :], AF.Exp))
            DV.wait(AC, mwe)
            if jp >= 2:
                DV.wait(PE, m_prep_pe[jp - 2])
            DV.emit(lambda dv, sp=sp: dv.tensor_scalar(
                w4f[:, :], w4f[:, :], statc[:, sp + 4:sp + 5], None, ALU.mult))
            DV.wait(DV, DV.n)
            mw4 = DV.emit(lambda dv, jp=jp, b=b: dv.tensor_scalar(
                w4b[b][:, :], w4f[:, :], pmask_s[:, jp:jp + 1], None, ALU.mult))
            m_prep_dv.append(mw4)
            DV.wait(DV, mw4)
            if jp >= 2:
                DV.wait(PE, m_prep_pe[jp - 2])
            mcast = DV.emit(lambda dv, b=b, sp=sp: dv.tensor_scalar(
                Hh[b][:, :], G[b][:, :], statc[:, sp + 1:sp + 2], None, ALU.add))
            m_prep_ac.append(mcast)
            PE.wait(DV, mcast)

            def pmm(pe, jp=jp, b=b):
                last = None
                for c in range(DC):
                    last = pe.matmul(pp2[:, 4 * c:4 * c + 4],
                                     Hh[b][:, c * 128:(c + 1) * 128],
                                     w4b[b][:, :],
                                     start=(jp == 0 and c == 0),
                                     stop=(jp == PREV - 1 and c == DC - 1))
                return last
            m_pe = PE.emit(pmm)
            m_prep_pe.append(m_pe)
        DV.wait(PE, m_prep_pe[-1])
        m_pev = DV.emit(lambda dv: dv.tensor_copy(pprev_bf[:, :], pp2[:, 0:64]))

        if kphase < 3:
            _cut()
            return nc
        stA = contextlib.ExitStack()
        _open_stacks.append(stA)
        wzT_s = stA.enter_context(nc.sbuf_tensor("wzT_s", [128, DC * D], BF16))
        xlnT_s = stA.enter_context(nc.sbuf_tensor("xlnT_s", [128, DC * T], BF16))
        gam_b = stA.enter_context(nc.sbuf_tensor("gam_b", [128, D], BF16))
        bet_b = stA.enter_context(nc.sbuf_tensor("bet_b", [128, D], BF16))
        zbf = stA.enter_context(nc.sbuf_tensor("zbf", [128, D], BF16))
        hb = stA.enter_context(nc.sbuf_tensor("hb", [128, D], F32))
        Hh = [stA.enter_context(nc.sbuf_tensor(f"H{i}", [128, D], BF16)) for i in range(2)]

        for c in range(DC):
            GPZ.emit(lambda gp, c=c: gp.dma_start(
                out=wzT_s[:, c * D:(c + 1) * D], in_=wzT[c * 128:(c + 1) * 128, :]))
        GPB.emit(lambda gp: gp.dma_start(out=gam_b[0:1, :], in_=gammain[:, :]))
        GPB.emit(lambda gp: gp.dma_start(out=bet_b[0:1, :], in_=betain[:, :]))
        m_gp_wz = GPZ.n
        m_gb = GPB.n

        PE.wait(GPB, m_gb)
        PE.wait(DV, m16)
        for row_t in (gam_b, bet_b):
            for s in range(4):
                mmm = PE.emit(lambda pe, r=row_t, s=s: pe.matmul(
                    pstat[:, 0:512], onesb[0:1, :], r[0:1, s * 512:(s + 1) * 512],
                    start=True, stop=True))
                DV.wait(PE, mmm)
                mvv = DV.emit(lambda dv, r=row_t, s=s: dv.tensor_copy(
                    r[:, s * 512:(s + 1) * 512], pstat[:, 0:512]))
                PE.wait(DV, mvv)
        m_gbb = DV.n

        PE.wait(DV, m_pev)
        PE.wait(GPZ, m_gp_wz)
        for s in range(4):
            def mmm(pe, s=s):
                last = None
                for c in range(DC):
                    last = pe.matmul(pstat[0:4, 0:512],
                                     pprev_bf[:, 4 * c:4 * c + 4],
                                     wzT_s[:, c * D + s * 512: c * D + (s + 1) * 512],
                                     start=(c == 0), stop=(c == DC - 1))
                return last
            mp = PE.emit(mmm)
            DV.wait(PE, mp)
            md = DV.emit(lambda dv, s=s: dv.tensor_copy(
                marow[0:4, s * 512:(s + 1) * 512], pstat[0:4, 0:512]))
            PE.wait(DV, md)
        m_M = DV.n

        if kphase < 4:
            _cut()
            return nc
        _close_stack(stP)
        m_a0_ac = []
        m_prep_end_dv = DV.n
        m_prep_end_ac = AC.n
        m_prep_end_pe = PE.n
        slot_user = [0] * 2
        for j in range(TC):
            b = j % 2
            SY.wait(AC, m_prep_end_ac)
            SY.wait(DV, m_prep_end_dv)
            if j >= 2:
                SY.wait(AC, m_a0_ac[j - 2])
            m_ld = SYG[b].emit(lambda sy, j=j, b=b: sy.dma_start(
                out=G[b][:, :], in_=xown[j * 128:(j + 1) * 128, :]))
            DV.wait(SYG[b], m_ld)
            DV.emit(lambda dv, b=b: dv.reduce_sum(statc[:, 0:1], G[b][:, :], axis=AX.X))
            DV.wait(DV, DV.n)
            DV.emit(lambda dv: dv.tensor_scalar_mul(statc[:, 1:2], statc[:, 0:1], -1.0 / D))
            mdn = DV.n
            AC.wait(DV, mdn)
            msq = AC.emit(lambda ac, b=b: ac.activation(G2h[:, :], G[b][:, 0:1024], AF.Square))
            DV.wait(AC, msq)
            mr0 = DV.emit(lambda dv: dv.reduce_sum(statc[:, 2:3], G2h[:, :], axis=AX.X))
            AC.wait(DV, mr0)
            msq2 = AC.emit(lambda ac, b=b: ac.activation(G2h[:, :], G[b][:, 1024:2048], AF.Square))
            DV.wait(AC, msq2)
            DV.emit(lambda dv: dv.reduce_sum(statc[:, 6:7], G2h[:, :], axis=AX.X))
            DV.wait(DV, DV.n)
            DV.emit(lambda dv: dv.tensor_add(statc[:, 2:3], statc[:, 2:3], statc[:, 6:7]))
            DV.wait(DV, DV.n)
            DV.emit(lambda dv: dv.tensor_scalar_mul(statc[:, 3:4], statc[:, 2:3], 1.0 / D))
            DV.emit(lambda dv: dv.tensor_mul(statc[:, 5:6], statc[:, 1:2], statc[:, 1:2]))
            DV.wait(DV, DV.n)
            mvar = DV.emit(lambda dv: dv.tensor_sub(statc[:, 3:4], statc[:, 3:4], statc[:, 5:6]))
            AC.wait(DV, mvar)
            AC.emit(lambda ac: ac.activation(statc[:, 4:5], statc[:, 3:4], AF.Ln,
                                             bias=consts[:, 0:1]))
            AC.wait(AC, AC.n)
            mrs = AC.emit(lambda ac: ac.activation(statc[:, 4:5], statc[:, 4:5], AF.Exp,
                                                   scale=consts[:, 2:3]))
            if ka0 < 2:
                m_a0_ac.append(AC.n)
                try:
                    m_a0_pe.append(PE.n)
                except NameError:
                    m_a0_pe = [PE.n]
                continue
            DV.wait(AC, mrs)
            if j >= 2:
                DV.wait(PE, m_a0_pe[j - 2])
            else:
                DV.wait(PE, m_prep_end_pe)
            mnorm = DV.emit(lambda dv, b=b: dv.tensor_scalar(
                Hh[b][:, :], G[b][:, :], statc[:, 1:2], statc[:, 4:5], ALU.add, ALU.mult))
            if ka0 < 3:
                m_a0_ac.append(AC.n)
                try:
                    m_a0_pe.append(PE.n)
                except NameError:
                    m_a0_pe = [PE.n]
                continue
            PE.wait(DV, mnorm)
            for c in range(DC):
                slot = c % 2
                PE.wait(AC, slot_user[slot])
                mt = PE.emit(lambda pe, b=b, c=c, slot=slot: pe.transpose(
                    ptA[slot][:, 0:128],
                    Hh[b][:, c * 128:(c + 1) * 128], i128b_s[:, :]))
                AC.wait(PE, mt)
                mcp = AC.emit(lambda ac, c=c, j=j, slot=slot: ac.copy(
                    xlnT_s[:, c * T + j * 128: c * T + (j + 1) * 128],
                    ptA[slot][:, 0:128]))
                slot_user[slot] = mcp
            m_a0_ac.append(AC.n)
            try:
                m_a0_pe.append(PE.n)
            except NameError:
                m_a0_pe = [PE.n]

        m_a0_done_pe = PE.n
        m_a0_end_dv = DV.n

        if kphase < 5:
            _cut()
            return nc
        _close_stack(stP2)
        stAz = contextlib.ExitStack()
        _open_stacks.append(stAz)
        ph2 = [stAz.enter_context(nc.psum_tensor(f"ph{i}", [128, 1024], F32)) for i in range(2)]
        pz2 = [stAz.enter_context(nc.psum_tensor(f"pz{i}", [128, 1024], F32)) for i in range(2)]

        m_zev = [0, 0]
        m_hev = 0
        m_ccast = 0
        m_clmm = [0, 0]
        m_x2st = []
        m_pzev = [0, 0]
        m_phdv = [0, 0]
        first_ph0 = True
        for j in range(TC):
            b = j % 2
            for hh in range(2):
                par = (j * 2 + hh) % 2
                PE.wait(AC, m_a0_ac[j])
                PE.wait(GPZ, m_gp_wz)
                PE.wait(AC, m_pzev[par])
                PE.wait(DV, m_ccast)

                def zmm(pe, j=j, hh=hh, par=par):
                    last = None
                    for c in range(DC):
                        for s in range(2):
                            last = pe.matmul(
                                pz2[par][:, s * 512:(s + 1) * 512],
                                xlnT_s[:, c * T + j * 128: c * T + (j + 1) * 128],
                                wzT_s[:, c * D + hh * 1024 + s * 512: c * D + hh * 1024 + (s + 1) * 512],
                                start=(c == 0), stop=(c == DC - 1))
                    return last
                mz = PE.emit(zmm)
                AC.wait(PE, mz)
                AC.wait(PE, m_clmm[hh])
                m_zev[hh] = AC.emit(lambda ac, hh=hh, par=par: ac.copy(
                    zbf[:, hh * 1024:(hh + 1) * 1024], pz2[par][:, :]))
                m_pzev[par] = m_zev[hh]
                PE.wait(AC, m_zev[hh])
                PE.wait(DV, max(m_M, m_ccast, m_phdv[par]))
                if first_ph0 and par == 0:
                    PE.wait(AC, m_a0_ac[TC - 1])
                    first_ph0 = False

                def hmm(pe, j=j, hh=hh, par=par):
                    last = None
                    for s in range(2):
                        sl = slice(hh * 1024 + s * 512, hh * 1024 + (s + 1) * 512)
                        psl = slice(s * 512, (s + 1) * 512)
                        pe.matmul(ph2[par][:, psl], bmat[:, :], zbf[:, sl], start=True, stop=False)
                        pe.matmul(ph2[par][:, psl], aloc[:, :], carry_bf[0:4, sl], start=False, stop=False)
                        last = pe.matmul(ph2[par][:, psl], am[:, j * 128:(j + 1) * 128],
                                         marow[0:4, sl], start=False, stop=True)
                    return last
                mh = PE.emit(hmm)
                DV.wait(PE, mh)
                if j >= 1:
                    DV.wait(SYX2, m_x2st[j - 1])
                m_hev = DV.emit(lambda dv, hh=hh, par=par: dv.tensor_copy(
                    hb[:, hh * 1024:(hh + 1) * 1024], ph2[par][:, :]))
                m_phdv[par] = m_hev
                PE.wait(AC, m_zev[hh])

                def clmm(pe, hh=hh, par=par):
                    last = None
                    for s in range(2):
                        sl = slice(hh * 1024 + s * 512, hh * 1024 + (s + 1) * 512)
                        last = pe.matmul(pz2[par][0:4, s * 512:(s + 1) * 512], clcoef[:, :],
                                         zbf[:, sl], start=True, stop=True)
                    return last
                mcl2 = PE.emit(clmm)
                m_clmm[hh] = mcl2
                DV.wait(PE, mcl2)
                DV.emit(lambda dv, hh=hh, par=par: dv.scalar_tensor_tensor(
                    carry[0:4, hh * 1024:(hh + 1) * 1024],
                    carry[0:4, hh * 1024:(hh + 1) * 1024],
                    scol[0:4, 3:4], pz2[par][0:4, 0:1024], ALU.mult, ALU.add))
                DV.wait(DV, DV.n)
                m_ccast = DV.emit(lambda dv, hh=hh: dv.tensor_copy(
                    carry_bf[0:4, hh * 1024:(hh + 1) * 1024],
                    carry[0:4, hh * 1024:(hh + 1) * 1024]))
            sp = 8 * (j % 2)
            AC.wait(DV, m_hev)
            if j >= 2:
                AC.wait(DV, m_x2dv[j - 2])
            mvar, mrs = emit_stats(lambda: hb[:, :], lambda: Hh[0][:, :], sp)
            if j >= 2:
                SY.wait(DV, m_x2dv[j - 2])
            else:
                SY.wait(DV, m_a0_end_dv)
            SY.wait(DV, m_a0_end_dv)
            mxl = SYG[b].emit(lambda sy, j=j, b=b: sy.dma_start(
                out=G[b][:, :], in_=xown[j * 128:(j + 1) * 128, :]))
            DV.wait(SYG[b], mxl)
            DV.emit(lambda dv, b=b: dv.tensor_add(G[b][:, :], G[b][:, :], bet_b[:, :]))
            DV.wait(AC, mrs)
            DV.wait(DV, DV.n)
            DV.emit(lambda dv, sp=sp: dv.tensor_scalar(
                hb[:, :], hb[:, :], statc[:, sp + 1:sp + 2], statc[:, sp + 4:sp + 5],
                ALU.add, ALU.mult))
            DV.wait(DV, DV.n)
            DV.emit(lambda dv: dv.tensor_mul(hb[:, :], hb[:, :], gam_b[:, :]))
            DV.wait(DV, DV.n)
            mx2 = DV.emit(lambda dv, b=b: dv.tensor_add(hb[:, :], hb[:, :], G[b][:, :]))
            try:
                m_x2dv.append(mx2)
            except NameError:
                m_x2dv = [mx2]
            SY.wait(DV, mx2)
            mst = SYX2.emit(lambda sy, j=j: sy.dma_start(
                out=x2_dram[j * 128:(j + 1) * 128, :], in_=hb[:, :]))
            m_x2st.append(mst)

        if kphase < 6:
            _cut()
            return nc
        m_a1z_end = dict(pe=PE.n, dv=DV.n, ac=AC.n, sy=SY.n)
        _close_stack(stAz)
        _close_stack(stA)

        stB1 = contextlib.ExitStack()
        _open_stacks.append(stB1)
        uT_s = stB1.enter_context(nc.sbuf_tensor("uT_s", [128, DC * T], BF16))
        HhB = [stB1.enter_context(nc.sbuf_tensor(f"HB{i}", [128, D], BF16)) for i in range(2)]
        su_b = stB1.enter_context(nc.sbuf_tensor("su_b", [128, D], F32))
        wslab = [stB1.enter_context(nc.sbuf_tensor(f"wslab{i}", [128, 16 * 256], BF16)) for i in range(2)]
        gslab = [stB1.enter_context(nc.sbuf_tensor(f"gslab{i}", [128, 16 * 256], BF16)) for i in range(2)]
        btmp = stB1.enter_context(nc.sbuf_tensor("btmp", [128, 1024], BF16))
        silb2 = stB1.enter_context(nc.sbuf_tensor("silb2", [128, 1024], F32))
        vtmp2 = stB1.enter_context(nc.sbuf_tensor("vtmp2", [128, 1024], F32))
        vchB = [stB1.enter_context(nc.sbuf_tensor(f"vchB{i}", [128, T], BF16)) for i in range(2)]

        stB0 = contextlib.ExitStack()
        _open_stacks.append(stB0)
        ptB = [stB0.enter_context(nc.psum_tensor(f"ptB{i}", [128, 1024], BF16))
               for i in range(2)]
        pH0 = stB0.enter_context(nc.psum_tensor("pH0", [128, 512], F32))
        pH1 = stB0.enter_context(nc.psum_tensor("pH1", [128, 512], F32))

        SYR.emit(lambda sy: sy.dma_start(out=su_b[0:1, :], in_=suugin[:, :]))
        m_sul = SYR.n
        PE.wait(SYR, m_sul)
        su_scale = 1.0 / (SCALE * float(np.sqrt(D)))
        for s in range(4):
            mmm = PE.emit(lambda pe, s=s: pe.matmul(
                pH0[:, 0:512], onesr[0:1, :], su_b[0:1, s * 512:(s + 1) * 512],
                start=True, stop=True))
            DV.wait(PE, mmm)
            mvv = DV.emit(lambda dv, s=s: dv.tensor_scalar_mul(
                su_b[:, s * 512:(s + 1) * 512], pH0[:, 0:512], su_scale))
            PE.wait(DV, mvv)
        m_sub = DV.n

        m_b0_ac = []
        slotB = [0] * 2
        m_b0_pe_chunks = []
        for j in range(TC):
            b = j % 2
            SY.wait(SYX2, m_x2st[j])
            if j >= 2:
                SY.wait(DV, m_b0_norm[j - 2])
            else:
                SY.wait(DV, m_a1z_end["dv"])
            mld = SYG[b].emit(lambda sy, j=j, b=b: sy.dma_start(
                out=G[b][:, :], in_=x2_dram[j * 128:(j + 1) * 128, :]))
            AC.wait(SYG[b], mld)
            if j < 2:
                AC.wait(DV, m_a1z_end["dv"])
            sp = 8 * (j % 2)
            if j >= 2:
                AC.wait(DV, m_b0_norm[j - 2])
            mvar, mrs = emit_stats(lambda b=b: G[b][:, :], lambda: vchB[0][:, :], sp,
                                   extra_dv_wait=[(SYG[b], mld)])
            DV.wait(AC, mrs)
            DV.wait(DV, m_sub)
            if j >= 2:
                DV.wait(PE, m_b0_pe_chunks[j - 2])
            else:
                DV.wait(PE, m_a1z_end["pe"])
            for hx in range(2):
                DV.emit(lambda dv, b=b, hx=hx, sp=sp: dv.tensor_scalar(
                    G2h[:, :], G[b][:, hx * 1024:(hx + 1) * 1024],
                    statc[:, sp + 1:sp + 2], statc[:, sp + 4:sp + 5], ALU.add, ALU.mult))
                DV.wait(DV, DV.n)
                DV.emit(lambda dv, b=b, hx=hx: dv.tensor_mul(
                    HhB[b][:, hx * 1024:(hx + 1) * 1024], G2h[:, :],
                    su_b[:, hx * 1024:(hx + 1) * 1024]))
                DV.wait(DV, DV.n)
            mu_ = DV.n
            prev = mu_
            for st_i, dist in enumerate((1, 2, 4, 8)):
                blkw = dist * 128
                ngrp = DC // (2 * dist)

                def bfly(dv, b=b, dist=dist, blkw=blkw, ngrp=ngrp, which=0):
                    vv = HhB[b][:, :].rearrange("p (g two q) -> p g two q", two=2, q=blkw)
                    a = vv[:, :, 0, :]
                    bb = vv[:, :, 1, :]
                    t = btmp[:, 0:ngrp * blkw].rearrange("p (g q) -> p g q", q=blkw)
                    if which == 0:
                        return dv.tensor_add(t, a, bb)
                    elif which == 1:
                        return dv.tensor_sub(bb, a, bb)
                    else:
                        return dv.tensor_copy(a, t)
                DV.wait(DV, prev)
                DV.emit(lambda dv, f=bfly: f(dv, which=0))
                DV.wait(DV, DV.n)
                DV.emit(lambda dv, f=bfly: f(dv, which=1))
                DV.wait(DV, DV.n)
                prev = DV.emit(lambda dv, f=bfly: f(dv, which=2))
            m_b0n = prev
            try:
                m_b0_norm.append(m_b0n)
            except NameError:
                m_b0_norm = [m_b0n]
            PE.wait(DV, m_b0n)
            for c in range(DC):
                slot = c % 2
                PE.wait(AC, slotB[slot])
                mt = PE.emit(lambda pe, b=b, c=c, slot=slot: pe.transpose(
                    ptB[slot][:, 0:128],
                    HhB[b][:, c * 128:(c + 1) * 128], i128b_s[:, :]))
                AC.wait(PE, mt)
                mcp = AC.emit(lambda ac, c=c, j=j, slot=slot: ac.copy(
                    uT_s[:, c * T + j * 128: c * T + (j + 1) * 128],
                    ptB[slot][:, 0:128]))
                slotB[slot] = mcp
            m_b0_ac.append(AC.n)
            m_b0_pe_chunks.append(PE.n)
        m_h128 = []
        PE.wait(AC, m_b0_ac[-1])
        prev_ev = [0, 0]
        for c in range(DC):
            for s in range(4):
                par = (c * 4 + s) % 2
                ptgt = pH0 if par == 0 else pH1
                PE.wait(AC, prev_ev[par])
                mm_ = PE.emit(lambda pe, c=c, s=s, ptgt=ptgt: pe.matmul(
                    ptgt[:, 0:512], h128_s[:, :],
                    uT_s[:, c * T + s * 512: c * T + (s + 1) * 512],
                    start=True, stop=True))
                AC.wait(PE, mm_)
                mev = AC.emit(lambda ac, c=c, s=s, ptgt=ptgt: ac.copy(
                    uT_s[:, c * T + s * 512: c * T + (s + 1) * 512], ptgt[:, 0:512]))
                prev_ev[par] = mev
        m_uT = AC.n

        if kphase < 7:
            _cut()
            return nc
        _close_stack(stB0)
        stB1p = contextlib.ExitStack()
        _open_stacks.append(stB1p)
        pu = [stB1p.enter_context(nc.psum_tensor(f"pu{i}", [128, 1024], F32)) for i in range(2)]
        pg = [stB1p.enter_context(nc.psum_tensor(f"pg{i}", [128, 1024], F32)) for i in range(2)]

        def emit_slab(w, wbuf, gbuf):
            sq = SYW[w % 2]
            for c in range(DC):
                sq.emit(lambda sy, w=w, c=c, wbuf=wbuf: sy.dma_start(
                    out=wbuf[:, c * 256:(c + 1) * 256],
                    in_=wup_bf[c * 128:(c + 1) * 128, w * 256:(w + 1) * 256]))
            for c in range(DC):
                sq.emit(lambda sy, w=w, c=c, gbuf=gbuf: sy.dma_start(
                    out=gbuf[:, c * 256:(c + 1) * 256],
                    in_=wgate_bf[c * 128:(c + 1) * 128, w * 256:(w + 1) * 256]))
            return sq.n

        m_slab = {}
        m_win_pe = {}
        SY.wait(GPW, m_gp_upgate)
        m_slab[0] = emit_slab(0, wslab[0], gslab[0])
        m_vst = []
        m_vdv = {}
        m_stt_prev = 0
        for jc in range(JC):
            w = jc // 2
            if jc % 2 == 0 and w + 1 < JC // 2:
                if w >= 1:
                    SY.wait(PE, m_win_pe[w - 1])
                m_slab[w + 1] = emit_slab(w + 1, wslab[(w + 1) % 2], gslab[(w + 1) % 2])
            for hh in range(2):
                par = (jc * 2 + hh) % 2
                PE.wait(SYW[w % 2], m_slab[w])
                PE.wait(AC, m_uT)
                if (jc * 2 + hh) >= 2:
                    PE.wait(DV, m_vdv[(jc * 2 + hh) - 2])

                def upmm(pe, jc=jc, hh=hh, w=w, par=par):
                    jj = jc % 2
                    last = None
                    for c in range(DC):
                        for s in range(2):
                            last = pe.matmul(
                                pu[par][:, s * 512:(s + 1) * 512],
                                wslab[w % 2][:, c * 256 + jj * 128: c * 256 + (jj + 1) * 128],
                                uT_s[:, c * T + hh * 1024 + s * 512: c * T + hh * 1024 + (s + 1) * 512],
                                start=(c == 0), stop=(c == DC - 1))
                    for c in range(DC):
                        for s in range(2):
                            last = pe.matmul(
                                pg[par][:, s * 512:(s + 1) * 512],
                                gslab[w % 2][:, c * 256 + jj * 128: c * 256 + (jj + 1) * 128],
                                uT_s[:, c * T + hh * 1024 + s * 512: c * T + hh * 1024 + (s + 1) * 512],
                                start=(c == 0), stop=(c == DC - 1))
                    return last
                mup = PE.emit(upmm)
                AC.wait(PE, mup)
                AC.wait(DV, m_stt_prev)
                msil = AC.emit(lambda ac, jc=jc, par=par: ac.activation(
                    silb2[:, :], pg[par][:, :], AF.Silu, scale=wsgT[:, jc:jc + 1]))
                DV.wait(AC, msil)
                if jc >= 2:
                    DV.wait(SYV[jc % 2], m_vst[jc - 2])
                DV.wait(DV, DV.n)
                m_stt_prev = DV.emit(lambda dv, jc=jc, par=par: dv.scalar_tensor_tensor(
                    vtmp2[:, :], pu[par][:, :], wsupT[:, jc:jc + 1], silb2[:, :],
                    ALU.mult, ALU.mult))
                DV.wait(DV, DV.n)
                mvd = DV.emit(lambda dv, jc=jc, hh=hh: dv.tensor_scalar(
                    vchB[jc % 2][:, hh * 1024:(hh + 1) * 1024], vtmp2[:, :],
                    sudpT[:, jc:jc + 1], None, ALU.mult))
                m_vdv[jc * 2 + hh] = mvd
            SY.wait(DV, m_vdv[jc * 2 + 1])
            mst = SYV[jc % 2].emit(lambda sy, jc=jc: sy.dma_start(
                out=v_dram[jc * 128:(jc + 1) * 128, :], in_=vchB[jc % 2][:, :]))
            m_vst.append(mst)
            if jc % 2 == 1:
                m_win_pe[jc // 2] = PE.n
        m_b1_end = dict(pe=PE.n, dv=DV.n, ac=AC.n, sy=SY.n)

        if kphase < 8:
            _cut()
            return nc
        _close_stack(stB1p)
        _close_stack(stB1)
        stB2 = contextlib.ExitStack()
        _open_stacks.append(stB2)
        vq2 = [stB2.enter_context(nc.sbuf_tensor(f"vq{i}", [128, JC * 256], BF16))
               for i in range(2)]
        btmp2 = stB2.enter_context(nc.sbuf_tensor("btmp2", [128, 32 * 256], BF16))
        pq = [stB2.enter_context(nc.psum_tensor(f"pq{i}", [128, 512], F32)) for i in range(2)]

        m_v2st = {}
        m_bf_end = {}
        prev_evq = [0, 0]
        for e in range(8):
            p = e % 2
            if e == 0:
                SY.wait(SYV[0], SYV[0].n)
                SY.wait(SYV[1], SYV[1].n)
            if e >= 2:
                SY.wait(SYV2[p], m_v2st[e - 2])
            for cg in range(8):
                def ldq(sy, e=e, cg=cg, p=p):
                    src_ap = v_dram[cg * 1024:(cg + 1) * 1024, e * 256:(e + 1) * 256]
                    src_ap = src_ap.rearrange("(c q) t -> q c t", q=128)
                    dst_ap = vq2[p][:, cg * 8 * 256:(cg + 1) * 8 * 256].rearrange(
                        "q (c t) -> q c t", t=256)
                    return sy.dma_start(out=dst_ap, in_=src_ap)
                SYVQ[p].emit(ldq)
            mldq = SYVQ[p].n
            for c in range(JC):
                par = c % 2
                PE.wait(SYVQ[p], mldq)
                PE.wait(AC, prev_evq[par])
                if e >= 2:
                    PE.wait(DV, m_bf_end[e - 2])
                mmq = PE.emit(lambda pe, c=c, par=par, p=p: pe.matmul(
                    pq[par][:, 0:256], h128_s[:, :], vq2[p][:, c * 256:(c + 1) * 256],
                    start=True, stop=True))
                AC.wait(PE, mmq)
                mevq = AC.emit(lambda ac, c=c, par=par, p=p: ac.copy(
                    vq2[p][:, c * 256:(c + 1) * 256], pq[par][:, 0:256]))
                prev_evq[par] = mevq
            m_h128q = AC.n
            DV.wait(AC, m_h128q)
            prev = DV.n
            for dist in (1, 2, 4, 8, 16, 32):
                blkw = dist * 256
                ngrp = JC // (2 * dist)

                def bfq(dv, dist=dist, blkw=blkw, ngrp=ngrp, p=p, which=0):
                    vv = vq2[p][:, :].rearrange("q (g two q2) -> q g two q2", two=2, q2=blkw)
                    a = vv[:, :, 0, :]
                    bb = vv[:, :, 1, :]
                    t = btmp2[:, 0:ngrp * blkw].rearrange("q (g q2) -> q g q2", q2=blkw)
                    if which == 0:
                        return dv.tensor_add(t, a, bb)
                    elif which == 1:
                        return dv.tensor_sub(bb, a, bb)
                    else:
                        return dv.tensor_copy(a, t)
                DV.wait(DV, prev)
                DV.emit(lambda dv, f=bfq: f(dv, which=0))
                DV.wait(DV, DV.n)
                DV.emit(lambda dv, f=bfq: f(dv, which=1))
                DV.wait(DV, DV.n)
                prev = DV.emit(lambda dv, f=bfq: f(dv, which=2))
            m_bf_end[e] = prev
            SY.wait(DV, prev)
            for cg in range(8):
                def stq(sy, e=e, cg=cg, p=p):
                    dst_ap = v2_dram[cg * 1024:(cg + 1) * 1024, e * 256:(e + 1) * 256]
                    dst_ap = dst_ap.rearrange("(c q) t -> q c t", q=128)
                    src_ap = vq2[p][:, cg * 8 * 256:(cg + 1) * 8 * 256].rearrange(
                        "q (c t) -> q c t", t=256)
                    return sy.dma_start(out=dst_ap, in_=src_ap)
                SYV2[p].emit(stq)
            m_v2st[e] = SYV2[p].n
        m_b2_end_sy = (SYV2[0].n, SYV2[1].n)
        m_b2_end_pe = PE.n
        m_b2_end_dv = DV.n

        _close_stack(stB2)
        stB3 = contextlib.ExitStack()
        _open_stacks.append(stB3)
        accs = [stB3.enter_context(nc.sbuf_tensor(f"acc{i}", [128, TC * 512], F32))
                for i in range(2)]
        v2sl = [stB3.enter_context(nc.sbuf_tensor(f"v2sl{i}", [128, 8 * T], BF16))
                for i in range(2)]
        wdsl = [stB3.enter_context(nc.sbuf_tensor(f"wdsl{i}", [128, 8 * 512], BF16))
                for i in range(2)]
        ep = [stB3.enter_context(nc.sbuf_tensor(f"ep{i}", [128, 512], F32)) for i in range(2)]
        wsdp_b = stB3.enter_context(nc.sbuf_tensor("wsdp_b", [128, D], BF16))
        pd = [stB3.enter_context(nc.psum_tensor(f"pd{i}", [128, 512], F32)) for i in range(4)]
        pep = pd[3]

        GP.wait(PE, m_b2_end_pe)
        GP.wait(DV, m_b2_end_dv)
        GP.wait(SYV2[0], m_b2_end_sy[0])
        GP.wait(SYV2[1], m_b2_end_sy[1])
        GPB.emit(lambda gp: gp.dma_start(out=wsdp_b[0:1, :], in_=wsdpin[:, :]))
        m_wsl = GPB.n
        PE.wait(GPB, m_wsl)
        PE.wait(SYV2[0], m_b2_end_sy[0])
        PE.wait(SYV2[1], m_b2_end_sy[1])
        for s in range(4):
            mmm = PE.emit(lambda pe, s=s: pe.matmul(
                pd[0][:, :], onesb[0:1, :], wsdp_b[0:1, s * 512:(s + 1) * 512],
                start=True, stop=True))
            DV.wait(PE, mmm)
            mvv = DV.emit(lambda dv, s=s: dv.tensor_scalar_mul(
                wsdp_b[:, s * 512:(s + 1) * 512], pd[0][:, :], SCALE))
            PE.wait(DV, mvv)
        m_wsb = DV.n

        m_yst = []
        m_padd = {}
        m_sg_pe = {}
        m_acc_ep = {}
        DV.wait(SYV2[0], m_b2_end_sy[0])
        DV.wait(SYV2[1], m_b2_end_sy[1])
        first = True

        def emit_epilogue(ihq):
            ap = ihq % 2
            for tch in range(TC):
                b = tch % 2
                if SYY[b].n > 0:
                    SY.wait(SYY[b], SYY[b].n)
                if len(m_yst) >= 2:
                    SY.wait(DV, m_ep_dv[ihq * TC + tch - 2])
                mxl = SYE[b].emit(lambda sy, tch=tch, ihq=ihq, b=b: sy.dma_start(
                    out=ep[b][:, :],
                    in_=x2_dram[tch * 128:(tch + 1) * 128, ihq * 512:(ihq + 1) * 512]))
                DV.wait(SYE[b], mxl)
                DV.wait(DV, DV.n)
                DV.emit(lambda dv, tch=tch, ihq=ihq, ap=ap: dv.tensor_mul(
                    pep[:, :], accs[ap][:, tch * 512:(tch + 1) * 512],
                    wsdp_b[:, ihq * 512:(ihq + 1) * 512]))
                DV.wait(DV, DV.n)
                mepd = DV.emit(lambda dv, b=b: dv.tensor_add(
                    ep[b][:, :], ep[b][:, :], pep[:, :]))
                m_ep_dv.append(mepd)
                SY.wait(DV, mepd)
                myst = SYY[b].emit(lambda sy, tch=tch, ihq=ihq, b=b: sy.dma_start(
                    out=yout[tch * 128:(tch + 1) * 128, ihq * 512:(ihq + 1) * 512],
                    in_=ep[b][:, :]))
                m_yst.append(myst)
            m_acc_ep[ap] = DV.n

        m_ep_dv = []
        for ihq in range(4):
            ap = ihq % 2
            m_accz = DV.emit(lambda dv, ap=ap: dv.memset(accs[ap][:, :], 0.0))
            m_last_add = {}
            for sg in range(8):
                sp_ = (ihq * 8 + sg) % 2
                if first:
                    SY.wait(SYV2[0], m_b2_end_sy[0])
                    SY.wait(SYV2[1], m_b2_end_sy[1])
                    GP.wait(PE, m_b2_end_pe)
                    GP.wait(DV, m_b2_end_dv)
                    GP.wait(SYV2[0], m_b2_end_sy[0])
                    GP.wait(SYV2[1], m_b2_end_sy[1])
                    first = False
                else:
                    SY.wait(PE, m_sg_pe.get(sp_, 0))
                    GP.wait(PE, m_sg_pe.get(sp_, 0))
                for jj in range(8):
                    jcg = sg * 8 + jj
                    SYSL2[sp_].emit(lambda sy, jcg=jcg, jj=jj, sp_=sp_: sy.dma_start(
                        out=v2sl[sp_][:, jj * T:(jj + 1) * T],
                        in_=v2_dram[jcg * 128:(jcg + 1) * 128, :]))
                mv2 = SYSL2[sp_].n
                for jj in range(8):
                    jcg = sg * 8 + jj
                    GPD2[sp_].emit(lambda gp, jcg=jcg, jj=jj, ihq=ihq, sp_=sp_: gp.dma_start(
                        out=wdsl[sp_][:, jj * 512:(jj + 1) * 512],
                        in_=wdnT[jcg * 128:(jcg + 1) * 128, ihq * 512:(ihq + 1) * 512]))
                mwd = GPD2[sp_].n
                PE.wait(SYSL2[sp_], mv2)
                PE.wait(GPD2[sp_], mwd)
                PE.wait(DV, m_accz)
                for tch in range(TC):
                    par = tch % 3
                    PE.wait(DV, m_padd.get(par, 0))

                    def dmm(pe, tch=tch, par=par, sp_=sp_):
                        last = None
                        for jj in range(8):
                            last = pe.matmul(
                                pd[par][:, :],
                                v2sl[sp_][:, jj * T + tch * 128: jj * T + (tch + 1) * 128],
                                wdsl[sp_][:, jj * 512:(jj + 1) * 512],
                                start=(jj == 0), stop=(jj == 7))
                        return last
                    mdm = PE.emit(dmm)
                    DV.wait(PE, mdm)
                    madd = DV.emit(lambda dv, tch=tch, par=par, ap=ap: dv.tensor_add(
                        accs[ap][:, tch * 512:(tch + 1) * 512],
                        accs[ap][:, tch * 512:(tch + 1) * 512], pd[par][:, :]))
                    m_padd[par] = madd
                    m_last_add[tch] = madd
                m_sg_pe[sp_] = PE.n
                if sg == 2 and ihq > 0:
                    emit_epilogue(ihq - 1)
            m_q_adds = DV.n
        emit_epilogue(3)
        _close_stack(stB3)

        SY.wait(SYY[0], SYY[0].n)
        SY.wait(SYY[1], SYY[1].n)
        SY.wait(SY, SY.n)
        GP.wait(GP, GP.n)

        _flush_all(blk, PE, DV, AC, SY, GP)

    return nc


def _hadamard(n):
    h = np.array([[1.0]], dtype=np.float64)
    while h.shape[0] < n:
        h = np.block([[h, h], [h, -h]])
    return h


_NC_CACHE = [None]


def kernel(**inputs):
    import ml_dtypes
    x = np.asarray(inputs["L0_attn_x"])
    W = np.asarray(inputs["L0_attn_W"])
    lam_logit = np.asarray(inputs["L0_attn_lam_logit"]).reshape(1, 4)
    eta = np.asarray(inputs["L0_attn_eta"]).reshape(1, 4)
    gamma = np.asarray(inputs["L0_attn_gamma"]).reshape(1, D)
    beta = np.asarray(inputs["L0_attn_beta"]).reshape(1, D)
    su_ug = np.asarray(inputs["L0_ffn_SU_ug"]).reshape(1, D)
    su_dp = np.asarray(inputs["L0_ffn_SU_dp"]).reshape(64, 128)
    ws_ug = np.asarray(inputs["L0_ffn_Wscale_ug"])
    ws_dp = np.asarray(inputs["L0_ffn_Wscale_dp"]).reshape(1, D)
    W_up = np.asarray(inputs["L0_ffn_W_up"])
    W_gate = np.asarray(inputs["L0_ffn_W_gate"])
    W_down = np.asarray(inputs["L0_ffn_W_down"])

    xf = np.ascontiguousarray(x.reshape(NCORE * T, D))
    xprev = np.ascontiguousarray(xf[:PREV * 128, :])
    wzT = np.ascontiguousarray(W.T)
    wupT = np.ascontiguousarray(W_up.T)
    wgateT = np.ascontiguousarray(W_gate.T)
    wdnT = np.ascontiguousarray(W_down.T)

    h128 = (_hadamard(128)).astype(ml_dtypes.bfloat16)
    i128b = np.eye(128, dtype=np.float32).astype(ml_dtypes.bfloat16)
    i128f = np.eye(128, dtype=np.float32)
    ti = np.arange(128, dtype=np.float32)
    tms = (ti[None, :] - ti[:, None]).astype(np.float32)
    mge = (ti[None, :] >= ti[:, None]).astype(np.float32)
    riota = np.repeat((127.0 - ti)[:, None], 4, axis=1).astype(np.float32)
    grid4 = np.repeat(ti[None, :], 4, axis=0).astype(np.float32)

    common = {
        "wzT": wzT, "wupT": wupT, "wgateT": wgateT, "wdnT": wdnT,
        "lamlog": lam_logit.astype(np.float32), "etain": eta.astype(np.float32),
        "gammain": gamma.astype(np.float32), "betain": beta.astype(np.float32),
        "suugin": su_ug.astype(np.float32), "sudpin": su_dp.astype(np.float32),
        "wsupin": np.ascontiguousarray(ws_ug[:J].reshape(64, 128)).astype(np.float32),
        "wsgin": np.ascontiguousarray(ws_ug[J:].reshape(64, 128)).astype(np.float32),
        "wsdpin": ws_dp.astype(np.float32),
        "h128in": h128, "i128fin": i128f, "i128bin": i128b,
        "tmsin": tms, "mgein": mge, "riotain": riota, "grid4in": grid4,
        "xfull": xprev,
    }
    in_maps = []
    for c in range(NCORE):
        tau = np.arange(128, dtype=np.float64)[:, None]
        jj = np.arange(PREV, dtype=np.float64)[None, :]
        expo = (2048.0 * c - 1.0) - 128.0 * jj - tau
        live = (jj < 16 * c) & np.ones((128, 1), bool)
        expo = np.where(live, expo, 1.0e6)
        pmask = live.astype(np.float32)
        m = dict(common)
        m["xown"] = np.ascontiguousarray(xf[c * T:(c + 1) * T, :])
        m["expoin"] = expo.astype(np.float32)
        m["pmaskin"] = np.ascontiguousarray(pmask)
        in_maps.append(m)

    if _NC_CACHE[0] is None:
        _NC_CACHE[0] = build_nc()
    nc = _NC_CACHE[0]
    res = run_bass_kernel_spmd(nc, in_maps, core_ids=list(range(NCORE)))
    out = np.concatenate([np.asarray(r["yout"]) for r in res.results], axis=0)
    return out.reshape(1, NCORE * T, D).astype(np.float32)


# version 07
# speedup vs baseline: 1.2179x; 1.0393x over previous
import numpy as np
import contextlib
import concourse.bass as bass
import concourse.mybir as mybir
from concourse.bass_utils import run_bass_kernel_spmd

F32 = mybir.dt.float32
BF16 = mybir.dt.bfloat16
AF = mybir.ActivationFunctionType
ALU = mybir.AluOpType
AX = mybir.AxisListType

NCORE = 8
T = 2048
D = 2048
J = 8192
TC = T // 128
DC = D // 128
JC = J // 128
PREV = (NCORE - 1) * TC
EPS = 1e-5
SCALE = 64.0


class Q:

    def __init__(self, name, dma=False):
        self.name = name
        self.dma = dma
        self.sem = None
        self.ops = []
        self.n = 0

    def emit(self, f, inc=None):
        if inc is None:
            inc = 16 if self.dma else 1
        sem = lambda: self.sem

        def op(eng, f=f, inc=inc):
            ins = f(eng)
            ins.then_inc(self.sem, inc)
        self.ops.append(op)
        self.n += inc
        return self.n

    def raw(self, f):
        self.ops.append(lambda eng, f=f: f(eng))

    def wait(self, other, val):
        if val > 0:
            self.ops.append(lambda eng, o=other, v=val: eng.wait_ge(o.sem, v))


class _CutDone(Exception):
    def __init__(self, env):
        self.env = env


def _flush_all(blk, PE, DV, AC, SY, GP):
    def flush(q, eng):
        q.eng = eng
        for op in q.ops:
            op(eng)
    blk.tensor(lambda eng: flush(PE, eng))
    blk.vector(lambda eng: flush(DV, eng))
    blk.scalar(lambda eng: flush(AC, eng))
    blk.sync(lambda eng: flush(SY, eng))
    blk.gpsimd(lambda eng: flush(GP, eng))


class SubQ:

    def __init__(self, parent):
        self.parent = parent
        self.sem = None
        self.n = 0

    def emit(self, f):
        def op(eng, f=f):
            f(eng).then_inc(self.sem, 16)
        self.parent.ops.append(op)
        self.n += 16
        return self.n


def build_nc():
    import os
    kphase = int(os.environ.get("KPHASE", "99"))
    ka0 = int(os.environ.get("KA0", "99"))
    nc = bass.Bass("TRN2", target_bir_lowering=False)

    xfull = nc.dram_tensor("xfull", [PREV * 128, D], F32, kind="ExternalInput")
    xown = nc.dram_tensor("xown", [T, D], F32, kind="ExternalInput")
    wzT = nc.dram_tensor("wzT", [D, D], F32, kind="ExternalInput")
    wupT = nc.dram_tensor("wupT", [D, J], F32, kind="ExternalInput")
    wgateT = nc.dram_tensor("wgateT", [D, J], F32, kind="ExternalInput")
    wdnT = nc.dram_tensor("wdnT", [J, D], F32, kind="ExternalInput")
    lamlog = nc.dram_tensor("lamlog", [1, 4], F32, kind="ExternalInput")
    etain = nc.dram_tensor("etain", [1, 4], F32, kind="ExternalInput")
    gammain = nc.dram_tensor("gammain", [1, D], F32, kind="ExternalInput")
    betain = nc.dram_tensor("betain", [1, D], F32, kind="ExternalInput")
    suugin = nc.dram_tensor("suugin", [1, D], F32, kind="ExternalInput")
    sudpin = nc.dram_tensor("sudpin", [64, 128], F32, kind="ExternalInput")
    wsupin = nc.dram_tensor("wsupin", [64, 128], F32, kind="ExternalInput")
    wsgin = nc.dram_tensor("wsgin", [64, 128], F32, kind="ExternalInput")
    wsdpin = nc.dram_tensor("wsdpin", [1, D], F32, kind="ExternalInput")
    h128in = nc.dram_tensor("h128in", [128, 128], BF16, kind="ExternalInput")
    i128fin = nc.dram_tensor("i128fin", [128, 128], F32, kind="ExternalInput")
    i128bin = nc.dram_tensor("i128bin", [128, 128], BF16, kind="ExternalInput")
    tmsin = nc.dram_tensor("tmsin", [128, 128], F32, kind="ExternalInput")
    mgein = nc.dram_tensor("mgein", [128, 128], F32, kind="ExternalInput")
    riotain = nc.dram_tensor("riotain", [128, 4], F32, kind="ExternalInput")
    grid4in = nc.dram_tensor("grid4in", [4, 128], F32, kind="ExternalInput")
    expoin = nc.dram_tensor("expoin", [128, PREV], F32, kind="ExternalInput")
    pmaskin = nc.dram_tensor("pmaskin", [128, PREV], F32, kind="ExternalInput")

    yout = nc.dram_tensor("yout", [T, D], F32, kind="ExternalOutput")
    x2_dram = nc.dram_tensor("x2_dram", [T, D], F32, kind="Internal")
    wup_bf = nc.dram_tensor("wup_bf", [D, J], BF16, kind="Internal")
    wgate_bf = nc.dram_tensor("wgate_bf", [D, J], BF16, kind="Internal")
    v_dram = nc.dram_tensor("v_dram", [J, T], BF16, kind="Internal")
    v2_dram = nc.dram_tensor("v2_dram", [J, T], BF16, kind="Internal")

    PE, DV, AC = Q("pe"), Q("dv"), Q("ac")
    SY, GP = Q("sy", dma=True), Q("gp", dma=True)
    QS = (PE, DV, AC, SY, GP)

    SYG = [SubQ(SY), SubQ(SY)]
    SYX2 = SubQ(SY)
    SYW = [SubQ(SY), SubQ(SY)]
    SYV = [SubQ(SY), SubQ(SY)]
    SYVQ = [SubQ(SY), SubQ(SY)]
    SYV2 = [SubQ(SY), SubQ(SY)]
    SYSL2 = [SubQ(SY), SubQ(SY)]
    GPD2 = [SubQ(GP), SubQ(GP)]
    SYE = [SubQ(SY), SubQ(SY)]
    SYY = [SubQ(SY), SubQ(SY)]
    SYR = SubQ(SY)
    GPW = SubQ(GP)
    GPZ = SubQ(GP)
    GPB = SubQ(GP)
    GPX3 = [SubQ(GP) for _ in range(3)]
    SUBS = [SYG[0], SYG[1], SYX2, SYW[0], SYW[1], SYV[0], SYV[1],
            SYVQ[0], SYVQ[1], SYV2[0], SYV2[1],
            SYSL2[0], SYSL2[1], GPD2[0], GPD2[1],
            SYE[0], SYE[1], SYY[0], SYY[1], SYR, GPW, GPZ, GPB,
            GPX3[0], GPX3[1], GPX3[2]]

    with contextlib.ExitStack() as top:
        for q, nm in zip(QS, ("s_pe", "s_dv", "s_ac", "s_sy", "s_gp")):
            q.sem = top.enter_context(nc.semaphore(nm))
        for i, sq in enumerate(SUBS):
            sq.sem = top.enter_context(nc.semaphore(f"s_sub{i}"))

        def sb(name, shape, dt=F32):
            return top.enter_context(nc.sbuf_tensor(name, shape, dt))

        h128_s = sb("h128_s", [128, 128], BF16)
        i128f_s = sb("i128f_s", [128, 128], F32)
        i128b_s = sb("i128b_s", [128, 128], BF16)
        tms_s = sb("tms_s", [128, 128], F32)
        mge_s = sb("mge_s", [128, 128], F32)
        riota_s = sb("riota_s", [128, 4], F32)
        grid4_s = sb("grid4_s", [4, 128], F32)
        consts = sb("consts", [128, 8], F32)
        onesr = sb("onesr", [1, 128], F32)
        srow = sb("srow", [1, 16], F32)
        scol = sb("scol", [4, 4], F32)
        lnlam_b = sb("lnlam_b", [128, 4], F32)
        eta_b = sb("eta_b", [128, 4], F32)
        bmat = sb("bmat", [128, 128], BF16)
        bmf = sb("bmf", [128, 128], F32)
        cscr = sb("cscr", [128, 128], F32)
        clcoef = sb("clcoef", [128, 4], BF16)
        aloc = sb("aloc", [4, 128], BF16)
        am = sb("am", [4, TC * 128], BF16)
        a4t = sb("a4t", [4, 128], F32)
        rowload = sb("rowload", [64, 128], F32)
        sudpT = sb("sudpT", [128, 64], F32)
        wsupT = sb("wsupT", [128, 64], F32)
        wsgT = sb("wsgT", [128, 64], F32)
        marow = sb("marow", [4, D], BF16)
        carry = sb("carry", [4, D], F32)
        carry_bf = sb("carry_bf", [4, D], BF16)
        pprev_bf = sb("pprev_bf", [128, 64], BF16)
        expo_s = sb("expo_s", [128, PREV], F32)
        pmask_s = sb("pmask_s", [128, PREV], F32)
        statc = sb("statc", [128, 16], F32)
        w4f = sb("w4f", [128, 4], F32)
        w4b = [sb(f"w4b{i}", [128, 4], BF16) for i in range(2)]
        G = [sb(f"G{i}", [128, D], F32) for i in range(2)]
        G2h = sb("G2h", [128, 1024], F32)
        onesb = sb("onesb", [1, 128], BF16)

        blk = top.enter_context(nc.Block())
        _open_stacks = []

        def _close_stack(s):
            s.close()
            _open_stacks.remove(s)

        def _cut():
            for s in reversed(_open_stacks):
                s.close()
            _open_stacks.clear()
            _flush_all(blk, PE, DV, AC, SY, GP)


        def emit_stats(src_tile_fn, sink_fn, sp, extra_dv_wait=None):
            if extra_dv_wait is not None:
                for w_ in extra_dv_wait:
                    DV.wait(*w_)
            DV.emit(lambda dv: dv.reduce_sum(statc[:, sp:sp + 1], src_tile_fn(), axis=AX.X))
            DV.wait(DV, DV.n)
            DV.emit(lambda dv: dv.tensor_scalar_mul(statc[:, sp + 1:sp + 2],
                                                    statc[:, sp:sp + 1], -1.0 / D))
            AC.wait(AC, AC.n)
            mss = AC.emit(lambda ac: ac.activation(sink_fn(), src_tile_fn(), AF.Square,
                                                   accum_out=statc[:, sp + 2:sp + 3]))
            DV.wait(AC, mss)
            DV.emit(lambda dv: dv.tensor_scalar_mul(statc[:, sp + 3:sp + 4],
                                                    statc[:, sp + 2:sp + 3], 1.0 / D))
            DV.wait(DV, DV.n)
            DV.emit(lambda dv: dv.tensor_mul(statc[:, sp + 5:sp + 6],
                                             statc[:, sp + 1:sp + 2], statc[:, sp + 1:sp + 2]))
            DV.wait(DV, DV.n)
            mvar = DV.emit(lambda dv: dv.tensor_sub(statc[:, sp + 3:sp + 4],
                                                    statc[:, sp + 3:sp + 4], statc[:, sp + 5:sp + 6]))
            AC.wait(DV, mvar)
            AC.emit(lambda ac: ac.activation(statc[:, sp + 4:sp + 5], statc[:, sp + 3:sp + 4],
                                             AF.Ln, bias=consts[:, 0:1]))
            AC.wait(AC, AC.n)
            mrs = AC.emit(lambda ac: ac.activation(statc[:, sp + 4:sp + 5], statc[:, sp + 4:sp + 5],
                                                   AF.Exp, scale=consts[:, 2:3]))
            return mvar, mrs


        m_sy0 = [None]
        for nm, (dst, src) in {
            "h128": (h128_s, h128in), "i128f": (i128f_s, i128fin), "i128b": (i128b_s, i128bin),
            "tms": (tms_s, tmsin), "mge": (mge_s, mgein), "riota": (riota_s, riotain),
            "grid4": (grid4_s, grid4in), "expo": (expo_s, expoin), "pmask": (pmask_s, pmaskin),
        }.items():
            SY.emit(lambda sy, d=dst, s=src: sy.dma_start(out=d[:, :], in_=s[:, :]))
        SY.emit(lambda sy: sy.dma_start(out=srow[0:1, 0:4], in_=lamlog[:, :]))
        SY.emit(lambda sy: sy.dma_start(out=srow[0:1, 8:12], in_=etain[:, :]))
        m_syc = SY.n

        stP2 = contextlib.ExitStack()
        _open_stacks.append(stP2)
        ptA = [stP2.enter_context(nc.psum_tensor(f"ptA{i}", [128, 1024], BF16))
               for i in range(2)]
        stP = contextlib.ExitStack()
        _open_stacks.append(stP)
        pstat = stP.enter_context(nc.psum_tensor("pstat", [128, 512], F32))
        pp2 = stP.enter_context(nc.psum_tensor("pp2", [128, 512], F32))

        for col, val in ((0, EPS), (1, 1.0), (2, -0.5), (3, -1.0)):
            DV.emit(lambda dv, c=col, v=val: dv.memset(consts[:, c:c + 1], v))
        DV.emit(lambda dv: dv.memset(onesr[:, :], 1.0))
        DV.emit(lambda dv: dv.memset(onesb[:, :], 1.0))
        DV.emit(lambda dv: dv.memset(carry[:, :], 0.0))
        DV.emit(lambda dv: dv.memset(carry_bf[:, :], 0.0))
        m_dvc = DV.n

        AC.wait(SY, m_syc)
        AC.wait(DV, m_dvc)
        m1 = AC.emit(lambda ac: ac.activation(srow[0:1, 0:4], srow[0:1, 0:4], AF.Exp,
                                              scale=consts[0:1, 3:4]))
        DV.wait(AC, m1)
        DV.emit(lambda dv: dv.tensor_scalar_add(srow[0:1, 0:4], srow[0:1, 0:4], 1.0))
        DV.wait(DV, DV.n)
        m2 = DV.emit(lambda dv: dv.reciprocal(srow[0:1, 0:4], srow[0:1, 0:4]))
        AC.wait(DV, m2)
        m3 = AC.emit(lambda ac: ac.activation(srow[0:1, 4:8], srow[0:1, 0:4], AF.Ln))
        DV.wait(AC, m3)
        m4 = DV.emit(lambda dv: dv.tensor_scalar_mul(srow[0:1, 12:16], srow[0:1, 4:8], 128.0))
        AC.wait(DV, m4)
        m5 = AC.emit(lambda ac: ac.activation(srow[0:1, 12:16], srow[0:1, 12:16], AF.Exp))

        PE.wait(AC, m5)
        PE.emit(lambda pe: pe.matmul(pstat[:, 0:4], onesr[0:1, :], srow[0:1, 4:8],
                                     start=True, stop=True))
        PE.emit(lambda pe: pe.matmul(pstat[:, 4:8], onesr[0:1, :], srow[0:1, 8:12],
                                     start=True, stop=True))
        for i, c0 in enumerate((0, 4, 8, 12)):
            PE.emit(lambda pe, i=i, c0=c0: pe.transpose(
                pstat[0:4, 8 + i:9 + i], srow[0:1, c0:c0 + 4], i128f_s[0:1, 0:1]))
        m6 = PE.n
        DV.wait(PE, m6)
        DV.emit(lambda dv: dv.tensor_copy(lnlam_b[:, :], pstat[:, 0:4]))
        DV.emit(lambda dv: dv.tensor_copy(eta_b[:, :], pstat[:, 4:8]))
        m7 = DV.emit(lambda dv: dv.tensor_copy(scol[0:4, 0:4], pstat[0:4, 8:12]))
        m_scalars = m7

        SYR.emit(lambda sy: sy.dma_start(out=rowload[:, :], in_=sudpin[:, :]))
        m8 = SYR.n
        PE.wait(SYR, m8)
        PE.wait(DV, m7)
        m9 = PE.emit(lambda pe: pe.transpose(pstat[:, 0:64], rowload[0:64, :], i128f_s[0:64, 0:64]))
        DV.wait(PE, m9)
        m10 = DV.emit(lambda dv: dv.tensor_scalar_mul(
            sudpT[:, :], pstat[:, 0:64], 1.0 / (SCALE * float(np.sqrt(J)))))
        SY.wait(DV, m10)
        SYR.emit(lambda sy: sy.dma_start(out=rowload[:, :], in_=wsupin[:, :]))
        m11 = SYR.n
        PE.wait(SYR, m11)
        PE.wait(DV, m10)
        m12 = PE.emit(lambda pe: pe.transpose(pstat[:, 64:128], rowload[0:64, :], i128f_s[0:64, 0:64]))
        DV.wait(PE, m12)
        m13 = DV.emit(lambda dv: dv.tensor_scalar_mul(wsupT[:, :], pstat[:, 64:128], SCALE))
        SY.wait(PE, m12)
        SYR.emit(lambda sy: sy.dma_start(out=rowload[:, :], in_=wsgin[:, :]))
        m14 = SYR.n
        PE.wait(SYR, m14)
        PE.wait(DV, m13)
        m15 = PE.emit(lambda pe: pe.transpose(pstat[:, 128:192], rowload[0:64, :], i128f_s[0:64, 0:64]))
        DV.wait(PE, m15)
        m16 = DV.emit(lambda dv: dv.tensor_scalar_mul(wsgT[:, :], pstat[:, 128:192], SCALE))

        DV.wait(DV, m_scalars)
        DV.emit(lambda dv: dv.memset(bmf[:, :], 0.0))
        for k in range(4):
            DV.wait(DV, DV.n)
            mk = DV.emit(lambda dv, k=k: dv.tensor_scalar(
                cscr[:, :], tms_s[:, :], lnlam_b[:, k:k + 1], None, ALU.mult))
            AC.wait(DV, mk)
            ma = AC.emit(lambda ac: ac.activation(cscr[:, :], cscr[:, :], AF.Exp))
            DV.wait(AC, ma)
            DV.emit(lambda dv, k=k: dv.tensor_scalar(
                cscr[:, :], cscr[:, :], eta_b[:, k:k + 1], None, ALU.mult))
            DV.wait(DV, DV.n)
            DV.emit(lambda dv: dv.tensor_add(bmf[:, :], bmf[:, :], cscr[:, :]))
        DV.wait(DV, DV.n)
        DV.emit(lambda dv: dv.tensor_mul(bmf[:, :], bmf[:, :], mge_s[:, :]))
        DV.wait(DV, DV.n)
        DV.emit(lambda dv: dv.tensor_add(bmf[:, :], bmf[:, :], i128f_s[:, :]))
        DV.wait(DV, DV.n)
        m_bmat = DV.emit(lambda dv: dv.tensor_copy(bmat[:, :], bmf[:, :]))

        DV.emit(lambda dv: dv.tensor_mul(w4f[:, :], riota_s[:, :], lnlam_b[:, :]))
        DV.wait(DV, DV.n)
        mcl = DV.n
        AC.wait(DV, mcl)
        ma = AC.emit(lambda ac: ac.activation(w4f[:, :], w4f[:, :], AF.Exp))
        DV.wait(AC, ma)
        m_cl = DV.emit(lambda dv: dv.tensor_copy(clcoef[:, :], w4f[:, :]))

        def coef4(dst_ap_fn, off):
            DV.wait(DV, DV.n)
            DV.emit(lambda dv: dv.tensor_scalar_add(a4t[:, :], grid4_s[:, :], float(off)))
            DV.wait(DV, DV.n)
            mx = DV.emit(lambda dv: dv.tensor_scalar(
                a4t[:, :], a4t[:, :], scol[0:4, 1:2], None, ALU.mult))
            AC.wait(DV, mx)
            my = AC.emit(lambda ac: ac.activation(a4t[:, :], a4t[:, :], AF.Exp))
            DV.wait(AC, my)
            return DV.emit(lambda dv: dv.tensor_scalar(
                dst_ap_fn(), a4t[:, :], scol[0:4, 2:3], None, ALU.mult))

        coef4(lambda: aloc[:, :], 1.0)
        for jj in range(TC):
            coef4(lambda jj=jj: am[:, jj * 128:(jj + 1) * 128], 128.0 * jj + 1.0)
        m_coefs = DV.n

        stA = contextlib.ExitStack()
        _open_stacks.append(stA)
        wzT_s = stA.enter_context(nc.sbuf_tensor("wzT_s", [128, DC * D], BF16))
        xlnT_s = stA.enter_context(nc.sbuf_tensor("xlnT_s", [128, DC * T], BF16))
        gam_b = stA.enter_context(nc.sbuf_tensor("gam_b", [128, D], BF16))
        bet_b = stA.enter_context(nc.sbuf_tensor("bet_b", [128, D], BF16))
        zbf = stA.enter_context(nc.sbuf_tensor("zbf", [128, D], BF16))
        hb = stA.enter_context(nc.sbuf_tensor("hb", [128, D], F32))
        Hh = [stA.enter_context(nc.sbuf_tensor(f"H{i}", [128, D], BF16)) for i in range(2)]

        GPB.emit(lambda gp: gp.dma_start(out=gam_b[0:1, :], in_=gammain[:, :]))
        GPB.emit(lambda gp: gp.dma_start(out=bet_b[0:1, :], in_=betain[:, :]))
        m_gb = GPB.n


        if kphase < 2:
            _cut()
            return nc
        m_prep_ld = []
        m_prep_dv = []
        m_prep_ac = []
        m_prep_pe = []
        for jp in range(PREV):
            b3 = jp % 3
            b = jp % 2
            sp = 8 * (jp % 2)
            if jp >= 3:
                GP.wait(AC, m_prep_ac[jp - 3])
                GP.wait(DV, m_prep_dv[jp - 3])
            GPX3[b3].emit(lambda gp, jp=jp, b3=b3: gp.dma_start(
                out=xlnT_s[:, b3 * D:(b3 + 1) * D],
                in_=xfull[jp * 128:(jp + 1) * 128, :]))
            m_ld = GPX3[b3].n
            m_prep_ld.append(m_ld)
            if jp >= 2:
                DV.wait(DV, m_prep_dv[jp - 2])
            DV.wait(GPX3[b3], m_ld)
            if jp == 0:
                DV.wait(DV, m_coefs)
            DV.emit(lambda dv, b3=b3, sp=sp: dv.reduce_sum(
                statc[:, sp:sp + 1], xlnT_s[:, b3 * D:(b3 + 1) * D], axis=AX.X))
            DV.wait(DV, DV.n)
            DV.emit(lambda dv, sp=sp: dv.tensor_scalar_mul(statc[:, sp + 1:sp + 2],
                                                           statc[:, sp:sp + 1], -1.0 / D))
            AC.wait(GPX3[b3], m_ld)
            AC.wait(AC, AC.n)
            if jp >= 2:
                AC.wait(DV, m_prep_dv[jp - 2])
            mss = AC.emit(lambda ac, b3=b3, sp=sp: ac.activation(
                zbf[:, :], xlnT_s[:, b3 * D:(b3 + 1) * D], AF.Square,
                accum_out=statc[:, sp + 2:sp + 3]))
            m_prep_ac.append(mss)
            DV.wait(AC, mss)
            DV.emit(lambda dv, sp=sp: dv.tensor_scalar_mul(statc[:, sp + 3:sp + 4],
                                                           statc[:, sp + 2:sp + 3], 1.0 / D))
            DV.wait(DV, DV.n)
            DV.emit(lambda dv, sp=sp: dv.tensor_mul(statc[:, sp + 5:sp + 6],
                                                    statc[:, sp + 1:sp + 2], statc[:, sp + 1:sp + 2]))
            DV.wait(DV, DV.n)
            mvar = DV.emit(lambda dv, sp=sp: dv.tensor_sub(
                statc[:, sp + 3:sp + 4], statc[:, sp + 3:sp + 4], statc[:, sp + 5:sp + 6]))
            AC.wait(DV, mvar)
            AC.emit(lambda ac, sp=sp: ac.activation(statc[:, sp + 4:sp + 5], statc[:, sp + 3:sp + 4],
                                                    AF.Ln, bias=consts[:, 0:1]))
            AC.wait(AC, AC.n)
            mrs = AC.emit(lambda ac, sp=sp: ac.activation(statc[:, sp + 4:sp + 5], statc[:, sp + 4:sp + 5],
                                                          AF.Exp, scale=consts[:, 2:3]))
            DV.wait(AC, mrs)
            DV.emit(lambda dv, jp=jp: dv.tensor_scalar(
                w4f[:, :], lnlam_b[:, :], expo_s[:, jp:jp + 1], None, ALU.mult))
            DV.wait(DV, DV.n)
            mw = DV.n
            AC.wait(DV, mw)
            mwe = AC.emit(lambda ac: ac.activation(w4f[:, :], w4f[:, :], AF.Exp))
            DV.wait(AC, mwe)
            if jp >= 2:
                DV.wait(PE, m_prep_pe[jp - 2])
            DV.emit(lambda dv, sp=sp: dv.tensor_scalar(
                w4f[:, :], w4f[:, :], statc[:, sp + 4:sp + 5], None, ALU.mult))
            DV.wait(DV, DV.n)
            mw4 = DV.emit(lambda dv, jp=jp, b=b: dv.tensor_scalar(
                w4b[b][:, :], w4f[:, :], pmask_s[:, jp:jp + 1], None, ALU.mult))
            DV.wait(DV, mw4)
            if jp >= 2:
                DV.wait(PE, m_prep_pe[jp - 2])
            mcast = DV.emit(lambda dv, b3=b3, b=b, sp=sp: dv.tensor_scalar(
                xlnT_s[:, (3 + b) * D:(4 + b) * D],
                xlnT_s[:, b3 * D:(b3 + 1) * D],
                statc[:, sp + 1:sp + 2], None, ALU.add))
            m_prep_dv.append(mcast)
            PE.wait(DV, mcast)

            def pmm(pe, jp=jp, b=b):
                last = None
                xc0 = (3 + b) * D
                for c in range(DC):
                    last = pe.matmul(pp2[:, 4 * c:4 * c + 4],
                                     xlnT_s[:, xc0 + c * 128: xc0 + (c + 1) * 128],
                                     w4b[b][:, :],
                                     start=(jp == 0 and c == 0),
                                     stop=(jp == PREV - 1 and c == DC - 1))
                return last
            m_pe = PE.emit(pmm)
            m_prep_pe.append(m_pe)
        for c in range(DC):
            GPZ.emit(lambda gp, c=c: gp.dma_start(
                out=wzT_s[:, c * D:(c + 1) * D], in_=wzT[c * 128:(c + 1) * 128, :]))
        m_gp_wz = GPZ.n
        for c in range(DC):
            GPW.emit(lambda gp, c=c: gp.dma_start(
                out=wup_bf[c * 128:(c + 1) * 128, :], in_=wupT[c * 128:(c + 1) * 128, :]))
        for c in range(DC):
            GPW.emit(lambda gp, c=c: gp.dma_start(
                out=wgate_bf[c * 128:(c + 1) * 128, :], in_=wgateT[c * 128:(c + 1) * 128, :]))
        m_gp_upgate = GPW.n

        DV.wait(PE, m_prep_pe[-1])
        m_pev = DV.emit(lambda dv: dv.tensor_copy(pprev_bf[:, :], pp2[:, 0:64]))

        if kphase < 3:
            _cut()
            return nc
        PE.wait(GPB, m_gb)
        PE.wait(DV, m16)
        for row_t in (gam_b, bet_b):
            for s in range(4):
                mmm = PE.emit(lambda pe, r=row_t, s=s: pe.matmul(
                    pstat[:, 0:512], onesb[0:1, :], r[0:1, s * 512:(s + 1) * 512],
                    start=True, stop=True))
                DV.wait(PE, mmm)
                mvv = DV.emit(lambda dv, r=row_t, s=s: dv.tensor_copy(
                    r[:, s * 512:(s + 1) * 512], pstat[:, 0:512]))
                PE.wait(DV, mvv)
        m_gbb = DV.n

        PE.wait(DV, m_pev)
        PE.wait(GPZ, m_gp_wz)
        for s in range(4):
            def mmm(pe, s=s):
                last = None
                for c in range(DC):
                    last = pe.matmul(pstat[0:4, 0:512],
                                     pprev_bf[:, 4 * c:4 * c + 4],
                                     wzT_s[:, c * D + s * 512: c * D + (s + 1) * 512],
                                     start=(c == 0), stop=(c == DC - 1))
                return last
            mp = PE.emit(mmm)
            DV.wait(PE, mp)
            md = DV.emit(lambda dv, s=s: dv.tensor_copy(
                marow[0:4, s * 512:(s + 1) * 512], pstat[0:4, 0:512]))
            PE.wait(DV, md)
        m_M = DV.n

        if kphase < 4:
            _cut()
            return nc
        _close_stack(stP)
        m_a0_ac = []
        m_prep_end_dv = DV.n
        m_prep_end_ac = AC.n
        m_prep_end_pe = PE.n
        slot_user = [0] * 2
        for j in range(TC):
            b = j % 2
            SY.wait(AC, m_prep_end_ac)
            SY.wait(DV, m_prep_end_dv)
            if j >= 2:
                SY.wait(AC, m_a0_ac[j - 2])
            m_ld = SYG[b].emit(lambda sy, j=j, b=b: sy.dma_start(
                out=G[b][:, :], in_=xown[j * 128:(j + 1) * 128, :]))
            DV.wait(SYG[b], m_ld)
            DV.emit(lambda dv, b=b: dv.reduce_sum(statc[:, 0:1], G[b][:, :], axis=AX.X))
            DV.wait(DV, DV.n)
            DV.emit(lambda dv: dv.tensor_scalar_mul(statc[:, 1:2], statc[:, 0:1], -1.0 / D))
            mdn = DV.n
            AC.wait(DV, mdn)
            msq = AC.emit(lambda ac, b=b: ac.activation(G2h[:, :], G[b][:, 0:1024], AF.Square))
            DV.wait(AC, msq)
            mr0 = DV.emit(lambda dv: dv.reduce_sum(statc[:, 2:3], G2h[:, :], axis=AX.X))
            AC.wait(DV, mr0)
            msq2 = AC.emit(lambda ac, b=b: ac.activation(G2h[:, :], G[b][:, 1024:2048], AF.Square))
            DV.wait(AC, msq2)
            DV.emit(lambda dv: dv.reduce_sum(statc[:, 6:7], G2h[:, :], axis=AX.X))
            DV.wait(DV, DV.n)
            DV.emit(lambda dv: dv.tensor_add(statc[:, 2:3], statc[:, 2:3], statc[:, 6:7]))
            DV.wait(DV, DV.n)
            DV.emit(lambda dv: dv.tensor_scalar_mul(statc[:, 3:4], statc[:, 2:3], 1.0 / D))
            DV.emit(lambda dv: dv.tensor_mul(statc[:, 5:6], statc[:, 1:2], statc[:, 1:2]))
            DV.wait(DV, DV.n)
            mvar = DV.emit(lambda dv: dv.tensor_sub(statc[:, 3:4], statc[:, 3:4], statc[:, 5:6]))
            AC.wait(DV, mvar)
            AC.emit(lambda ac: ac.activation(statc[:, 4:5], statc[:, 3:4], AF.Ln,
                                             bias=consts[:, 0:1]))
            AC.wait(AC, AC.n)
            mrs = AC.emit(lambda ac: ac.activation(statc[:, 4:5], statc[:, 4:5], AF.Exp,
                                                   scale=consts[:, 2:3]))
            if ka0 < 2:
                m_a0_ac.append(AC.n)
                try:
                    m_a0_pe.append(PE.n)
                except NameError:
                    m_a0_pe = [PE.n]
                continue
            DV.wait(AC, mrs)
            if j >= 2:
                DV.wait(PE, m_a0_pe[j - 2])
            else:
                DV.wait(PE, m_prep_end_pe)
            mnorm = DV.emit(lambda dv, b=b: dv.tensor_scalar(
                Hh[b][:, :], G[b][:, :], statc[:, 1:2], statc[:, 4:5], ALU.add, ALU.mult))
            if ka0 < 3:
                m_a0_ac.append(AC.n)
                try:
                    m_a0_pe.append(PE.n)
                except NameError:
                    m_a0_pe = [PE.n]
                continue
            PE.wait(DV, mnorm)
            for c in range(DC):
                slot = c % 2
                PE.wait(AC, slot_user[slot])
                mt = PE.emit(lambda pe, b=b, c=c, slot=slot: pe.transpose(
                    ptA[slot][:, 0:128],
                    Hh[b][:, c * 128:(c + 1) * 128], i128b_s[:, :]))
                AC.wait(PE, mt)
                mcp = AC.emit(lambda ac, c=c, j=j, slot=slot: ac.copy(
                    xlnT_s[:, c * T + j * 128: c * T + (j + 1) * 128],
                    ptA[slot][:, 0:128]))
                slot_user[slot] = mcp
            m_a0_ac.append(AC.n)
            try:
                m_a0_pe.append(PE.n)
            except NameError:
                m_a0_pe = [PE.n]

        m_a0_done_pe = PE.n
        m_a0_end_dv = DV.n

        if kphase < 5:
            _cut()
            return nc
        _close_stack(stP2)
        stAz = contextlib.ExitStack()
        _open_stacks.append(stAz)
        ph2 = [stAz.enter_context(nc.psum_tensor(f"ph{i}", [128, 1024], F32)) for i in range(2)]
        pz2 = [stAz.enter_context(nc.psum_tensor(f"pz{i}", [128, 1024], F32)) for i in range(2)]

        m_zev = [0, 0]
        m_hev = 0
        m_ccast = 0
        m_clmm = [0, 0]
        m_x2st = []
        m_pzev = [0, 0]
        m_phdv = [0, 0]
        first_ph0 = True
        for j in range(TC):
            b = j % 2
            for hh in range(2):
                par = (j * 2 + hh) % 2
                PE.wait(AC, m_a0_ac[j])
                PE.wait(GPZ, m_gp_wz)
                PE.wait(AC, m_pzev[par])
                PE.wait(DV, m_ccast)

                def zmm(pe, j=j, hh=hh, par=par):
                    last = None
                    for c in range(DC):
                        for s in range(2):
                            last = pe.matmul(
                                pz2[par][:, s * 512:(s + 1) * 512],
                                xlnT_s[:, c * T + j * 128: c * T + (j + 1) * 128],
                                wzT_s[:, c * D + hh * 1024 + s * 512: c * D + hh * 1024 + (s + 1) * 512],
                                start=(c == 0), stop=(c == DC - 1))
                    return last
                mz = PE.emit(zmm)
                AC.wait(PE, mz)
                AC.wait(PE, m_clmm[hh])
                m_zev[hh] = AC.emit(lambda ac, hh=hh, par=par: ac.copy(
                    zbf[:, hh * 1024:(hh + 1) * 1024], pz2[par][:, :]))
                m_pzev[par] = m_zev[hh]
                PE.wait(AC, m_zev[hh])
                PE.wait(DV, max(m_M, m_ccast, m_phdv[par]))
                if first_ph0 and par == 0:
                    PE.wait(AC, m_a0_ac[TC - 1])
                    first_ph0 = False

                def hmm(pe, j=j, hh=hh, par=par):
                    last = None
                    for s in range(2):
                        sl = slice(hh * 1024 + s * 512, hh * 1024 + (s + 1) * 512)
                        psl = slice(s * 512, (s + 1) * 512)
                        pe.matmul(ph2[par][:, psl], bmat[:, :], zbf[:, sl], start=True, stop=False)
                        pe.matmul(ph2[par][:, psl], aloc[:, :], carry_bf[0:4, sl], start=False, stop=False)
                        last = pe.matmul(ph2[par][:, psl], am[:, j * 128:(j + 1) * 128],
                                         marow[0:4, sl], start=False, stop=True)
                    return last
                mh = PE.emit(hmm)
                DV.wait(PE, mh)
                if j >= 1:
                    DV.wait(SYX2, m_x2st[j - 1])
                m_hev = DV.emit(lambda dv, hh=hh, par=par: dv.tensor_copy(
                    hb[:, hh * 1024:(hh + 1) * 1024], ph2[par][:, :]))
                m_phdv[par] = m_hev
                PE.wait(AC, m_zev[hh])

                def clmm(pe, hh=hh, par=par):
                    last = None
                    for s in range(2):
                        sl = slice(hh * 1024 + s * 512, hh * 1024 + (s + 1) * 512)
                        last = pe.matmul(pz2[par][0:4, s * 512:(s + 1) * 512], clcoef[:, :],
                                         zbf[:, sl], start=True, stop=True)
                    return last
                mcl2 = PE.emit(clmm)
                m_clmm[hh] = mcl2
                DV.wait(PE, mcl2)
                DV.emit(lambda dv, hh=hh, par=par: dv.scalar_tensor_tensor(
                    carry[0:4, hh * 1024:(hh + 1) * 1024],
                    carry[0:4, hh * 1024:(hh + 1) * 1024],
                    scol[0:4, 3:4], pz2[par][0:4, 0:1024], ALU.mult, ALU.add))
                DV.wait(DV, DV.n)
                m_ccast = DV.emit(lambda dv, hh=hh: dv.tensor_copy(
                    carry_bf[0:4, hh * 1024:(hh + 1) * 1024],
                    carry[0:4, hh * 1024:(hh + 1) * 1024]))
            sp = 8 * (j % 2)
            AC.wait(DV, m_hev)
            if j >= 2:
                AC.wait(DV, m_x2dv[j - 2])
            mvar, mrs = emit_stats(lambda: hb[:, :], lambda: Hh[0][:, :], sp)
            if j >= 2:
                SY.wait(DV, m_x2dv[j - 2])
            else:
                SY.wait(DV, m_a0_end_dv)
            SY.wait(DV, m_a0_end_dv)
            mxl = SYG[b].emit(lambda sy, j=j, b=b: sy.dma_start(
                out=G[b][:, :], in_=xown[j * 128:(j + 1) * 128, :]))
            DV.wait(SYG[b], mxl)
            DV.emit(lambda dv, b=b: dv.tensor_add(G[b][:, :], G[b][:, :], bet_b[:, :]))
            DV.wait(AC, mrs)
            DV.wait(DV, DV.n)
            DV.emit(lambda dv, sp=sp: dv.tensor_scalar(
                hb[:, :], hb[:, :], statc[:, sp + 1:sp + 2], statc[:, sp + 4:sp + 5],
                ALU.add, ALU.mult))
            DV.wait(DV, DV.n)
            DV.emit(lambda dv: dv.tensor_mul(hb[:, :], hb[:, :], gam_b[:, :]))
            DV.wait(DV, DV.n)
            mx2 = DV.emit(lambda dv, b=b: dv.tensor_add(hb[:, :], hb[:, :], G[b][:, :]))
            try:
                m_x2dv.append(mx2)
            except NameError:
                m_x2dv = [mx2]
            SY.wait(DV, mx2)
            mst = SYX2.emit(lambda sy, j=j: sy.dma_start(
                out=x2_dram[j * 128:(j + 1) * 128, :], in_=hb[:, :]))
            m_x2st.append(mst)

        if kphase < 6:
            _cut()
            return nc
        m_a1z_end = dict(pe=PE.n, dv=DV.n, ac=AC.n, sy=SY.n)
        _close_stack(stAz)
        _close_stack(stA)

        stB1 = contextlib.ExitStack()
        _open_stacks.append(stB1)
        uT_s = stB1.enter_context(nc.sbuf_tensor("uT_s", [128, DC * T], BF16))
        HhB = [stB1.enter_context(nc.sbuf_tensor(f"HB{i}", [128, D], BF16)) for i in range(2)]
        su_b = stB1.enter_context(nc.sbuf_tensor("su_b", [128, D], F32))
        wslab = [stB1.enter_context(nc.sbuf_tensor(f"wslab{i}", [128, 16 * 256], BF16)) for i in range(2)]
        gslab = [stB1.enter_context(nc.sbuf_tensor(f"gslab{i}", [128, 16 * 256], BF16)) for i in range(2)]
        btmp = stB1.enter_context(nc.sbuf_tensor("btmp", [128, 1024], BF16))
        silb2 = stB1.enter_context(nc.sbuf_tensor("silb2", [128, 1024], F32))
        vtmp2 = stB1.enter_context(nc.sbuf_tensor("vtmp2", [128, 1024], F32))
        vchB = [stB1.enter_context(nc.sbuf_tensor(f"vchB{i}", [128, T], BF16)) for i in range(2)]

        stB0 = contextlib.ExitStack()
        _open_stacks.append(stB0)
        ptB = [stB0.enter_context(nc.psum_tensor(f"ptB{i}", [128, 1024], BF16))
               for i in range(2)]
        pH0 = stB0.enter_context(nc.psum_tensor("pH0", [128, 512], F32))
        pH1 = stB0.enter_context(nc.psum_tensor("pH1", [128, 512], F32))

        SYR.emit(lambda sy: sy.dma_start(out=su_b[0:1, :], in_=suugin[:, :]))
        m_sul = SYR.n
        PE.wait(SYR, m_sul)
        su_scale = 1.0 / (SCALE * float(np.sqrt(D)))
        for s in range(4):
            mmm = PE.emit(lambda pe, s=s: pe.matmul(
                pH0[:, 0:512], onesr[0:1, :], su_b[0:1, s * 512:(s + 1) * 512],
                start=True, stop=True))
            DV.wait(PE, mmm)
            mvv = DV.emit(lambda dv, s=s: dv.tensor_scalar_mul(
                su_b[:, s * 512:(s + 1) * 512], pH0[:, 0:512], su_scale))
            PE.wait(DV, mvv)
        m_sub = DV.n

        m_b0_ac = []
        slotB = [0] * 2
        m_b0_pe_chunks = []
        for j in range(TC):
            b = j % 2
            SY.wait(SYX2, m_x2st[j])
            if j >= 2:
                SY.wait(DV, m_b0_norm[j - 2])
            else:
                SY.wait(DV, m_a1z_end["dv"])
            mld = SYG[b].emit(lambda sy, j=j, b=b: sy.dma_start(
                out=G[b][:, :], in_=x2_dram[j * 128:(j + 1) * 128, :]))
            AC.wait(SYG[b], mld)
            if j < 2:
                AC.wait(DV, m_a1z_end["dv"])
            sp = 8 * (j % 2)
            if j >= 2:
                AC.wait(DV, m_b0_norm[j - 2])
            mvar, mrs = emit_stats(lambda b=b: G[b][:, :], lambda: vchB[0][:, :], sp,
                                   extra_dv_wait=[(SYG[b], mld)])
            DV.wait(AC, mrs)
            DV.wait(DV, m_sub)
            if j >= 2:
                DV.wait(PE, m_b0_pe_chunks[j - 2])
            else:
                DV.wait(PE, m_a1z_end["pe"])
            for hx in range(2):
                DV.emit(lambda dv, b=b, hx=hx, sp=sp: dv.tensor_scalar(
                    G2h[:, :], G[b][:, hx * 1024:(hx + 1) * 1024],
                    statc[:, sp + 1:sp + 2], statc[:, sp + 4:sp + 5], ALU.add, ALU.mult))
                DV.wait(DV, DV.n)
                DV.emit(lambda dv, b=b, hx=hx: dv.tensor_mul(
                    HhB[b][:, hx * 1024:(hx + 1) * 1024], G2h[:, :],
                    su_b[:, hx * 1024:(hx + 1) * 1024]))
                DV.wait(DV, DV.n)
            mu_ = DV.n
            prev = mu_
            for st_i, dist in enumerate((1, 2, 4, 8)):
                blkw = dist * 128
                ngrp = DC // (2 * dist)

                def bfly(dv, b=b, dist=dist, blkw=blkw, ngrp=ngrp, which=0):
                    vv = HhB[b][:, :].rearrange("p (g two q) -> p g two q", two=2, q=blkw)
                    a = vv[:, :, 0, :]
                    bb = vv[:, :, 1, :]
                    t = btmp[:, 0:ngrp * blkw].rearrange("p (g q) -> p g q", q=blkw)
                    if which == 0:
                        return dv.tensor_add(t, a, bb)
                    elif which == 1:
                        return dv.tensor_sub(bb, a, bb)
                    else:
                        return dv.tensor_copy(a, t)
                DV.wait(DV, prev)
                DV.emit(lambda dv, f=bfly: f(dv, which=0))
                DV.wait(DV, DV.n)
                DV.emit(lambda dv, f=bfly: f(dv, which=1))
                DV.wait(DV, DV.n)
                prev = DV.emit(lambda dv, f=bfly: f(dv, which=2))
            m_b0n = prev
            try:
                m_b0_norm.append(m_b0n)
            except NameError:
                m_b0_norm = [m_b0n]
            PE.wait(DV, m_b0n)
            for c in range(DC):
                slot = c % 2
                PE.wait(AC, slotB[slot])
                mt = PE.emit(lambda pe, b=b, c=c, slot=slot: pe.transpose(
                    ptB[slot][:, 0:128],
                    HhB[b][:, c * 128:(c + 1) * 128], i128b_s[:, :]))
                AC.wait(PE, mt)
                mcp = AC.emit(lambda ac, c=c, j=j, slot=slot: ac.copy(
                    uT_s[:, c * T + j * 128: c * T + (j + 1) * 128],
                    ptB[slot][:, 0:128]))
                slotB[slot] = mcp
            m_b0_ac.append(AC.n)
            m_b0_pe_chunks.append(PE.n)
        m_h128 = []
        PE.wait(AC, m_b0_ac[-1])
        prev_ev = [0, 0]
        for c in range(DC):
            for s in range(4):
                par = (c * 4 + s) % 2
                ptgt = pH0 if par == 0 else pH1
                PE.wait(AC, prev_ev[par])
                mm_ = PE.emit(lambda pe, c=c, s=s, ptgt=ptgt: pe.matmul(
                    ptgt[:, 0:512], h128_s[:, :],
                    uT_s[:, c * T + s * 512: c * T + (s + 1) * 512],
                    start=True, stop=True))
                AC.wait(PE, mm_)
                mev = AC.emit(lambda ac, c=c, s=s, ptgt=ptgt: ac.copy(
                    uT_s[:, c * T + s * 512: c * T + (s + 1) * 512], ptgt[:, 0:512]))
                prev_ev[par] = mev
        m_uT = AC.n

        if kphase < 7:
            _cut()
            return nc
        _close_stack(stB0)
        stB1p = contextlib.ExitStack()
        _open_stacks.append(stB1p)
        pu = [stB1p.enter_context(nc.psum_tensor(f"pu{i}", [128, 1024], F32)) for i in range(2)]
        pg = [stB1p.enter_context(nc.psum_tensor(f"pg{i}", [128, 1024], F32)) for i in range(2)]

        def emit_slab(w, wbuf, gbuf):
            sq = SYW[w % 2]
            for c in range(DC):
                sq.emit(lambda sy, w=w, c=c, wbuf=wbuf: sy.dma_start(
                    out=wbuf[:, c * 256:(c + 1) * 256],
                    in_=wup_bf[c * 128:(c + 1) * 128, w * 256:(w + 1) * 256]))
            for c in range(DC):
                sq.emit(lambda sy, w=w, c=c, gbuf=gbuf: sy.dma_start(
                    out=gbuf[:, c * 256:(c + 1) * 256],
                    in_=wgate_bf[c * 128:(c + 1) * 128, w * 256:(w + 1) * 256]))
            return sq.n

        m_slab = {}
        m_win_pe = {}
        SY.wait(GPW, m_gp_upgate)
        m_slab[0] = emit_slab(0, wslab[0], gslab[0])
        m_vst = []
        m_vdv = {}
        m_stt_prev = 0
        for jc in range(JC):
            w = jc // 2
            if jc % 2 == 0 and w + 1 < JC // 2:
                if w >= 1:
                    SY.wait(PE, m_win_pe[w - 1])
                m_slab[w + 1] = emit_slab(w + 1, wslab[(w + 1) % 2], gslab[(w + 1) % 2])
            for hh in range(2):
                par = (jc * 2 + hh) % 2
                PE.wait(SYW[w % 2], m_slab[w])
                PE.wait(AC, m_uT)
                if (jc * 2 + hh) >= 2:
                    PE.wait(DV, m_vdv[(jc * 2 + hh) - 2])

                def upmm(pe, jc=jc, hh=hh, w=w, par=par):
                    jj = jc % 2
                    last = None
                    for c in range(DC):
                        for s in range(2):
                            last = pe.matmul(
                                pu[par][:, s * 512:(s + 1) * 512],
                                wslab[w % 2][:, c * 256 + jj * 128: c * 256 + (jj + 1) * 128],
                                uT_s[:, c * T + hh * 1024 + s * 512: c * T + hh * 1024 + (s + 1) * 512],
                                start=(c == 0), stop=(c == DC - 1))
                    for c in range(DC):
                        for s in range(2):
                            last = pe.matmul(
                                pg[par][:, s * 512:(s + 1) * 512],
                                gslab[w % 2][:, c * 256 + jj * 128: c * 256 + (jj + 1) * 128],
                                uT_s[:, c * T + hh * 1024 + s * 512: c * T + hh * 1024 + (s + 1) * 512],
                                start=(c == 0), stop=(c == DC - 1))
                    return last
                mup = PE.emit(upmm)
                AC.wait(PE, mup)
                AC.wait(DV, m_stt_prev)
                msil = AC.emit(lambda ac, jc=jc, par=par: ac.activation(
                    silb2[:, :], pg[par][:, :], AF.Silu, scale=wsgT[:, jc:jc + 1]))
                DV.wait(AC, msil)
                if jc >= 2:
                    DV.wait(SYV[jc % 2], m_vst[jc - 2])
                DV.wait(DV, DV.n)
                m_stt_prev = DV.emit(lambda dv, jc=jc, par=par: dv.scalar_tensor_tensor(
                    vtmp2[:, :], pu[par][:, :], wsupT[:, jc:jc + 1], silb2[:, :],
                    ALU.mult, ALU.mult))
                DV.wait(DV, DV.n)
                mvd = DV.emit(lambda dv, jc=jc, hh=hh: dv.tensor_scalar(
                    vchB[jc % 2][:, hh * 1024:(hh + 1) * 1024], vtmp2[:, :],
                    sudpT[:, jc:jc + 1], None, ALU.mult))
                m_vdv[jc * 2 + hh] = mvd
            SY.wait(DV, m_vdv[jc * 2 + 1])
            mst = SYV[jc % 2].emit(lambda sy, jc=jc: sy.dma_start(
                out=v_dram[jc * 128:(jc + 1) * 128, :], in_=vchB[jc % 2][:, :]))
            m_vst.append(mst)
            if jc % 2 == 1:
                m_win_pe[jc // 2] = PE.n
        m_b1_end = dict(pe=PE.n, dv=DV.n, ac=AC.n, sy=SY.n)

        if kphase < 8:
            _cut()
            return nc
        _close_stack(stB1p)
        _close_stack(stB1)
        stB2 = contextlib.ExitStack()
        _open_stacks.append(stB2)
        vq2 = [stB2.enter_context(nc.sbuf_tensor(f"vq{i}", [128, JC * 256], BF16))
               for i in range(2)]
        btmp2 = stB2.enter_context(nc.sbuf_tensor("btmp2", [128, 32 * 256], BF16))
        pq = [stB2.enter_context(nc.psum_tensor(f"pq{i}", [128, 512], F32)) for i in range(2)]

        m_v2st = {}
        m_bf_end = {}
        prev_evq = [0, 0]
        for e in range(8):
            p = e % 2
            if e == 0:
                SY.wait(SYV[0], SYV[0].n)
                SY.wait(SYV[1], SYV[1].n)
            if e >= 2:
                SY.wait(SYV2[p], m_v2st[e - 2])
            for cg in range(8):
                def ldq(sy, e=e, cg=cg, p=p):
                    src_ap = v_dram[cg * 1024:(cg + 1) * 1024, e * 256:(e + 1) * 256]
                    src_ap = src_ap.rearrange("(c q) t -> q c t", q=128)
                    dst_ap = vq2[p][:, cg * 8 * 256:(cg + 1) * 8 * 256].rearrange(
                        "q (c t) -> q c t", t=256)
                    return sy.dma_start(out=dst_ap, in_=src_ap)
                SYVQ[p].emit(ldq)
            mldq = SYVQ[p].n
            for c in range(JC):
                par = c % 2
                PE.wait(SYVQ[p], mldq)
                PE.wait(AC, prev_evq[par])
                if e >= 2:
                    PE.wait(DV, m_bf_end[e - 2])
                mmq = PE.emit(lambda pe, c=c, par=par, p=p: pe.matmul(
                    pq[par][:, 0:256], h128_s[:, :], vq2[p][:, c * 256:(c + 1) * 256],
                    start=True, stop=True))
                AC.wait(PE, mmq)
                mevq = AC.emit(lambda ac, c=c, par=par, p=p: ac.copy(
                    vq2[p][:, c * 256:(c + 1) * 256], pq[par][:, 0:256]))
                prev_evq[par] = mevq
            m_h128q = AC.n
            DV.wait(AC, m_h128q)
            prev = DV.n
            for dist in (1, 2, 4, 8, 16, 32):
                blkw = dist * 256
                ngrp = JC // (2 * dist)

                def bfq(dv, dist=dist, blkw=blkw, ngrp=ngrp, p=p, which=0):
                    vv = vq2[p][:, :].rearrange("q (g two q2) -> q g two q2", two=2, q2=blkw)
                    a = vv[:, :, 0, :]
                    bb = vv[:, :, 1, :]
                    t = btmp2[:, 0:ngrp * blkw].rearrange("q (g q2) -> q g q2", q2=blkw)
                    if which == 0:
                        return dv.tensor_add(t, a, bb)
                    elif which == 1:
                        return dv.tensor_sub(bb, a, bb)
                    else:
                        return dv.tensor_copy(a, t)
                DV.wait(DV, prev)
                DV.emit(lambda dv, f=bfq: f(dv, which=0))
                DV.wait(DV, DV.n)
                DV.emit(lambda dv, f=bfq: f(dv, which=1))
                DV.wait(DV, DV.n)
                prev = DV.emit(lambda dv, f=bfq: f(dv, which=2))
            m_bf_end[e] = prev
            SY.wait(DV, prev)
            for cg in range(8):
                def stq(sy, e=e, cg=cg, p=p):
                    dst_ap = v2_dram[cg * 1024:(cg + 1) * 1024, e * 256:(e + 1) * 256]
                    dst_ap = dst_ap.rearrange("(c q) t -> q c t", q=128)
                    src_ap = vq2[p][:, cg * 8 * 256:(cg + 1) * 8 * 256].rearrange(
                        "q (c t) -> q c t", t=256)
                    return sy.dma_start(out=dst_ap, in_=src_ap)
                SYV2[p].emit(stq)
            m_v2st[e] = SYV2[p].n
        m_b2_end_sy = (SYV2[0].n, SYV2[1].n)
        m_b2_end_pe = PE.n
        m_b2_end_dv = DV.n

        _close_stack(stB2)
        stB3 = contextlib.ExitStack()
        _open_stacks.append(stB3)
        accs = [stB3.enter_context(nc.sbuf_tensor(f"acc{i}", [128, TC * 512], F32))
                for i in range(2)]
        v2sl = [stB3.enter_context(nc.sbuf_tensor(f"v2sl{i}", [128, 8 * T], BF16))
                for i in range(2)]
        wdsl = [stB3.enter_context(nc.sbuf_tensor(f"wdsl{i}", [128, 8 * 512], BF16))
                for i in range(2)]
        ep = [stB3.enter_context(nc.sbuf_tensor(f"ep{i}", [128, 512], F32)) for i in range(2)]
        wsdp_b = stB3.enter_context(nc.sbuf_tensor("wsdp_b", [128, D], BF16))
        pd = [stB3.enter_context(nc.psum_tensor(f"pd{i}", [128, 512], F32)) for i in range(4)]
        pep = pd[3]

        GP.wait(PE, m_b2_end_pe)
        GP.wait(DV, m_b2_end_dv)
        GP.wait(SYV2[0], m_b2_end_sy[0])
        GP.wait(SYV2[1], m_b2_end_sy[1])
        GPB.emit(lambda gp: gp.dma_start(out=wsdp_b[0:1, :], in_=wsdpin[:, :]))
        m_wsl = GPB.n
        PE.wait(GPB, m_wsl)
        PE.wait(SYV2[0], m_b2_end_sy[0])
        PE.wait(SYV2[1], m_b2_end_sy[1])
        for s in range(4):
            mmm = PE.emit(lambda pe, s=s: pe.matmul(
                pd[0][:, :], onesb[0:1, :], wsdp_b[0:1, s * 512:(s + 1) * 512],
                start=True, stop=True))
            DV.wait(PE, mmm)
            mvv = DV.emit(lambda dv, s=s: dv.tensor_scalar_mul(
                wsdp_b[:, s * 512:(s + 1) * 512], pd[0][:, :], SCALE))
            PE.wait(DV, mvv)
        m_wsb = DV.n

        m_yst = []
        m_padd = {}
        m_sg_pe = {}
        m_acc_ep = {}
        DV.wait(SYV2[0], m_b2_end_sy[0])
        DV.wait(SYV2[1], m_b2_end_sy[1])
        first = True

        def emit_epilogue(ihq):
            ap = ihq % 2
            for tch in range(TC):
                b = tch % 2
                if SYY[b].n > 0:
                    SY.wait(SYY[b], SYY[b].n)
                if len(m_yst) >= 2:
                    SY.wait(DV, m_ep_dv[ihq * TC + tch - 2])
                mxl = SYE[b].emit(lambda sy, tch=tch, ihq=ihq, b=b: sy.dma_start(
                    out=ep[b][:, :],
                    in_=x2_dram[tch * 128:(tch + 1) * 128, ihq * 512:(ihq + 1) * 512]))
                DV.wait(SYE[b], mxl)
                DV.wait(DV, DV.n)
                DV.emit(lambda dv, tch=tch, ihq=ihq, ap=ap: dv.tensor_mul(
                    pep[:, :], accs[ap][:, tch * 512:(tch + 1) * 512],
                    wsdp_b[:, ihq * 512:(ihq + 1) * 512]))
                DV.wait(DV, DV.n)
                mepd = DV.emit(lambda dv, b=b: dv.tensor_add(
                    ep[b][:, :], ep[b][:, :], pep[:, :]))
                m_ep_dv.append(mepd)
                SY.wait(DV, mepd)
                myst = SYY[b].emit(lambda sy, tch=tch, ihq=ihq, b=b: sy.dma_start(
                    out=yout[tch * 128:(tch + 1) * 128, ihq * 512:(ihq + 1) * 512],
                    in_=ep[b][:, :]))
                m_yst.append(myst)
            m_acc_ep[ap] = DV.n

        m_ep_dv = []
        for ihq in range(4):
            ap = ihq % 2
            m_accz = DV.emit(lambda dv, ap=ap: dv.memset(accs[ap][:, :], 0.0))
            m_last_add = {}
            for sg in range(8):
                sp_ = (ihq * 8 + sg) % 2
                if first:
                    SY.wait(SYV2[0], m_b2_end_sy[0])
                    SY.wait(SYV2[1], m_b2_end_sy[1])
                    GP.wait(PE, m_b2_end_pe)
                    GP.wait(DV, m_b2_end_dv)
                    GP.wait(SYV2[0], m_b2_end_sy[0])
                    GP.wait(SYV2[1], m_b2_end_sy[1])
                    first = False
                else:
                    SY.wait(PE, m_sg_pe.get(sp_, 0))
                    GP.wait(PE, m_sg_pe.get(sp_, 0))
                for jj in range(8):
                    jcg = sg * 8 + jj
                    SYSL2[sp_].emit(lambda sy, jcg=jcg, jj=jj, sp_=sp_: sy.dma_start(
                        out=v2sl[sp_][:, jj * T:(jj + 1) * T],
                        in_=v2_dram[jcg * 128:(jcg + 1) * 128, :]))
                mv2 = SYSL2[sp_].n
                for jj in range(8):
                    jcg = sg * 8 + jj
                    GPD2[sp_].emit(lambda gp, jcg=jcg, jj=jj, ihq=ihq, sp_=sp_: gp.dma_start(
                        out=wdsl[sp_][:, jj * 512:(jj + 1) * 512],
                        in_=wdnT[jcg * 128:(jcg + 1) * 128, ihq * 512:(ihq + 1) * 512]))
                mwd = GPD2[sp_].n
                PE.wait(SYSL2[sp_], mv2)
                PE.wait(GPD2[sp_], mwd)
                PE.wait(DV, m_accz)
                for tch in range(TC):
                    par = tch % 3
                    PE.wait(DV, m_padd.get(par, 0))

                    def dmm(pe, tch=tch, par=par, sp_=sp_):
                        last = None
                        for jj in range(8):
                            last = pe.matmul(
                                pd[par][:, :],
                                v2sl[sp_][:, jj * T + tch * 128: jj * T + (tch + 1) * 128],
                                wdsl[sp_][:, jj * 512:(jj + 1) * 512],
                                start=(jj == 0), stop=(jj == 7))
                        return last
                    mdm = PE.emit(dmm)
                    DV.wait(PE, mdm)
                    madd = DV.emit(lambda dv, tch=tch, par=par, ap=ap: dv.tensor_add(
                        accs[ap][:, tch * 512:(tch + 1) * 512],
                        accs[ap][:, tch * 512:(tch + 1) * 512], pd[par][:, :]))
                    m_padd[par] = madd
                    m_last_add[tch] = madd
                m_sg_pe[sp_] = PE.n
                if sg == 2 and ihq > 0:
                    emit_epilogue(ihq - 1)
            m_q_adds = DV.n
        emit_epilogue(3)
        _close_stack(stB3)

        SY.wait(SYY[0], SYY[0].n)
        SY.wait(SYY[1], SYY[1].n)
        SY.wait(SY, SY.n)
        GP.wait(GP, GP.n)

        _flush_all(blk, PE, DV, AC, SY, GP)

    return nc


def _hadamard(n):
    h = np.array([[1.0]], dtype=np.float64)
    while h.shape[0] < n:
        h = np.block([[h, h], [h, -h]])
    return h


_NC_CACHE = [None]


def kernel(**inputs):
    import ml_dtypes
    x = np.asarray(inputs["L0_attn_x"])
    W = np.asarray(inputs["L0_attn_W"])
    lam_logit = np.asarray(inputs["L0_attn_lam_logit"]).reshape(1, 4)
    eta = np.asarray(inputs["L0_attn_eta"]).reshape(1, 4)
    gamma = np.asarray(inputs["L0_attn_gamma"]).reshape(1, D)
    beta = np.asarray(inputs["L0_attn_beta"]).reshape(1, D)
    su_ug = np.asarray(inputs["L0_ffn_SU_ug"]).reshape(1, D)
    su_dp = np.asarray(inputs["L0_ffn_SU_dp"]).reshape(64, 128)
    ws_ug = np.asarray(inputs["L0_ffn_Wscale_ug"])
    ws_dp = np.asarray(inputs["L0_ffn_Wscale_dp"]).reshape(1, D)
    W_up = np.asarray(inputs["L0_ffn_W_up"])
    W_gate = np.asarray(inputs["L0_ffn_W_gate"])
    W_down = np.asarray(inputs["L0_ffn_W_down"])

    xf = np.ascontiguousarray(x.reshape(NCORE * T, D))
    xprev = np.ascontiguousarray(xf[:PREV * 128, :])
    wzT = np.ascontiguousarray(W.T)
    wupT = np.ascontiguousarray(W_up.T)
    wgateT = np.ascontiguousarray(W_gate.T)
    wdnT = np.ascontiguousarray(W_down.T)

    h128 = (_hadamard(128)).astype(ml_dtypes.bfloat16)
    i128b = np.eye(128, dtype=np.float32).astype(ml_dtypes.bfloat16)
    i128f = np.eye(128, dtype=np.float32)
    ti = np.arange(128, dtype=np.float32)
    tms = (ti[None, :] - ti[:, None]).astype(np.float32)
    mge = (ti[None, :] >= ti[:, None]).astype(np.float32)
    riota = np.repeat((127.0 - ti)[:, None], 4, axis=1).astype(np.float32)
    grid4 = np.repeat(ti[None, :], 4, axis=0).astype(np.float32)

    common = {
        "wzT": wzT, "wupT": wupT, "wgateT": wgateT, "wdnT": wdnT,
        "lamlog": lam_logit.astype(np.float32), "etain": eta.astype(np.float32),
        "gammain": gamma.astype(np.float32), "betain": beta.astype(np.float32),
        "suugin": su_ug.astype(np.float32), "sudpin": su_dp.astype(np.float32),
        "wsupin": np.ascontiguousarray(ws_ug[:J].reshape(64, 128)).astype(np.float32),
        "wsgin": np.ascontiguousarray(ws_ug[J:].reshape(64, 128)).astype(np.float32),
        "wsdpin": ws_dp.astype(np.float32),
        "h128in": h128, "i128fin": i128f, "i128bin": i128b,
        "tmsin": tms, "mgein": mge, "riotain": riota, "grid4in": grid4,
        "xfull": xprev,
    }
    in_maps = []
    for c in range(NCORE):
        tau = np.arange(128, dtype=np.float64)[:, None]
        jj = np.arange(PREV, dtype=np.float64)[None, :]
        expo = (2048.0 * c - 1.0) - 128.0 * jj - tau
        live = (jj < 16 * c) & np.ones((128, 1), bool)
        expo = np.where(live, expo, 1.0e6)
        pmask = live.astype(np.float32)
        m = dict(common)
        m["xown"] = np.ascontiguousarray(xf[c * T:(c + 1) * T, :])
        m["expoin"] = expo.astype(np.float32)
        m["pmaskin"] = np.ascontiguousarray(pmask)
        in_maps.append(m)

    if _NC_CACHE[0] is None:
        _NC_CACHE[0] = build_nc()
    nc = _NC_CACHE[0]
    res = run_bass_kernel_spmd(nc, in_maps, core_ids=list(range(NCORE)))
    out = np.concatenate([np.asarray(r["yout"]) for r in res.results], axis=0)
    return out.reshape(1, NCORE * T, D).astype(np.float32)
